# version 14
# baseline (speedup 1.0000x reference)
import numpy as np
import ml_dtypes
import concourse.bass as bass
import concourse.mybir as mybir
from concourse import dve_ops
from concourse.bass_utils import run_bass_kernel_spmd

F32 = mybir.dt.float32
BF16 = mybir.dt.bfloat16
I32 = mybir.dt.int32
AF = mybir.ActivationFunctionType
ALU = mybir.AluOpType
AX = mybir.AxisListType

MAGIC = 12582912.0
CLIP32 = 2147483520.0
S, E, H, D, QR, KR, RD, F = 2048, 2048, 16, 128, 64, 128, 64, 8192
EPS = 1e-5
ISD = 0.08838834764831845
NT, NS, NC, NF = S // 128, S // 512, E // 128, F // 128
VCOLS = H * D
WKVA_C = KR + VCOLS

ENGINES = ("sync", "gpsimd", "tensor", "vector", "scalar")
ENG_SEM = {"tensor": "pe", "vector": "dv", "scalar": "ac", "gpsimd": "gp",
           "sync": "sy"}
DMA_QUEUE = {"sync": "hw", "gpsimd": "sw"}
NLANES = 8


class Buf:
    __slots__ = ("name", "writers", "readers")

    def __init__(self, name):
        self.name = name
        self.writers = []
        self.readers = []


class Op:
    __slots__ = ("eng", "fn", "deps", "tag", "dma", "inc_sem", "inc_val",
                 "_clk", "_wait")

    def __init__(self, eng, fn, tag, dma):
        self.eng = eng
        self.fn = fn
        self.deps = []
        self.tag = tag
        self.dma = dma
        self.inc_sem = None
        self.inc_val = 0
        self._clk = {}
        self._wait = []


class Trk:
    def __init__(self, nc):
        self.nc = nc
        self.ops = {e: [] for e in ENGINES}
        names = ["pe", "dv", "ac", "gp", "sy"]
        for q in ("hw", "sw"):
            names += [f"{q}{i}" for i in range(NLANES)]
        self.sems = {s: nc.alloc_semaphore(s) for s in names}
        self.dma_count = {"hw": 0, "sw": 0}
        self.lane_last = {f"{q}{i}": None for q in ("hw", "sw")
                          for i in range(NLANES)}

    def op(self, eng, fn, reads=(), writes=(), appends=(), tag="", dma=False):
        o = Op(eng, fn, tag, dma)
        deps = set()
        for b in reads:
            deps.update(b.writers)
        for b in writes:
            deps.update(b.writers)
            deps.update(b.readers)
        if dma:
            q = DMA_QUEUE[eng]
            lane = f"{q}{self.dma_count[q] % NLANES}"
            self.dma_count[q] += 1
            o.inc_sem = lane
            prev = self.lane_last[lane]
            if prev is not None:
                deps.add(prev)
            self.lane_last[lane] = o
        o.deps = list(deps)
        self.ops[eng].append(o)
        for b in reads:
            b.readers.append(o)
        for b in writes:
            b.writers = [o]
            b.readers = []
        for b in appends:
            b.writers.append(o)
        return o

    def finalize(self):
        needs = set()
        for e in ENGINES:
            for o in self.ops[e]:
                needs.update(o.deps)
        cnt = {s: 0 for s in self.sems}
        for e in ENGINES:
            for o in self.ops[e]:
                if o.dma:
                    cnt[o.inc_sem] += 16
                    o.inc_val = cnt[o.inc_sem]
                elif o in needs:
                    s = ENG_SEM[e]
                    cnt[s] += 1
                    o.inc_sem, o.inc_val = s, cnt[s]
        idx = {e: 0 for e in ENGINES}
        clock = {e: {} for e in ENGINES}
        done = set()
        n_total = sum(len(self.ops[e]) for e in ENGINES)
        n_done = 0
        progress = True
        while progress:
            progress = False
            for e in ENGINES:
                while idx[e] < len(self.ops[e]):
                    o = self.ops[e][idx[e]]
                    if not all(d in done for d in o.deps):
                        break
                    ck = clock[e]
                    w = []
                    for dd in sorted(o.deps,
                                     key=lambda d: (d.inc_sem, d.inc_val)):
                        s, v = dd.inc_sem, dd.inc_val
                        if ck.get(s, 0) < v:
                            w.append((s, v))
                            ck[s] = v
                            for s2, v2 in dd._clk.items():
                                if ck.get(s2, 0) < v2:
                                    ck[s2] = v2
                    o._wait = w
                    o._clk = dict(ck)
                    if o.inc_sem:
                        o._clk[o.inc_sem] = max(o._clk.get(o.inc_sem, 0),
                                                o.inc_val)
                    done.add(o)
                    idx[e] += 1
                    n_done += 1
                    progress = True
        if n_done != n_total:
            stuck = {e: self.ops[e][idx[e]].tag for e in ENGINES
                     if idx[e] < len(self.ops[e])}
            raise RuntimeError(f"build-order deadlock: {stuck}")
        self.totals = dict(cnt)

    def emit(self, engine, eng):
        for o in self.ops[engine]:
            for s, v in o._wait[1:]:
                eng.wait_ge(self.sems[s], v)
            ins = o.fn(eng)
            if o._wait:
                s, v = o._wait[0]
                ins._wait_ge(self.sems[s], v)
            if o.inc_sem:
                ins.then_inc(self.sems[o.inc_sem], 16 if o.dma else 1)


def _consts():
    c = {}
    c["ident"] = np.eye(128, dtype=ml_dtypes.bfloat16)
    c["ident32"] = np.eye(128, dtype=np.float32)
    Smat = np.zeros((RD, RD), dtype=np.float32)
    half = RD // 2
    for i in range(half):
        Smat[i, i + half] = -1.0
        Smat[i + half, i] = 1.0
    c["stmat"] = np.ascontiguousarray(Smat.T.astype(ml_dtypes.bfloat16))
    c["trimask"] = np.ascontiguousarray(
        np.triu(np.ones((128, 128), np.float32)).astype(ml_dtypes.bfloat16))
    am = np.triu(np.full((128, 128), -1e30, np.float32), 1)
    c["addmask"] = np.ascontiguousarray(am.astype(ml_dtypes.bfloat16))
    c["ones_col"] = np.ones((128, 1), dtype=ml_dtypes.bfloat16)
    c["ones_row"] = np.ones((1, 128), dtype=ml_dtypes.bfloat16)
    c["ones32"] = np.ones((1, 128), dtype=np.float32)
    inv_freq = 1.0 / (10000.0 ** (np.arange(0, RD, 2, dtype=np.float64) / RD))
    freqs = np.arange(S, dtype=np.float64)[:, None] * inv_freq[None, :]
    cs, sn = np.cos(freqs).T, np.sin(freqs).T
    c["costab"] = np.ascontiguousarray(
        np.concatenate([cs, cs], 0).astype(ml_dtypes.bfloat16))
    c["sintab"] = np.ascontiguousarray(
        np.concatenate([sn, sn], 0).astype(ml_dtypes.bfloat16))
    return c


def build():
    nc = bass.Bass("TRN2", target_bir_lowering=False)
    t = Trk(nc)

    ap = {}
    for name, shape in (("x", [S, E]), ("gq", [1, E]), ("gkv", [1, E]),
                        ("wqa", [E, QR]), ("wqb", [QR, E]),
                        ("wkva", [E, WKVA_C]), ("wkvb", [KR, E]),
                        ("wo", [E, E]), ("w1t", [E, F]), ("b1", [1, F]),
                        ("w2t", [F, E]), ("b2", [1, E]), ("s1", [1, 1]),
                        ("s2", [1, 1])):
        ap[name] = nc.dram_tensor(name, shape, F32, kind="ExternalInput").ap()
    out = nc.dram_tensor("out", [S, E], F32, kind="ExternalOutput").ap()
    for k, v in _consts().items():
        ap[k] = nc.inline_tensor(v, name=f"c_{k}").ap()

    x2s = nc.dram_tensor("x2s", [S, E], F32, kind="Internal").ap()
    mts = nc.dram_tensor("mts", [NS * NC * 128, 512], BF16, kind="Internal").ap()
    w1i = nc.dram_tensor("w1i", [NF * NC * 128, 128], BF16, kind="Internal").ap()
    w2i = nc.dram_tensor("w2i", [4 * NF * 128, 512], BF16, kind="Internal").ap()
    nscr = nc.dram_tensor("nscr", [NS * NF * 128, 512], BF16, kind="Internal").ap()

    AL = nc.alloc_sbuf_tensor
    arena = AL("arena", [128, NT * 2048], BF16)
    wbuf = AL("wbuf", [128, 32768], BF16)
    stage32 = AL("stage32", [128, 4096], F32)
    xh = AL("xh", [128, 2048], BF16)
    i32t = AL("i32t", [128, 1024], I32)
    xh2 = AL("xh2", [128, 2048], BF16)
    mstg = AL("mstg", [128, 1024], BF16)
    scal = AL("scal", [128, 16], F32)
    stats = AL("stats", [128, 16], F32)
    sv = AL("svals", [1, 16], F32)
    wmx = AL("wmx", [128, 66], F32)
    idt = AL("idt", [128, 128], BF16)
    idt32 = AL("idt32", [128, 128], F32)
    stm = AL("stm", [64, 64], BF16)
    trim = AL("trim", [128, 128], BF16)
    amsk = AL("amsk", [128, 128], BF16)
    onec = AL("onec", [128, 1], BF16)
    oner = AL("oner", [1, 128], BF16)
    one32 = AL("one32", [1, 128], F32)
    guards = []

    def tmp_alloc(name, shape, dt):
        g = nc.sbuf_tensor(name, shape, dt)
        guards.append(g)
        return g.__enter__()
    kct = tmp_alloc("kct", [128, 2048], BF16)
    tT = tmp_alloc("tTq", [64, 2048], BF16)
    wqb_b = tmp_alloc("wqb_b", [64, 2048], BF16)
    wkvb_b = tmp_alloc("wkvb_b", [128, 2048], BF16)
    ga = nc.sbuf_tensor("wqa_b", [128, NC * QR], BF16)
    wqa_b = ga.__enter__()
    gb = nc.sbuf_tensor("gq_pc", [128, NC], F32)
    gq_pc = gb.__enter__()
    gc = nc.sbuf_tensor("gkv_pc", [128, NC], F32)
    gkv_pc = gc.__enter__()
    gc.__exit__(None, None, None)
    gb.__exit__(None, None, None)
    ga.__exit__(None, None, None)
    QhA = [tmp_alloc("Qh0", [128, 2048], BF16), tmp_alloc("Qh1", [128, 2048], BF16)]
    KhA = [tmp_alloc("Kh0", [128, 2048], BF16), tmp_alloc("Kh1", [128, 2048], BF16)]
    Pr = tmp_alloc("Pr", [128, 4096], BF16)
    PTt = tmp_alloc("PTt", [128, 1024], BF16)
    dgt = tmp_alloc("dgt", [128, 256], BF16)
    rtab = tmp_alloc("rtab", [128, S], BF16)
    mneg = tmp_alloc("mneg", [128, 20], F32)
    for g in reversed(guards):
        g.__exit__(None, None, None)
    rlb = stage32[:, 3584:4096]
    r2sup = AL("r2sup", [128, 8192], F32)
    b2qb = AL("b2qb", [128, 2048], BF16)
    b1qt = AL("b1qt", [128, 64], F32)

    WV0, WV1, KCW = 8192, 16384, 24576
    W1A, W1B, MTSL = 0, 2048, 8192

    pp = [nc.alloc_psum_tensor(f"pp{i}", [128, 512], F32) for i in range(6)]
    ppbt = [nc.alloc_psum_tensor("ppb0", [128, 512], BF16)]
    ppf = nc.alloc_psum_tensor("ppf", [128, 512], F32)
    p1 = pp[5][0:1, 0:512]

    PP = [Buf(f"pp{i}") for i in range(6)]
    PPB = [Buf("ppb0")]
    PPF = Buf("ppf")
    P1B = Buf("p1")
    CONST = Buf("const")
    ARENA = Buf("arena")
    STG = [Buf("stg0"), Buf("stg1")]
    T1B = [Buf("t1a"), Buf("t1b")]
    XHB = Buf("xh")
    XHB2 = Buf("xh2")
    I32H = [Buf("i32a"), Buf("i32b")]
    MSTG = [Buf("mstg0"), Buf("mstg1")]
    STATS = Buf("stats")
    SCAL = Buf("scal")
    SVB = Buf("sv")
    WMX = Buf("wmx")
    KCT = Buf("kct")
    TTB = Buf("tT")
    WQA = Buf("wqa_b")
    WQB = Buf("wqb_b")
    WKVB = Buf("wkvb_b")
    GQ = Buf("gq")
    GKV = Buf("gkv")
    QHBA = [[Buf(f"Qh{i}_{j}") for j in range(4)] for i in range(2)]
    KHBA = [[Buf(f"Kh{i}_{j}") for j in range(4)] for i in range(2)]
    PRB = [[Buf(f"Pr{p}_{i}") for i in range(4)] for p in range(2)]
    PTRB = [Buf("PT0"), Buf("PT1")]
    DGB = [Buf("dg0"), Buf("dg1")]
    MNEG = [Buf(f"mneg{i}") for i in range(16)]
    MNEGP = [Buf(f"mnegp{i}") for i in range(4)]
    RLR = Buf("rlrow")
    RLBB = Buf("rlb")
    XB = [[Buf(f"xb{g}_{b}") for b in range(4)] for g in range(4)]
    WCH = [Buf(f"wch{i}") for i in range(16)]
    ACH = [Buf(f"ach{i}") for i in range(NT)]
    KCWS = [Buf(f"kcws{i}") for i in range(4)]
    W1S = [[Buf(f"w1s{s_}_{j}") for j in range(2)] for s_ in range(2)]
    B1Q = Buf("b1q")
    B2QB = Buf("b2qb")
    R2T = [Buf(f"r2_{i}") for i in range(4)]
    SCR = {n: Buf("scr_" + n) for n in ("x2s", "mts", "w1i", "w2i", "nscr")}

    op = t.op

    xr3 = ap["x"].rearrange("(t p) e -> t p e", p=128)
    wkva3 = ap["wkva"].rearrange("(c p) q -> c p q", p=128)
    wo3 = ap["wo"].rearrange("(c p) e -> c p e", p=128)
    w1r = ap["w1t"].rearrange("(c p) f -> c p f", p=128)
    w2r = ap["w2t"].rearrange("(c p) d -> c p d", p=128)
    w1i4 = w1i.rearrange("(f c p) q -> f c p q", c=NC, p=128)
    w2i4 = w2i.rearrange("(b f p) q -> b f p q", f=NF, p=128)
    mts4 = mts.rearrange("(s c p) q -> s c p q", c=NC, p=128)
    ns4 = nscr.rearrange("(s f p) q -> s f p q", f=NF, p=128)
    x2s3 = x2s.rearrange("(t p) e -> t p e", p=128)
    out3 = out.rearrange("(t p) e -> t p e", p=128)
    wb_xb = wbuf[:, 0:8192].rearrange("p (k b q) -> p k b q", b=4, q=128)
    wb_wv = [wbuf[:, WV0:WV0 + 8192].rearrange("p (k q) -> p k q", q=512),
             wbuf[:, WV1:WV1 + 8192].rearrange("p (k q) -> p k q", q=512)]
    wb_kcw = wbuf[:, KCW:KCW + 2048].rearrange("p (k q) -> p k q", q=128)
    wb_wo = wbuf[:, :].rearrange("p (c e) -> p c e", e=2048)
    wb_mt = wbuf[:, MTSL:MTSL + 8192].rearrange("p (c q) -> p c q", q=512)
    wb_w1 = [wbuf[:, W1A:W1A + 2048].rearrange("p (c q) -> p c q", q=128),
             wbuf[:, W1B:W1B + 2048].rearrange("p (c q) -> p c q", q=128)]
    wb_w2 = wbuf[:, :].rearrange("p (f q) -> p f q", q=512)
    ar_n = arena[:, 0:NF * 512].rearrange("p (f q) -> p f q", q=512)
    ar_y = arena[:, :].rearrange("p (c h q) -> p c h q", h=H, q=128)

    for src, dst in (("ident", idt), ("ident32", idt32), ("addmask", amsk),
                     ("stmat", stm), ("trimask", trim),
                     ("ones_col", onec), ("ones_row", oner), ("ones32", one32),
                     ("costab", rtab[0:64, :]),
                     ("sintab", rtab[64:128, :])):
        op("sync", lambda e, s=src, d=dst: e.dma_start(out=d[:, :], in_=ap[s]),
           appends=[CONST], tag=f"ld_{src}", dma=True)
    for nm, coff in (("gq", 1024), ("gkv", 1152)):
        op("sync", lambda e, nm=nm, coff=coff: e.dma_start(
            out=stage32[0:16, coff:coff + 128],
            in_=ap[nm].rearrange("a (c p) -> (a c) p", p=128)),
           appends=[STG[0]], tag=f"ld_{nm}", dma=True)
    op("tensor", lambda e: e.transpose(
        pp[5][:, 0:16], stage32[0:16, 1024:1152], idt32[0:16, 0:16]),
       reads=[STG[0], CONST], writes=[PP[5]], tag="tp_gq")
    op("vector", lambda e: e.tensor_copy(gq_pc[:, :], pp[5][:, 0:16]),
       reads=[PP[5]], writes=[GQ], tag="ev_gq")
    op("tensor", lambda e: e.transpose(
        pp[5][:, 0:16], stage32[0:16, 1152:1280], idt32[0:16, 0:16]),
       reads=[STG[0], CONST], writes=[PP[5]], tag="tp_gkv")
    op("vector", lambda e: e.tensor_copy(gkv_pc[:, :], pp[5][:, 0:16]),
       reads=[PP[5]], writes=[GKV], tag="ev_gkv")
    op("sync", lambda e: e.dma_start(out=sv[0:1, 0:1], in_=ap["s1"]),
       appends=[SVB], tag="ld_s1", dma=True)
    op("sync", lambda e: e.dma_start(out=sv[0:1, 1:2], in_=ap["s2"]),
       appends=[SVB], tag="ld_s2", dma=True)
    op("sync", lambda e: e.dma_start(
        out=stage32[:, 0:NC * QR].rearrange("p (c q) -> p c q", q=QR),
        in_=ap["wqa"].rearrange("(c p) q -> c p q", p=128).rearrange(
            "c p q -> p c q")),
       writes=[STG[0]], tag="ld_wqa", dma=True)
    op("sync", lambda e: e.dma_start(out=stage32[0:64, 2048:2048 + E],
                                     in_=ap["wqb"]),
       writes=[STG[1]], tag="ld_wqb", dma=True)
    op("gpsimd", lambda e: e.dma_start(out=wkvb_b[:, :], in_=ap["wkvb"]),
       writes=[WKVB], tag="ld_wkvb", dma=True)
    for j in range(4):
        op("gpsimd", lambda e, j=j: e.dma_start(
            out=wb_kcw[:, 4 * j:4 * (j + 1), :],
            in_=wkva3[4 * j:4 * (j + 1), :, 0:KR].rearrange("c p q -> p c q")),
           writes=[KCWS[j]], tag=f"ld_kcw{j}", dma=True)

    for c in range(NC):
        op("vector", lambda e, c=c: e.tensor_scalar(
            wqa_b[:, c * QR:(c + 1) * QR], stage32[:, c * QR:(c + 1) * QR],
            gq_pc[:, c:c + 1], None, ALU.mult),
           reads=[STG[0], GQ], appends=[WQA], tag=f"f_wqa{c}")
    op("vector", lambda e: e.tensor_scalar(
        wqb_b[:, :], stage32[0:64, 2048:2048 + E], ISD, None, ALU.mult),
       reads=[STG[1]], writes=[WQB], tag="f_wqb")

    def bcast(src_slice, col, reads, tag, bank=5):
        op("tensor", lambda e, s=src_slice, c=col, bank=bank: e.matmul(
            pp[bank][:, c:c + 1], one32[:, :], s, start=True, stop=True),
           reads=[CONST] + reads, writes=[PP[bank]], tag=f"bcm_{tag}")
        op("vector", lambda e, c=col, bank=bank: e.tensor_copy(
            scal[:, c:c + 1], pp[bank][:, c:c + 1]),
           reads=[PP[bank]], appends=[SCAL], tag=f"bcc_{tag}")

    bcast(sv[0:1, 0:1], 0, [SVB], "s1")
    bcast(sv[0:1, 1:2], 1, [SVB], "s2")
    op("vector", lambda e: e.reciprocal(scal[:, 2:3], scal[:, 0:1]),
       reads=[SCAL], appends=[SCAL], tag="rs1")
    op("vector", lambda e: e.reciprocal(scal[:, 3:4], scal[:, 1:2]),
       reads=[SCAL], appends=[SCAL], tag="rs2")

    def ln_stats(rbuf_ap, rtoks, tag, dump_ap=None, dump_bufs=None):
        op("vector", lambda e: e.tensor_scalar(
            dump_ap, rbuf_ap, 1.0, None, ALU.mult, ALU.add,
            accum_out=stats[:, 0:1]),
           reads=rtoks, writes=[STATS] + dump_bufs, tag=f"{tag}_sum")
        op("vector", lambda e: e.scalar_tensor_tensor(
            dump_ap, rbuf_ap, 1.0, rbuf_ap, ALU.mult, ALU.mult,
            accum_out=stats[:, 1:2]),
           reads=rtoks, writes=dump_bufs, appends=[STATS], tag=f"{tag}_ssq")
        op("vector", lambda e: e.tensor_scalar(stats[:, 2:3], stats[:, 0:1],
                                               1.0 / E, None, ALU.mult),
           reads=[STATS], appends=[STATS], tag=f"{tag}_mu")
        op("vector", lambda e: e.tensor_scalar(stats[:, 3:4], stats[:, 2:3],
                                               stats[:, 2:3], EPS,
                                               ALU.mult, ALU.subtract),
           reads=[STATS], appends=[STATS], tag=f"{tag}_mu2")
        op("vector", lambda e: e.tensor_scalar(stats[:, 4:5], stats[:, 1:2],
                                               1.0 / E, stats[:, 3:4],
                                               ALU.mult, ALU.subtract),
           reads=[STATS], appends=[STATS], tag=f"{tag}_var")
        op("scalar", lambda e: e.activation(stats[:, 5:6], stats[:, 4:5],
                                            AF.Sqrt),
           reads=[STATS], appends=[STATS], tag=f"{tag}_sig")
        op("vector", lambda e: e.reciprocal(stats[:, 6:7], stats[:, 5:6]),
           reads=[STATS], appends=[STATS], tag=f"{tag}_rsig")

    op("sync", lambda e: e.dma_start(out=stage32[:, 0:2048], in_=xr3[0]),
       writes=[STG[0]], tag="ldx0", dma=True)

    for ts in range(NS):
        for tbi in range(4):
            tb = ts * 4 + tbi
            par = tb % 2
            xt = stage32[:, par * 2048:(par + 1) * 2048]
            ln_stats(xt, [STG[par]], f"A{tb}", xh2[:, :], [XHB2])
            op("vector", lambda e, xt=xt: e.tensor_scalar(
                xh[:, :], xt, stats[:, 2:3], stats[:, 6:7],
                ALU.subtract, ALU.mult),
               reads=[STG[par], STATS], writes=[XHB], tag=f"A{tb}_xh")
            if tb + 1 < NT:
                npar = (tb + 1) % 2
                op("sync", lambda e, npar=npar, tb=tb: e.dma_start(
                    out=stage32[:, npar * 2048:(npar + 1) * 2048],
                    in_=xr3[tb + 1]),
                   writes=[STG[npar]], tag=f"ldx{tb + 1}", dma=True)
            for gch in range(4):
                for j in range(4):
                    k = gch * 4 + j
                    op("tensor", lambda e, k=k, j=j: e.transpose(
                        ppbt[0][:, j * 128:(j + 1) * 128],
                        xh[:, k * 128:(k + 1) * 128], idt[:, :]),
                       reads=[XHB, CONST],
                       writes=[PPB[0]] if j == 0 else [],
                       appends=[PPB[0]] if j > 0 else [],
                       tag=f"A{tb}_tp{k}")
                op("scalar", lambda e, gch=gch, tbi=tbi: e.copy(
                    wb_xb[:, gch * 4:(gch + 1) * 4, tbi, :],
                    ppbt[0][:, :]),
                   reads=[PPB[0]], writes=[XB[gch][tbi]],
                   tag=f"A{tb}_xe{gch}")
        xb_all = [XB[g][b] for g in range(4) for b in range(4)]
        for k in range(NC):
            op("tensor", lambda e, k=k: e.matmul(
                pp[4][:, :], wbuf[:, KCW + k * KR:KCW + (k + 1) * KR],
                wbuf[:, k * 512:(k + 1) * 512],
                start=(k == 0), stop=(k == NC - 1)),
               reads=[KCWS[k // 4]] + XB[k // 4],
               writes=[PP[4]] if k == 0 else [],
               appends=[PP[4]] if k > 0 else [], tag=f"A{ts}_kc{k}")
        op("scalar", lambda e, ts=ts: e.copy(kct[:, ts * 512:(ts + 1) * 512],
                                             pp[4][:, :]),
           reads=[PP[4]], appends=[KCT], tag=f"A{ts}_kce")
        for k in range(NC):
            op("tensor", lambda e, k=k: e.matmul(
                pp[5][0:64, :], wqa_b[:, k * QR:(k + 1) * QR],
                wbuf[:, k * 512:(k + 1) * 512],
                start=(k == 0), stop=(k == NC - 1)),
               reads=[WQA] + XB[k // 4],
               writes=[PP[5]] if k == 0 else [],
               appends=[PP[5]] if k > 0 else [], tag=f"A{ts}_tt{k}")
        op("scalar", lambda e, ts=ts: e.copy(tT[:, ts * 512:(ts + 1) * 512],
                                             pp[5][0:64, :]),
           reads=[PP[5]], appends=[TTB], tag=f"A{ts}_tte")
        for vcb in range(4):
            wv = vcb % 2
            wv_off = WV0 if wv == 0 else WV1
            wvch = 4 + 4 * wv
            for j in range(4):
                op("gpsimd", lambda e, vcb=vcb, wv=wv, j=j: e.dma_start(
                    out=wb_wv[wv][:, 4 * j:4 * (j + 1), :],
                    in_=wkva3[4 * j:4 * (j + 1), :,
                              KR + vcb * 512:KR + (vcb + 1) * 512]
                    .rearrange("c p q -> p c q")),
                   writes=[WCH[wvch + j]], tag=f"A{ts}_wv{vcb}_{j}", dma=True)
            for k in range(NC):
                op("vector", lambda e, k=k, wv_off=wv_off: e.tensor_scalar(
                    wbuf[:, wv_off + k * 512:wv_off + (k + 1) * 512],
                    wbuf[:, wv_off + k * 512:wv_off + (k + 1) * 512],
                    gkv_pc[:, k:k + 1], None, ALU.mult),
                   reads=[WCH[wvch + k // 4], GKV],
                   appends=[WCH[wvch + k // 4]],
                   tag=f"A{ts}_gf{vcb}_{k}")
            for k in range(NC):
                for tbi in range(4):
                    op("tensor",
                       lambda e, k=k, tbi=tbi, wv_off=wv_off: e.matmul(
                           pp[tbi][:, :],
                           wbuf[:, k * 512 + tbi * 128:k * 512 + tbi * 128 + 128],
                           wbuf[:, wv_off + k * 512:wv_off + (k + 1) * 512],
                           start=(k == 0), stop=(k == NC - 1)),
                       reads=[XB[k // 4][tbi], WCH[wvch + k // 4]],
                       writes=[PP[tbi]] if k == 0 else [],
                       appends=[PP[tbi]] if k > 0 else [],
                       tag=f"A{ts}_v{vcb}_{k}_{tbi}")
            for tbi in range(4):
                tb = ts * 4 + tbi
                op("scalar", lambda e, tb=tb, vcb=vcb, tbi=tbi: e.copy(
                    arena[:, tb * 2048 + vcb * 512:tb * 2048 + (vcb + 1) * 512],
                    pp[tbi][:, :]),
                   reads=[PP[tbi]], appends=[ARENA, ACH[tb]],
                   tag=f"A{tb}_ve{vcb}")

    for c in range(NC):
        extra = []
        if c < 4:
            extra = XB[c][:]
        if c == 12:
            extra = KCWS[:]
        if c < 2:
            extra = extra + W1S[c][:]
        op("gpsimd", lambda e, c=c: e.dma_start(
            out=wbuf[:, c * 2048:(c + 1) * 2048], in_=wo3[c]),
           writes=[WCH[c]] + extra, tag=f"ld_wo{c}", dma=True)

    w1_jobs = ([("max", i) for i in range(64)] + [("fin", 0)] +
               [("quant", i) for i in range(64)])
    w1_state = {"i": 0}

    def w1_step():
        if w1_state["i"] >= len(w1_jobs):
            return
        kind, i = w1_jobs[w1_state["i"]]
        w1_state["i"] += 1
        if kind == "max":
            par = 0
            sl = slice(0, 2048)
            op("sync", lambda e, i=i, sl=sl: e.dma_start(
                out=stage32[:, sl],
                in_=w1r[i // 4, :, (i % 4) * 2048:(i % 4 + 1) * 2048]),
               writes=[STG[par]], tag=f"w1l{i}", dma=True)
            op("vector", lambda e, i=i, sl=sl: e.tensor_reduce(
                wmx[:, i:i + 1], stage32[:, sl], AX.X, ALU.max,
                apply_absolute_value=True),
               reads=[STG[par]], appends=[WMX], tag=f"w1m{i}")
        elif kind == "fin":
            op("vector", lambda e: e.tensor_reduce(
                wmx[:, 64:65], wmx[:, 0:64], AX.X, ALU.max),
               reads=[WMX], appends=[WMX], tag="w1fin")
            op("gpsimd", lambda e: e.tensor_reduce(
                sv[0:1, 4:5], wmx[:, 64:65], AX.C, ALU.max),
               reads=[WMX], appends=[SVB], tag="w1fing")
            bcast(sv[0:1, 4:5], 4, [SVB], "w1max", bank=2)
            op("vector", lambda e: e.tensor_scalar(
                scal[:, 4:5], scal[:, 4:5], 1.0 / 7.0, None, ALU.mult),
               reads=[SCAL], appends=[SCAL], tag="sw1")
            op("vector", lambda e: e.tensor_tensor(
                scal[:, 5:6], scal[:, 0:1], scal[:, 4:5], ALU.mult),
               reads=[SCAL], appends=[SCAL], tag="a1")
            op("vector", lambda e: e.reciprocal(scal[:, 6:7], scal[:, 4:5]),
               reads=[SCAL], appends=[SCAL], tag="rsw1")
        else:
            par = 0
            sl = slice(0, 2048)
            dch, q4 = i // 4, i % 4
            op("sync", lambda e, dch=dch, q4=q4, sl=sl: e.dma_start(
                out=stage32[:, sl],
                in_=w1r[dch, :, q4 * 2048:(q4 + 1) * 2048]),
               writes=[STG[par]], tag=f"w1ql{i}", dma=True)
            op("gpsimd", lambda e, sl=sl: e.tensor_scalar(
                stage32[:, sl], stage32[:, sl], scal[:, 6:7], MAGIC,
                ALU.mult, ALU.add),
               reads=[STG[par], SCAL], appends=[STG[par]], tag=f"w1q1_{i}")
            for st_ in range(2):
                op("vector", lambda e, sl=sl, st_=st_: e.tensor_scalar(
                    i32t[:, st_ * 512:(st_ + 1) * 512 + 512]
                    if False else i32t[:, :],
                    stage32[:, sl.start + st_ * 1024:sl.start + (st_ + 1) * 1024],
                    MAGIC, None, ALU.subtract),
                   reads=[STG[par]], writes=[I32H[0], I32H[1]],
                   tag=f"w1q2_{i}_{st_}")
                op("gpsimd", lambda e, st_=st_: e.tensor_scalar(
                    xh[:, st_ * 1024:(st_ + 1) * 1024], i32t[:, :],
                    7.0, -8.0, ALU.min, ALU.max),
                   reads=[I32H[0], I32H[1]], writes=[XHB] if st_ == 0 else [],
                   appends=[XHB] if st_ > 0 else [], tag=f"w1q3_{i}_{st_}")
            for jj in range(2):
                op("gpsimd", lambda e, dch=dch, q4=q4, jj=jj: e.dma_start(
                    out=w1i4[q4 * 16 + 8 * jj:q4 * 16 + 8 * (jj + 1), dch, :, :]
                    .rearrange("c p q -> p c q"),
                    in_=xh[:, 1024 * jj:1024 * (jj + 1)]
                    .rearrange("p (c q) -> p c q", q=128)),
                   reads=[XHB], appends=[SCR["w1i"]], tag=f"w1qs_{i}_{jj}",
                   dma=True)

    def qk_proj(h):
        Qh, Kh = QhA[h % 2], KhA[h % 2]
        QHB, KHB = QHBA[h % 2], KHBA[h % 2]
        for q4 in range(4):
            op("tensor", lambda e, h=h, q4=q4: e.matmul(
                pp[q4][:, :], wqb_b[:, h * 128:(h + 1) * 128],
                tT[:, q4 * 512:(q4 + 1) * 512], start=True, stop=True),
               reads=[WQB, TTB], writes=[PP[q4]], tag=f"B{h}q{q4}")
        for q4 in range(4):
            op("scalar", lambda e, q4=q4, Qh=Qh: e.copy(
                Qh[:, q4 * 512:(q4 + 1) * 512], pp[q4][:, :]),
               reads=[PP[q4]], writes=[QHB[q4]], tag=f"B{h}qe{q4}")
        for q4 in range(4):
            op("tensor", lambda e, h=h, q4=q4: e.matmul(
                pp[q4][:, :], wkvb_b[:, h * 128:(h + 1) * 128],
                kct[:, q4 * 512:(q4 + 1) * 512], start=True, stop=True),
               reads=[WKVB, KCT], writes=[PP[q4]], tag=f"B{h}k{q4}")
        for q4 in range(4):
            op("scalar", lambda e, q4=q4, Kh=Kh: e.copy(
                Kh[:, q4 * 512:(q4 + 1) * 512], pp[q4][:, :]),
               reads=[PP[q4]], writes=[KHB[q4]], tag=f"B{h}ke{q4}")
    def qk_rope(h):
        Qh, Kh = QhA[h % 2], KhA[h % 2]
        QHB, KHB = QHBA[h % 2], KHBA[h % 2]
        for tname, tbuf, TB in (("q", Qh, QHB), ("k", Kh, KHB)):
            for sp in range(4):
                sl = slice(sp * 512, (sp + 1) * 512)
                bk = sp % 4
                pslot = sp % 4
                op("tensor", lambda e, tbuf=tbuf, sl=sl, bk=bk: e.matmul(
                    pp[bk][0:64, :], stm[:, :], tbuf[0:64, sl],
                    start=True, stop=True),
                   reads=[TB[sp], CONST], writes=[PP[bk]],
                   tag=f"B{h}rr{tname}{sp}")
                op("vector", lambda e, sl=sl, bk=bk, pslot=pslot: e.tensor_tensor(
                    mstg[0:64, (pslot % 2) * 512:(pslot % 2 + 1) * 512],
                    pp[bk][0:64, :], rtab[64:64 + 64, sl], ALU.mult),
                   reads=[PP[bk], CONST], writes=[MSTG[pslot % 2]],
                   tag=f"B{h}r1{tname}{sp}")
                op("vector", lambda e, tbuf=tbuf, sl=sl, pslot=pslot:
                   e.scalar_tensor_tensor(
                       xh2[0:64, (pslot % 4) * 512:(pslot % 4 + 1) * 512],
                       tbuf[0:64, sl], 1.0, rtab[0:64, sl], ALU.mult, ALU.mult),
                   reads=[TB[sp], CONST], writes=[XHB2] if sp == 0 and
                   tname == "q" else [],
                   appends=[] if sp == 0 and tname == "q" else [XHB2],
                   tag=f"B{h}r2{tname}{sp}")
                op("vector", lambda e, tbuf=tbuf, sl=sl, pslot=pslot:
                   e.tensor_tensor(
                       tbuf[0:64, sl],
                       mstg[0:64, (pslot % 2) * 512:(pslot % 2 + 1) * 512],
                       xh2[0:64, (pslot % 4) * 512:(pslot % 4 + 1) * 512],
                       ALU.add),
                   reads=[MSTG[pslot % 2], XHB2], appends=[TB[sp]],
                   tag=f"B{h}r3{tname}{sp}")

    def attention_head(h, rope_cb=None):
        Qh, Kh = QhA[h % 2], KhA[h % 2]
        QHB, KHB = QHBA[h % 2], KHBA[h % 2]
        state = {"tav": None, "rope": rope_cb}

        def s1_part(qt, qtl, h=h):
            kext = (qt + 1) * 128
            nkb = (kext + 511) // 512
            dslot = qtl % 2
            for kb in range(nkb):
                n = min(512, kext - kb * 512)
                op("tensor", lambda e, qt=qt, kb=kb, n=n: e.matmul(
                    pp[kb][:, 0:n], Qh[:, qt * 128:(qt + 1) * 128],
                    Kh[:, kb * 512:kb * 512 + n], start=True, stop=True),
                   reads=[QHB[qt // 4], KHB[kb]], writes=[PP[kb]],
                   tag=f"B{h}s1_{qt}_{kb}")
            lkb = nkb - 1
            ln_ = min(512, kext - lkb * 512)
            op("vector", lambda e, lkb=lkb, ln_=ln_: e.tensor_tensor(
                pp[lkb][:, ln_ - 128:ln_], pp[lkb][:, ln_ - 128:ln_],
                amsk[:, :], ALU.add),
               reads=[CONST], writes=[PP[lkb]], tag=f"B{h}msk_{qt}")
            for kb in range(nkb):
                n = min(512, kext - kb * 512)
                if nkb == 1:
                    op("vector", lambda e, kb=kb, n=n, qt=qt: e.tensor_reduce(
                        mneg[:, qt:qt + 1], pp[kb][:, 0:n], AX.X, ALU.max,
                        negate=True),
                       reads=[PP[kb]], writes=[MNEG[qt]], tag=f"B{h}s1r_{qt}")
                else:
                    op("vector", lambda e, kb=kb, n=n: e.tensor_reduce(
                        mneg[:, 16 + kb:17 + kb], pp[kb][:, 0:n], AX.X,
                        ALU.max, negate=True),
                       reads=[PP[kb]], writes=[MNEGP[kb]],
                       tag=f"B{h}s1r_{qt}_{kb}")
            if nkb > 1:
                op("vector", lambda e, qt=qt, nkb=nkb: e.tensor_reduce(
                    mneg[:, qt:qt + 1], mneg[:, 16:16 + nkb], AX.X, ALU.min),
                   reads=MNEGP[0:nkb], writes=[MNEG[qt]], tag=f"B{h}s1c_{qt}")
            for kb in range(nkb):
                n = min(512, kext - kb * 512)
                par = qt % 2
                op("scalar", lambda e, kb=kb, n=n, qt=qt, par=par:
                   e.activation(
                    Pr[:, par * 2048 + kb * 512:par * 2048 + kb * 512 + n],
                    pp[kb][:, 0:n], AF.Exp,
                    bias=mneg[:, qt:qt + 1],
                    accum_out=stats[:, 8 + kb:9 + kb]),
                   reads=[PP[kb], MNEG[qt]], writes=[PRB[par][kb], STATS]
                   if kb == 0 else [PRB[par][kb]],
                   appends=[] if kb == 0 else [STATS],
                   tag=f"B{h}ex_{qt}_{kb}")
            op("vector", lambda e, nkb=nkb: e.tensor_reduce(
                stats[:, 1:2], stats[:, 8:8 + nkb], AX.X, ALU.add),
               reads=[STATS], appends=[STATS], tag=f"B{h}l_{qt}")
            op("vector", lambda e: e.reciprocal(stats[:, 2:3], stats[:, 1:2]),
               reads=[STATS], appends=[STATS], tag=f"B{h}rl_{qt}")
            op("vector", lambda e, dslot=dslot: e.tensor_scalar(
                dgt[:, dslot * 128:(dslot + 1) * 128], idt[:, :],
                stats[:, 2:3], None, ALU.mult),
               reads=[STATS, CONST], writes=[DGB[dslot]], tag=f"B{h}dg_{qt}")

        def tav_part(qt, qtl, qs, h=h):
            kext = (qt + 1) * 128
            nkt = kext // 128
            dslot = qtl % 2
            for g in range((nkt + 3) // 4):
                half = g % 2
                pt_ps = pp[5] if half == 0 else ppf
                PT_B = PP[5] if half == 0 else PPF
                cnt = min(4, nkt - g * 4)
                par = qt % 2
                for j in range(cnt):
                    kt = g * 4 + j
                    op("tensor", lambda e, kt=kt, j=j, pt_ps=pt_ps, dslot=dslot,
                       par=par: e.matmul(
                           pt_ps[:, j * 128:(j + 1) * 128],
                           Pr[:, par * 2048 + kt * 128:
                              par * 2048 + (kt + 1) * 128],
                           dgt[:, dslot * 128:(dslot + 1) * 128],
                           start=True, stop=True),
                       reads=[PRB[par][kt // 4], DGB[dslot], CONST],
                       writes=[PT_B] if j == 0 else [],
                       appends=[PT_B] if j > 0 else [],
                       tag=f"B{h}tp_{qt}_{kt}")
                pts = g % 2
                op("vector", lambda e, cnt=cnt, pts=pts, pt_ps=pt_ps:
                   e.tensor_copy(
                    PTt[:, pts * 512:pts * 512 + cnt * 128],
                    pt_ps[:, 0:cnt * 128]),
                   reads=[PT_B], writes=[PTRB[pts]],
                   tag=f"B{h}pte_{qt}_{g}")
                for j in range(cnt):
                    kt = g * 4 + j
                    op("tensor", lambda e, kt=kt, j=j, qtl=qtl, pts=pts, h=h,
                       nkt=nkt: e.matmul(
                           pp[4][:, qtl * 128:(qtl + 1) * 128],
                           arena[:, kt * 2048 + h * 128:
                                 kt * 2048 + (h + 1) * 128],
                           PTt[:, pts * 512 + j * 128:pts * 512 + (j + 1) * 128],
                           start=(kt == 0), stop=(kt == nkt - 1)),
                       reads=[ACH[kt], PTRB[pts]],
                       writes=[PP[4]] if kt == 0 and qtl == 0 else [],
                       appends=[] if kt == 0 and qtl == 0 else [PP[4]],
                       tag=f"B{h}av_{qt}_{kt}")

        for qs in range(NS - 1, -1, -1):
            for qti in range(4):
                qt = qs * 4 + qti
                s1_part(qt, qti)
                if state["rope"] is not None and qti == 3 and qs == NS - 1:
                    cb = state["rope"]
                    state["rope"] = None
                    cb()
                if state["tav"] is not None:
                    state["tav"]()
                state["tav"] = (lambda qt=qt, qti=qti, qs=qs:
                                tav_part(qt, qti, qs))
            state["tav"]()
            state["tav"] = None
            op("scalar", lambda e, h=h, qs=qs: e.copy(
                ar_y[:, qs * 4:qs * 4 + 4, h, :], pp[4][:, :]),
               reads=[PP[4]],
               writes=[ACH[qs * 4 + i] for i in range(4)],
               appends=[ARENA], tag=f"B{h}ye{qs}")

    qk_proj(0)
    qk_rope(0)
    for h in range(H):
        for _ in range(9):
            w1_step()
        if h + 1 < H:
            qk_proj(h + 1)
            attention_head(h, rope_cb=lambda hh=h + 1: qk_rope(hh))
        else:
            attention_head(h)

    def c_transposes(tb):
        xh_t = xh if tb % 2 == 0 else xh2
        XH_T = XHB if tb % 2 == 0 else XHB2
        for gch in range(4):
            half = gch % 2
            for j in range(4):
                k = gch * 4 + j
                op("tensor", lambda e, k=k, j=j, xh_t=xh_t:
                   e.transpose(
                    ppbt[0][:, j * 128:(j + 1) * 128],
                    xh_t[:, k * 128:(k + 1) * 128], idt[:, :]),
                   reads=[XH_T, CONST],
                   writes=[PPB[0]] if j == 0 else [],
                   appends=[PPB[0]] if j > 0 else [], tag=f"C{tb}tp{k}")
            op("vector", lambda e, half=half: e.tensor_copy(
                mstg[:, half * 512:(half + 1) * 512], ppbt[0][:, :]),
               reads=[PPB[0]], writes=[MSTG[half]], tag=f"C{tb}te{gch}")
            op("sync", lambda e, tb=tb, gch=gch, half=half: e.dma_start(
                out=mts4[tb // 4, gch * 4:(gch + 1) * 4, :,
                         (tb % 4) * 128:(tb % 4 + 1) * 128]
                .rearrange("c p q -> p c q"),
                in_=mstg[:, half * 512:(half + 1) * 512]
                .rearrange("p (c q) -> p c q", q=128)),
               reads=[MSTG[half]], appends=[SCR["mts"]], tag=f"C{tb}ts{gch}",
               dma=True)

    for tb in range(NT):
        r = stage32[:, 2048:4096]
        op("sync", lambda e, tb=tb: e.dma_start(out=stage32[:, 0:2048],
                                                in_=xr3[tb]),
           writes=[STG[0]], tag=f"C{tb}x", dma=True)
        for ecb in range(4):
            par = ecb % 2
            for yc in range(NC):
                op("tensor", lambda e, tb=tb, ecb=ecb, yc=yc, par=par: e.matmul(
                    pp[par][:, :],
                    arena[:, tb * 2048 + yc * 128:tb * 2048 + (yc + 1) * 128],
                    wbuf[:, yc * 2048 + ecb * 512:yc * 2048 + (ecb + 1) * 512],
                    start=(yc == 0), stop=(yc == NC - 1)),
                   reads=[ACH[tb], WCH[yc]],
                   writes=[PP[par]] if yc == 0 else [],
                   appends=[PP[par]] if yc > 0 else [],
                   tag=f"C{tb}wo{ecb}_{yc}")
            op("vector", lambda e, ecb=ecb, par=par, r=r: e.tensor_tensor(
                r[:, ecb * 512:(ecb + 1) * 512], pp[par][:, :],
                stage32[:, ecb * 512:(ecb + 1) * 512], ALU.add),
               reads=[PP[par], STG[0]],
               writes=([STG[1], RLBB, RLR] if tb == 0 else [STG[1]])
               if ecb == 0 else [],
               appends=[STG[1]] if ecb > 0 else [], tag=f"C{tb}re{ecb}")
        ln_stats(r, [STG[1]], f"C{tb}", QhA[0][:, :], QHBA[0])
        op("vector", lambda e, r=r: e.tensor_scalar(
            r, r, stats[:, 2:3], stats[:, 6:7], ALU.subtract, ALU.mult),
           reads=[STG[1], STATS], writes=[STG[1]], tag=f"C{tb}x2")
        op("sync", lambda e, tb=tb, r=r: e.dma_start(out=x2s3[tb], in_=r),
           reads=[STG[1]], appends=[SCR["x2s"]], tag=f"C{tb}x2s", dma=True)
        op("vector", lambda e, r=r: e.tensor_scalar(
            r, r, scal[:, 2:3], MAGIC, ALU.mult, ALU.add),
           reads=[STG[1], SCAL], writes=[STG[1]], tag=f"C{tb}mv")
        xh_t = xh if tb % 2 == 0 else xh2
        XH_T = XHB if tb % 2 == 0 else XHB2
        for st_ in range(2):
            op("vector", lambda e, r=r, st_=st_: e.tensor_scalar(
                i32t[:, :], r[:, st_ * 1024:(st_ + 1) * 1024],
                MAGIC, None, ALU.subtract),
               reads=[STG[1]], writes=[I32H[0], I32H[1]], tag=f"C{tb}mi{st_}")
            op("vector", lambda e, xh_t=xh_t, st_=st_: e.tensor_scalar(
                xh_t[:, st_ * 1024:(st_ + 1) * 1024], i32t[:, :],
                7.0, -8.0, ALU.min, ALU.max),
               reads=[I32H[0], I32H[1]], writes=[XH_T] if st_ == 0 else [],
               appends=[XH_T] if st_ > 0 else [], tag=f"C{tb}mc{st_}")
        if tb >= 1:
            c_transposes(tb - 1)
    c_transposes(NT - 1)

    op("sync", lambda e: e.dma_start(
        out=stage32[0:64, 0:128],
        in_=ap["b1"].rearrange("a (c p) -> (a c) p", p=128)),
       writes=[STG[0]], tag="ld_b1", dma=True)
    op("tensor", lambda e: e.transpose(
        pp[5][:, 0:64], stage32[0:64, 0:128], idt32[0:64, 0:64]),
       reads=[STG[0], CONST], writes=[PP[5]], tag="tp_b1")
    op("vector", lambda e: e.tensor_copy(b1qt[:, :], pp[5][:, 0:64]),
       reads=[PP[5]], writes=[B1Q], tag="ev_b1")
    op("vector", lambda e: e.reciprocal(scal[:, 7:8], scal[:, 5:6]),
       reads=[SCAL], appends=[SCAL], tag="ra1")
    op("vector", lambda e: e.tensor_scalar(
        b1qt[:, :], b1qt[:, :], scal[:, 7:8], MAGIC, ALU.mult, ALU.add),
       reads=[B1Q, SCAL], writes=[B1Q], tag="b1q1")
    op("vector", lambda e: e.tensor_scalar(
        i32t[:, 0:64], b1qt[:, :], MAGIC, None, ALU.subtract),
       reads=[B1Q], writes=[I32H[0]], tag="b1q2")
    op("vector", lambda e: e.tensor_scalar(
        b1qt[:, :], i32t[:, 0:64], CLIP32, -CLIP32, ALU.min, ALU.max),
       reads=[I32H[0]], writes=[B1Q], tag="b1q3")
    op("vector", lambda e: e.tensor_scalar(
        b1qt[:, :], b1qt[:, :], scal[:, 5:6], None, ALU.mult),
       reads=[B1Q, SCAL], writes=[B1Q], tag="b1q4")

    w2m_state = {"i": 0}

    def w2m_step():
        i = w2m_state["i"]
        if i > NF:
            return
        w2m_state["i"] += 1
        if i < NF:
            op("gpsimd", lambda e, i=i: e.dma_start(out=stage32[:, 0:2048],
                                                    in_=w2r[i]),
               writes=[STG[0]], tag=f"w2l{i}", dma=True)
            op("vector", lambda e, i=i: e.tensor_reduce(
                wmx[:, i:i + 1], stage32[:, 0:2048], AX.X, ALU.max,
                apply_absolute_value=True),
               reads=[STG[0]], writes=[WMX] if i == 0 else [],
               appends=[WMX] if i > 0 else [], tag=f"w2m{i}")
            return
        op("vector", lambda e: e.tensor_reduce(
            wmx[:, 65:66], wmx[:, 0:64], AX.X, ALU.max),
           reads=[WMX], appends=[WMX], tag="w2fin")
        op("gpsimd", lambda e: e.tensor_reduce(
            sv[0:1, 8:9], wmx[:, 65:66], AX.C, ALU.max),
           reads=[WMX], appends=[SVB], tag="w2fing")
        bcast(sv[0:1, 8:9], 8, [SVB], "w2max")
        op("vector", lambda e: e.tensor_scalar(
            scal[:, 8:9], scal[:, 8:9], 1.0 / 7.0, None, ALU.mult),
           reads=[SCAL], appends=[SCAL], tag="sw2")
        op("vector", lambda e: e.tensor_tensor(
            scal[:, 9:10], scal[:, 1:2], scal[:, 8:9], ALU.mult),
           reads=[SCAL], appends=[SCAL], tag="b2s")
        op("vector", lambda e: e.reciprocal(scal[:, 10:11], scal[:, 8:9]),
           reads=[SCAL], appends=[SCAL], tag="rsw2")

    w2q_state = {"i": 0}

    def w2q_step():
        if w2q_state["i"] >= NF:
            return
        i = w2q_state["i"]
        w2q_state["i"] += 1
        op("gpsimd", lambda e, i=i: e.dma_start(out=stage32[:, 0:2048],
                                                in_=w2r[i]),
           writes=[STG[0]], tag=f"w2ql{i}", dma=True)
        op("vector", lambda e: e.tensor_scalar(
            stage32[:, 0:2048], stage32[:, 0:2048], scal[:, 10:11], MAGIC,
            ALU.mult, ALU.add),
           reads=[STG[0], SCAL], appends=[STG[0]], tag=f"w2q1_{i}")
        for st_ in range(2):
            op("vector", lambda e, st_=st_: e.tensor_scalar(
                i32t[:, :], stage32[:, st_ * 1024:(st_ + 1) * 1024],
                MAGIC, None, ALU.subtract),
               reads=[STG[0]], writes=[I32H[0], I32H[1]], tag=f"w2q2_{i}_{st_}")
            op("vector", lambda e, st_=st_: e.tensor_scalar(
                xh[:, st_ * 1024:(st_ + 1) * 1024], i32t[:, :],
                7.0, -8.0, ALU.min, ALU.max),
               reads=[I32H[0], I32H[1]], writes=[XHB] if st_ == 0 else [],
               appends=[XHB] if st_ > 0 else [], tag=f"w2q3_{i}_{st_}")
        op("gpsimd", lambda e, i=i: e.dma_start(
            out=w2i4[:, i, :, :].rearrange("b p q -> p b q"),
            in_=xh[:, :].rearrange("p (b q) -> p b q", q=512)),
           reads=[XHB], appends=[SCR["w2i"]], tag=f"w2qs_{i}", dma=True)

    for tsup in range(NS):
        for j in range(4):
            op("sync", lambda e, tsup=tsup, j=j: e.dma_start(
                out=wb_mt[:, 4 * j:4 * (j + 1), :],
                in_=mts4[tsup][4 * j:4 * (j + 1)].rearrange("c p q -> p c q")),
               reads=[SCR["mts"]], writes=[WCH[4 + j]],
               tag=f"D{tsup}mtl{j}", dma=True)
        def d_slab_load(fch, tsup=tsup):
            wslot = fch % 2
            for jj in range(2):
                op("sync", lambda e, fch=fch, wslot=wslot, jj=jj: e.dma_start(
                    out=wb_w1[wslot][:, 8 * jj:8 * (jj + 1), :],
                    in_=w1i4[fch][8 * jj:8 * (jj + 1)]
                    .rearrange("c p q -> p c q")),
                   reads=[SCR["w1i"]], writes=[W1S[wslot][jj]],
                   tag=f"D{tsup}_{fch}wl{jj}", dma=True)

        def d_nstore(fch, tsup=tsup):
            t1 = fch % 2
            op("gpsimd", lambda e, tsup=tsup, fch=fch, t1=t1: e.dma_start(
                out=ns4[tsup, fch], in_=mstg[:, t1 * 512:(t1 + 1) * 512]),
               reads=[MSTG[t1]], appends=[SCR["nscr"]], tag=f"D{tsup}_{fch}ns",
               dma=True)

        d_slab_load(0)
        for fch in range(NF):
            wslot = fch % 2
            woff = W1A if wslot == 0 else W1B
            if fch + 1 < NF:
                d_slab_load(fch + 1)
            par = fch % 2
            for dch in range(NC):
                op("tensor", lambda e, dch=dch, woff=woff, par=par: e.matmul(
                    pp[par][:, :],
                    wbuf[:, woff + dch * 128:woff + (dch + 1) * 128],
                    wbuf[:, MTSL + dch * 512:MTSL + (dch + 1) * 512],
                    start=(dch == 0), stop=(dch == NC - 1)),
                   reads=[W1S[wslot][dch // 8], WCH[4 + dch // 4]],
                   writes=[PP[par]] if dch == 0 else [],
                   appends=[PP[par]] if dch > 0 else [],
                   tag=f"D{tsup}_{fch}m{dch}")
            t1 = fch % 2
            t1sl = slice(2048 + t1 * 512, 2048 + (t1 + 1) * 512)
            wrt = [T1B[t1]] + ([STG[1]] if tsup == 0 and fch < 2 else [])
            op("scalar", lambda e, fch=fch, par=par, t1sl=t1sl: e.activation(
                stage32[:, t1sl], pp[par][:, :], AF.Relu,
                bias=b1qt[:, fch:fch + 1], scale=scal[:, 5:6]),
               reads=[PP[par], B1Q, SCAL], writes=wrt,
               tag=f"D{tsup}_{fch}relu")
            op("vector", lambda e, t1sl=t1sl: e.tensor_scalar(
                stage32[:, t1sl], stage32[:, t1sl], scal[:, 3:4], MAGIC,
                ALU.mult, ALU.add),
               reads=[T1B[t1], SCAL], appends=[T1B[t1]], tag=f"D{tsup}_{fch}q1")
            op("vector", lambda e, t1=t1, t1sl=t1sl: e.tensor_scalar(
                i32t[:, t1 * 512:(t1 + 1) * 512], stage32[:, t1sl], MAGIC,
                None, ALU.subtract),
               reads=[T1B[t1]], writes=[I32H[t1]], tag=f"D{tsup}_{fch}q2")
            op("vector", lambda e, t1=t1: e.tensor_scalar(
                mstg[:, t1 * 512:(t1 + 1) * 512],
                i32t[:, t1 * 512:(t1 + 1) * 512], 15.0, 0.0, ALU.min, ALU.max),
               reads=[I32H[t1]], writes=[MSTG[t1]], tag=f"D{tsup}_{fch}q3")
            if fch >= 1:
                d_nstore(fch - 1)
            if tsup < 2:
                w2m_step()
            elif fch % 2 == 1:
                w2q_step()
        d_nstore(NF - 1)

    op("sync", lambda e: e.dma_start(out=r2sup[0:1, 0:2048], in_=ap["b2"]),
       writes=[R2T[0]], tag="ld_b2", dma=True)
    op("vector", lambda e: e.tensor_scalar(
        r2sup[0:1, 0:2048], r2sup[0:1, 0:2048], scal[0:1, 10:11], MAGIC,
        ALU.mult, ALU.add),
       reads=[R2T[0], SCAL], writes=[R2T[0]], tag="b2q1")
    for st_ in range(2):
        op("vector", lambda e, st_=st_: e.tensor_scalar(
            i32t[0:1, :], r2sup[0:1, st_ * 1024:(st_ + 1) * 1024],
            MAGIC, None, ALU.subtract),
           reads=[R2T[0]], writes=[I32H[0], I32H[1]], tag=f"b2q2_{st_}")
        op("vector", lambda e, st_=st_: e.tensor_scalar(
            r2sup[0:1, st_ * 1024:(st_ + 1) * 1024], i32t[0:1, :],
            CLIP32, -CLIP32, ALU.min, ALU.max),
           reads=[I32H[0], I32H[1]], writes=[R2T[0]], tag=f"b2q3_{st_}")
    op("vector", lambda e: e.tensor_scalar(
        r2sup[0:1, 0:2048], r2sup[0:1, 0:2048], scal[0:1, 9:10], None,
        ALU.mult),
       reads=[R2T[0], SCAL], writes=[R2T[0]], tag="b2q4")
    for q4 in range(4):
        op("tensor", lambda e, q4=q4: e.matmul(
            pp[5][:, :], one32[:, :], r2sup[0:1, q4 * 512:(q4 + 1) * 512],
            start=True, stop=True),
           reads=[CONST, R2T[0]], writes=[PP[5]], tag=f"b2b{q4}")
        op("vector", lambda e, q4=q4: e.tensor_copy(
            b2qb[:, q4 * 512:(q4 + 1) * 512], pp[5][:, :]),
           reads=[PP[5]], writes=[B2QB] if q4 == 0 else [],
           appends=[B2QB] if q4 > 0 else [], tag=f"b2be{q4}")

    for tsup in range(NS):
        for c in range(NT):
            op("gpsimd", lambda e, tsup=tsup, c=c: e.dma_start(
                out=ar_n[:, 4 * c:4 * (c + 1), :],
                in_=ns4[tsup][4 * c:4 * (c + 1)].rearrange("f p q -> p f q")),
               reads=[SCR["nscr"]], writes=[ACH[c]], tag=f"E{tsup}nl{c}",
               dma=True)
        for ecb in range(4):
            for c in range(NC):
                extra = []
                if tsup == 0 and ecb == 0 and c < 2:
                    extra = W1S[c][:]
                op("sync", lambda e, ecb=ecb, c=c: e.dma_start(
                    out=wb_w2[:, 4 * c:4 * (c + 1), :],
                    in_=w2i4[ecb][4 * c:4 * (c + 1)]
                    .rearrange("f p q -> p f q")),
                   reads=[SCR["w2i"]], writes=[WCH[c]] + extra,
                   tag=f"E{tsup}_{ecb}wl{c}", dma=True)
            for tbi in range(4):
                par = tbi % 2 + (2 if ecb % 2 else 0)
                for fch in range(NF):
                    op("tensor",
                       lambda e, tbi=tbi, fch=fch, par=par: e.matmul(
                           pp[par][:, :],
                           arena[:, fch * 512 + tbi * 128:
                                 fch * 512 + tbi * 128 + 128],
                           wbuf[:, fch * 512:(fch + 1) * 512],
                           start=(fch == 0), stop=(fch == NF - 1)),
                       reads=[ACH[fch // 4], WCH[fch // 4]],
                       writes=[PP[par]] if fch == 0 else [],
                       appends=[PP[par]] if fch > 0 else [],
                       tag=f"E{tsup}_{ecb}_{tbi}m{fch}")
                op("vector", lambda e, tbi=tbi, ecb=ecb, par=par:
                   e.tensor_scalar(
                       r2sup[:, tbi * 2048 + ecb * 512:
                             tbi * 2048 + (ecb + 1) * 512],
                       pp[par][:, :], scal[:, 9:10], None, ALU.mult),
                   reads=[PP[par], SCAL],
                   writes=[R2T[tbi]] if ecb == 0 else [],
                   appends=[R2T[tbi]] if ecb > 0 else [],
                   tag=f"E{tsup}_{ecb}_{tbi}ev")
        for tbi in range(4):
            tb = tsup * 4 + tbi
            r2 = r2sup[:, tbi * 2048:(tbi + 1) * 2048]
            op("vector", lambda e, r2=r2: e.tensor_tensor(
                r2, r2, b2qb[:, :], ALU.add),
               reads=[R2T[tbi], B2QB], writes=[R2T[tbi]], tag=f"E{tb}badd")
            op("sync", lambda e, tb=tb: e.dma_start(
                out=stage32[:, 0:2048], in_=x2s3[tb]),
               reads=[SCR["x2s"]], writes=[STG[0]], tag=f"E{tb}x2l", dma=True)
            op("vector", lambda e, r2=r2: e.tensor_tensor(
                r2, r2, stage32[:, 0:2048], ALU.add),
               reads=[R2T[tbi], STG[0]], writes=[R2T[tbi]], tag=f"E{tb}radd")
            ln_stats(r2, [R2T[tbi]], f"E{tb}",
                     stage32[:, 2048:4096], [STG[1], T1B[0], T1B[1]])
            op("vector", lambda e, r2=r2: e.tensor_scalar(
                r2, r2, stats[:, 2:3], stats[:, 6:7], ALU.subtract, ALU.mult),
               reads=[R2T[tbi], STATS], writes=[R2T[tbi]], tag=f"E{tb}x3")
            op("sync", lambda e, tb=tb, r2=r2: e.dma_start(out=out3[tb],
                                                           in_=r2),
               reads=[R2T[tbi]], tag=f"E{tb}out", dma=True)

    t.finalize()

    with nc.Block() as block:
        @block.sync
        def _(eng):
            t.emit("sync", eng)
            for q in ("hw", "sw"):
                for i in range(NLANES):
                    lane = f"{q}{i}"
                    if t.totals[lane]:
                        eng.wait_ge(t.sems[lane], t.totals[lane])

        @block.gpsimd
        def _(eng):
            t.emit("gpsimd", eng)
            for i in range(NLANES):
                lane = f"sw{i}"
                if t.totals[lane]:
                    eng.wait_ge(t.sems[lane], t.totals[lane])

        @block.tensor
        def _(eng):
            t.emit("tensor", eng)

        @block.vector
        def _(eng):
            t.emit("vector", eng)

        @block.scalar
        def _(eng):
            t.emit("scalar", eng)

    return nc


_cached = {}


def kernel(**inputs):
    if "nc" not in _cached:
        _cached["nc"] = build()
    nc = _cached["nc"]
    f32 = np.float32
    x = np.ascontiguousarray(np.asarray(inputs["L0_attn_x"], dtype=f32))
    wkva = np.asarray(inputs["L0_attn_w_kva"], dtype=f32)
    kva_used = np.ascontiguousarray(np.concatenate(
        [wkva[:, 0:KR]] +
        [wkva[:, KR + h * 2 * D:KR + h * 2 * D + D] for h in range(H)], axis=1))
    base = {
        "gq": np.ascontiguousarray(np.asarray(inputs["L0_attn_g_q"], f32)[None, :]),
        "gkv": np.ascontiguousarray(np.asarray(inputs["L0_attn_g_kv"], f32)[None, :]),
        "wqa": np.ascontiguousarray(np.asarray(inputs["L0_attn_w_qa"], f32)),
        "wqb": np.ascontiguousarray(np.asarray(inputs["L0_attn_w_qb"], f32)),
        "wkva": kva_used,
        "wkvb": np.ascontiguousarray(np.asarray(inputs["L0_attn_w_kvb"], f32)),
        "wo": np.ascontiguousarray(np.asarray(inputs["L0_attn_w_o"], f32)),
        "w1t": np.ascontiguousarray(np.asarray(inputs["L0_ffn_W1"], f32).T),
        "b1": np.ascontiguousarray(np.asarray(inputs["L0_ffn_b1"], f32)[None, :]),
        "w2t": np.ascontiguousarray(np.asarray(inputs["L0_ffn_W2"], f32).T),
        "b2": np.ascontiguousarray(np.asarray(inputs["L0_ffn_b2"], f32)[None, :]),
        "s1": np.ascontiguousarray(np.asarray(inputs["L0_ffn_s_act1"], f32)[:, None]),
        "s2": np.ascontiguousarray(np.asarray(inputs["L0_ffn_s_act2"], f32)[:, None]),
    }
    in_maps = []
    for c in range(8):
        m = dict(base)
        m["x"] = np.ascontiguousarray(x[c])
        in_maps.append(m)
    res = run_bass_kernel_spmd(nc, in_maps, core_ids=list(range(8)))
    return np.stack([np.asarray(res.results[c]["out"]) for c in range(8)],
                    axis=0)


# version 15
# speedup vs baseline: 1.2477x; 1.2477x over previous
import numpy as np
import ml_dtypes
import concourse.bass as bass
import concourse.mybir as mybir
from concourse import dve_ops
from concourse.bass_utils import run_bass_kernel_spmd

F32 = mybir.dt.float32
BF16 = mybir.dt.bfloat16
I32 = mybir.dt.int32
AF = mybir.ActivationFunctionType
ALU = mybir.AluOpType
AX = mybir.AxisListType

MAGIC = 12582912.0
CLIP32 = 2147483520.0
S, E, H, D, QR, KR, RD, F = 2048, 2048, 16, 128, 64, 128, 64, 8192
EPS = 1e-5
ISD = 0.08838834764831845
NT, NS, NC, NF = S // 128, S // 512, E // 128, F // 128
VCOLS = H * D
WKVA_C = KR + VCOLS

ENGINES = ("sync", "gpsimd", "tensor", "vector", "scalar")
ENG_SEM = {"tensor": "pe", "vector": "dv", "scalar": "ac", "gpsimd": "gp",
           "sync": "sy"}
DMA_QUEUE = {"sync": "hw", "gpsimd": "sw"}
NLANES = 8


class Buf:
    __slots__ = ("name", "writers", "readers")

    def __init__(self, name):
        self.name = name
        self.writers = []
        self.readers = []


class Op:
    __slots__ = ("eng", "fn", "deps", "tag", "dma", "inc_sem", "inc_val",
                 "_clk", "_wait")

    def __init__(self, eng, fn, tag, dma):
        self.eng = eng
        self.fn = fn
        self.deps = []
        self.tag = tag
        self.dma = dma
        self.inc_sem = None
        self.inc_val = 0
        self._clk = {}
        self._wait = []


class Trk:
    def __init__(self, nc):
        self.nc = nc
        self.ops = {e: [] for e in ENGINES}
        names = ["pe", "dv", "ac", "gp", "sy"]
        for q in ("hw", "sw"):
            names += [f"{q}{i}" for i in range(NLANES)]
        self.sems = {s: nc.alloc_semaphore(s) for s in names}
        self.dma_count = {"hw": 0, "sw": 0}
        self.lane_last = {f"{q}{i}": None for q in ("hw", "sw")
                          for i in range(NLANES)}

    def op(self, eng, fn, reads=(), writes=(), appends=(), tag="", dma=False):
        o = Op(eng, fn, tag, dma)
        deps = set()
        for b in reads:
            deps.update(b.writers)
        for b in writes:
            deps.update(b.writers)
            deps.update(b.readers)
        if dma:
            q = DMA_QUEUE[eng]
            lane = f"{q}{self.dma_count[q] % NLANES}"
            self.dma_count[q] += 1
            o.inc_sem = lane
            prev = self.lane_last[lane]
            if prev is not None:
                deps.add(prev)
            self.lane_last[lane] = o
        o.deps = list(deps)
        self.ops[eng].append(o)
        for b in reads:
            b.readers.append(o)
        for b in writes:
            b.writers = [o]
            b.readers = []
        for b in appends:
            b.writers.append(o)
        return o

    def finalize(self):
        needs = set()
        for e in ENGINES:
            for o in self.ops[e]:
                needs.update(o.deps)
        cnt = {s: 0 for s in self.sems}
        for e in ENGINES:
            for o in self.ops[e]:
                if o.dma:
                    cnt[o.inc_sem] += 16
                    o.inc_val = cnt[o.inc_sem]
                elif o in needs:
                    s = ENG_SEM[e]
                    cnt[s] += 1
                    o.inc_sem, o.inc_val = s, cnt[s]
        idx = {e: 0 for e in ENGINES}
        clock = {e: {} for e in ENGINES}
        done = set()
        n_total = sum(len(self.ops[e]) for e in ENGINES)
        n_done = 0
        progress = True
        while progress:
            progress = False
            for e in ENGINES:
                while idx[e] < len(self.ops[e]):
                    o = self.ops[e][idx[e]]
                    if not all(d in done for d in o.deps):
                        break
                    ck = clock[e]
                    w = []
                    for dd in sorted(o.deps,
                                     key=lambda d: (d.inc_sem, d.inc_val)):
                        s, v = dd.inc_sem, dd.inc_val
                        if ck.get(s, 0) < v:
                            w.append((s, v))
                            ck[s] = v
                            for s2, v2 in dd._clk.items():
                                if ck.get(s2, 0) < v2:
                                    ck[s2] = v2
                    o._wait = w
                    o._clk = dict(ck)
                    if o.inc_sem:
                        o._clk[o.inc_sem] = max(o._clk.get(o.inc_sem, 0),
                                                o.inc_val)
                    done.add(o)
                    idx[e] += 1
                    n_done += 1
                    progress = True
        if n_done != n_total:
            stuck = {e: self.ops[e][idx[e]].tag for e in ENGINES
                     if idx[e] < len(self.ops[e])}
            raise RuntimeError(f"build-order deadlock: {stuck}")
        self.totals = dict(cnt)

    def emit(self, engine, eng):
        for o in self.ops[engine]:
            for s, v in o._wait[1:]:
                eng.wait_ge(self.sems[s], v)
            ins = o.fn(eng)
            if o._wait:
                s, v = o._wait[0]
                ins._wait_ge(self.sems[s], v)
            if o.inc_sem:
                ins.then_inc(self.sems[o.inc_sem], 16 if o.dma else 1)


def _consts():
    c = {}
    c["ident"] = np.eye(128, dtype=ml_dtypes.bfloat16)
    c["ident32"] = np.eye(128, dtype=np.float32)
    Smat = np.zeros((RD, RD), dtype=np.float32)
    half = RD // 2
    for i in range(half):
        Smat[i, i + half] = -1.0
        Smat[i + half, i] = 1.0
    c["stmat"] = np.ascontiguousarray(Smat.T.astype(ml_dtypes.bfloat16))
    c["trimask"] = np.ascontiguousarray(
        np.triu(np.ones((128, 128), np.float32)).astype(ml_dtypes.bfloat16))
    am = np.triu(np.full((128, 128), -1e30, np.float32), 1)
    c["addmask"] = np.ascontiguousarray(am.astype(ml_dtypes.bfloat16))
    c["ones_col"] = np.ones((128, 1), dtype=ml_dtypes.bfloat16)
    c["ones_row"] = np.ones((1, 128), dtype=ml_dtypes.bfloat16)
    c["ones32"] = np.ones((1, 128), dtype=np.float32)
    inv_freq = 1.0 / (10000.0 ** (np.arange(0, RD, 2, dtype=np.float64) / RD))
    freqs = np.arange(S, dtype=np.float64)[:, None] * inv_freq[None, :]
    cs, sn = np.cos(freqs).T, np.sin(freqs).T
    c["costab"] = np.ascontiguousarray(
        np.concatenate([cs, cs], 0).astype(ml_dtypes.bfloat16))
    c["sintab"] = np.ascontiguousarray(
        np.concatenate([sn, sn], 0).astype(ml_dtypes.bfloat16))
    return c


def build():
    nc = bass.Bass("TRN2", target_bir_lowering=False)
    t = Trk(nc)

    ap = {}
    for name, shape in (("x", [S, E]), ("gq", [1, E]), ("gkv", [1, E]),
                        ("wqa", [E, QR]), ("wqb", [QR, E]),
                        ("wkva", [E, WKVA_C]), ("wkvb", [KR, E]),
                        ("wo", [E, E]), ("w1t", [E, F]), ("b1", [1, F]),
                        ("w2t", [F, E]), ("b2", [1, E]), ("s1", [1, 1]),
                        ("s2", [1, 1])):
        ap[name] = nc.dram_tensor(name, shape, F32, kind="ExternalInput").ap()
    out = nc.dram_tensor("out", [S, E], F32, kind="ExternalOutput").ap()
    for k, v in _consts().items():
        ap[k] = nc.inline_tensor(v, name=f"c_{k}").ap()

    x2s = nc.dram_tensor("x2s", [S, E], F32, kind="Internal").ap()
    mts = nc.dram_tensor("mts", [NS * NC * 128, 512], BF16, kind="Internal").ap()
    w1i = nc.dram_tensor("w1i", [NF * NC * 128, 128], BF16, kind="Internal").ap()
    w2i = nc.dram_tensor("w2i", [4 * NF * 128, 512], BF16, kind="Internal").ap()
    nscr = nc.dram_tensor("nscr", [NS * NF * 128, 512], BF16, kind="Internal").ap()

    AL = nc.alloc_sbuf_tensor
    arena = AL("arena", [128, NT * 2048], BF16)
    wbuf = AL("wbuf", [128, 32768], BF16)
    stage32 = AL("stage32", [128, 4096], F32)
    xh = AL("xh", [128, 2048], BF16)
    i32t = AL("i32t", [128, 1024], I32)
    xh2 = AL("xh2", [128, 2048], BF16)
    mstg = AL("mstg", [128, 1024], BF16)
    scal = AL("scal", [128, 16], F32)
    stats = AL("stats", [128, 32], F32)
    sv = AL("svals", [1, 16], F32)
    wmx = AL("wmx", [128, 66], F32)
    idt = AL("idt", [128, 128], BF16)
    idt32 = AL("idt32", [128, 128], F32)
    stm = AL("stm", [64, 64], BF16)
    trim = AL("trim", [128, 128], BF16)
    amsk = AL("amsk", [128, 128], BF16)
    onec = AL("onec", [128, 1], BF16)
    oner = AL("oner", [1, 128], BF16)
    one32 = AL("one32", [1, 128], F32)
    guards = []

    def tmp_alloc(name, shape, dt):
        g = nc.sbuf_tensor(name, shape, dt)
        guards.append(g)
        return g.__enter__()
    kct = tmp_alloc("kct", [128, 2048], BF16)
    tT = tmp_alloc("tTq", [64, 2048], BF16)
    wqb_b = tmp_alloc("wqb_b", [64, 2048], BF16)
    wkvb_b = tmp_alloc("wkvb_b", [128, 2048], BF16)
    ga = nc.sbuf_tensor("wqa_b", [128, NC * QR], BF16)
    wqa_b = ga.__enter__()
    gb = nc.sbuf_tensor("gq_pc", [128, NC], F32)
    gq_pc = gb.__enter__()
    gc = nc.sbuf_tensor("gkv_pc", [128, NC], F32)
    gkv_pc = gc.__enter__()
    gc.__exit__(None, None, None)
    gb.__exit__(None, None, None)
    ga.__exit__(None, None, None)
    QhA = [tmp_alloc("Qh0", [128, 2048], BF16), tmp_alloc("Qh1", [128, 2048], BF16)]
    KhA = [tmp_alloc("Kh0", [128, 2048], BF16), tmp_alloc("Kh1", [128, 2048], BF16)]
    Pr = tmp_alloc("Pr", [128, 4096], BF16)
    PTt = tmp_alloc("PTt", [128, 1024], BF16)
    dgt = tmp_alloc("dgt", [128, 512], BF16)
    rtab = tmp_alloc("rtab", [128, S], BF16)
    mneg = tmp_alloc("mneg", [128, 32], F32)
    for g in reversed(guards):
        g.__exit__(None, None, None)
    rlb = stage32[:, 3584:4096]
    r2sup = AL("r2sup", [128, 8192], F32)
    b2qb = AL("b2qb", [128, 2048], BF16)
    b1qt = AL("b1qt", [128, 64], F32)

    WV0, WV1, KCW = 8192, 16384, 24576
    W1A, W1B, MTSL = 0, 2048, 8192

    pp = [nc.alloc_psum_tensor(f"pp{i}", [128, 512], F32) for i in range(6)]
    ppbt = [nc.alloc_psum_tensor("ppb0", [128, 512], BF16)]
    ppf = nc.alloc_psum_tensor("ppf", [128, 512], F32)
    p1 = pp[5][0:1, 0:512]

    PP = [Buf(f"pp{i}") for i in range(6)]
    PPB = [Buf("ppb0")]
    PPF = Buf("ppf")
    P1B = Buf("p1")
    CONST = Buf("const")
    ARENA = Buf("arena")
    STG = [Buf("stg0"), Buf("stg1")]
    T1B = [Buf("t1a"), Buf("t1b")]
    XHB = Buf("xh")
    XHB2 = Buf("xh2")
    I32H = [Buf("i32a"), Buf("i32b")]
    MSTG = [Buf("mstg0"), Buf("mstg1")]
    STATS = Buf("stats")
    SCAL = Buf("scal")
    SVB = Buf("sv")
    WMX = Buf("wmx")
    KCT = Buf("kct")
    TTB = Buf("tT")
    WQA = Buf("wqa_b")
    WQB = Buf("wqb_b")
    WKVB = Buf("wkvb_b")
    GQ = Buf("gq")
    GKV = Buf("gkv")
    QHBA = [[Buf(f"Qh{i}_{j}") for j in range(4)] for i in range(2)]
    KHBA = [[Buf(f"Kh{i}_{j}") for j in range(4)] for i in range(2)]
    PRB = [Buf(f"Pr{i}") for i in range(8)]
    PTRB = [Buf("PT0"), Buf("PT1")]
    DGB = [Buf(f"dg{i}") for i in range(4)]
    MNEG = [Buf(f"mneg{i}") for i in range(16)]
    MNEGP = [[Buf(f"mnegp{s_}_{i}") for i in range(4)] for s_ in range(4)]
    LPB = [Buf(f"lp{i}") for i in range(4)]
    LB_ = [Buf(f"lb{i}") for i in range(4)]
    RLQ = [Buf(f"rlq{i}") for i in range(4)]
    RLR = Buf("rlrow")
    RLBB = Buf("rlb")
    XB = [[Buf(f"xb{g}_{b}") for b in range(4)] for g in range(4)]
    WCH = [Buf(f"wch{i}") for i in range(16)]
    ACH = [Buf(f"ach{i}") for i in range(NT)]
    KCWS = [Buf(f"kcws{i}") for i in range(4)]
    W1S = [[Buf(f"w1s{s_}_{j}") for j in range(2)] for s_ in range(2)]
    B1Q = Buf("b1q")
    B2QB = Buf("b2qb")
    R2T = [Buf(f"r2_{i}") for i in range(4)]
    SCR = {n: Buf("scr_" + n) for n in ("x2s", "mts", "w1i", "w2i", "nscr")}

    op = t.op

    xr3 = ap["x"].rearrange("(t p) e -> t p e", p=128)
    wkva3 = ap["wkva"].rearrange("(c p) q -> c p q", p=128)
    wo3 = ap["wo"].rearrange("(c p) e -> c p e", p=128)
    w1r = ap["w1t"].rearrange("(c p) f -> c p f", p=128)
    w2r = ap["w2t"].rearrange("(c p) d -> c p d", p=128)
    w1i4 = w1i.rearrange("(f c p) q -> f c p q", c=NC, p=128)
    w2i4 = w2i.rearrange("(b f p) q -> b f p q", f=NF, p=128)
    mts4 = mts.rearrange("(s c p) q -> s c p q", c=NC, p=128)
    ns4 = nscr.rearrange("(s f p) q -> s f p q", f=NF, p=128)
    x2s3 = x2s.rearrange("(t p) e -> t p e", p=128)
    out3 = out.rearrange("(t p) e -> t p e", p=128)
    wb_xb = wbuf[:, 0:8192].rearrange("p (k b q) -> p k b q", b=4, q=128)
    wb_wv = [wbuf[:, WV0:WV0 + 8192].rearrange("p (k q) -> p k q", q=512),
             wbuf[:, WV1:WV1 + 8192].rearrange("p (k q) -> p k q", q=512)]
    wb_kcw = wbuf[:, KCW:KCW + 2048].rearrange("p (k q) -> p k q", q=128)
    wb_wo = wbuf[:, :].rearrange("p (c e) -> p c e", e=2048)
    wb_mt = wbuf[:, MTSL:MTSL + 8192].rearrange("p (c q) -> p c q", q=512)
    wb_w1 = [wbuf[:, W1A:W1A + 2048].rearrange("p (c q) -> p c q", q=128),
             wbuf[:, W1B:W1B + 2048].rearrange("p (c q) -> p c q", q=128)]
    wb_w2 = wbuf[:, :].rearrange("p (f q) -> p f q", q=512)
    ar_n = arena[:, 0:NF * 512].rearrange("p (f q) -> p f q", q=512)
    ar_y = arena[:, :].rearrange("p (c h q) -> p c h q", h=H, q=128)

    for src, dst in (("ident", idt), ("ident32", idt32), ("addmask", amsk),
                     ("stmat", stm), ("trimask", trim),
                     ("ones_col", onec), ("ones_row", oner), ("ones32", one32),
                     ("costab", rtab[0:64, :]),
                     ("sintab", rtab[64:128, :])):
        op("sync", lambda e, s=src, d=dst: e.dma_start(out=d[:, :], in_=ap[s]),
           appends=[CONST], tag=f"ld_{src}", dma=True)
    for nm, coff in (("gq", 1024), ("gkv", 1152)):
        op("sync", lambda e, nm=nm, coff=coff: e.dma_start(
            out=stage32[0:16, coff:coff + 128],
            in_=ap[nm].rearrange("a (c p) -> (a c) p", p=128)),
           appends=[STG[0]], tag=f"ld_{nm}", dma=True)
    op("tensor", lambda e: e.transpose(
        pp[5][:, 0:16], stage32[0:16, 1024:1152], idt32[0:16, 0:16]),
       reads=[STG[0], CONST], writes=[PP[5]], tag="tp_gq")
    op("vector", lambda e: e.tensor_copy(gq_pc[:, :], pp[5][:, 0:16]),
       reads=[PP[5]], writes=[GQ], tag="ev_gq")
    op("tensor", lambda e: e.transpose(
        pp[5][:, 0:16], stage32[0:16, 1152:1280], idt32[0:16, 0:16]),
       reads=[STG[0], CONST], writes=[PP[5]], tag="tp_gkv")
    op("vector", lambda e: e.tensor_copy(gkv_pc[:, :], pp[5][:, 0:16]),
       reads=[PP[5]], writes=[GKV], tag="ev_gkv")
    op("sync", lambda e: e.dma_start(out=sv[0:1, 0:1], in_=ap["s1"]),
       appends=[SVB], tag="ld_s1", dma=True)
    op("sync", lambda e: e.dma_start(out=sv[0:1, 1:2], in_=ap["s2"]),
       appends=[SVB], tag="ld_s2", dma=True)
    op("sync", lambda e: e.dma_start(
        out=stage32[:, 0:NC * QR].rearrange("p (c q) -> p c q", q=QR),
        in_=ap["wqa"].rearrange("(c p) q -> c p q", p=128).rearrange(
            "c p q -> p c q")),
       writes=[STG[0]], tag="ld_wqa", dma=True)
    op("sync", lambda e: e.dma_start(out=stage32[0:64, 2048:2048 + E],
                                     in_=ap["wqb"]),
       writes=[STG[1]], tag="ld_wqb", dma=True)
    op("gpsimd", lambda e: e.dma_start(out=wkvb_b[:, :], in_=ap["wkvb"]),
       writes=[WKVB], tag="ld_wkvb", dma=True)
    for j in range(4):
        op("gpsimd", lambda e, j=j: e.dma_start(
            out=wb_kcw[:, 4 * j:4 * (j + 1), :],
            in_=wkva3[4 * j:4 * (j + 1), :, 0:KR].rearrange("c p q -> p c q")),
           writes=[KCWS[j]], tag=f"ld_kcw{j}", dma=True)

    for c in range(NC):
        op("vector", lambda e, c=c: e.tensor_scalar(
            wqa_b[:, c * QR:(c + 1) * QR], stage32[:, c * QR:(c + 1) * QR],
            gq_pc[:, c:c + 1], None, ALU.mult),
           reads=[STG[0], GQ], appends=[WQA], tag=f"f_wqa{c}")
    op("vector", lambda e: e.tensor_scalar(
        wqb_b[:, :], stage32[0:64, 2048:2048 + E], ISD, None, ALU.mult),
       reads=[STG[1]], writes=[WQB], tag="f_wqb")

    def bcast(src_slice, col, reads, tag, bank=5):
        op("tensor", lambda e, s=src_slice, c=col, bank=bank: e.matmul(
            pp[bank][:, c:c + 1], one32[:, :], s, start=True, stop=True),
           reads=[CONST] + reads, writes=[PP[bank]], tag=f"bcm_{tag}")
        op("vector", lambda e, c=col, bank=bank: e.tensor_copy(
            scal[:, c:c + 1], pp[bank][:, c:c + 1]),
           reads=[PP[bank]], appends=[SCAL], tag=f"bcc_{tag}")

    bcast(sv[0:1, 0:1], 0, [SVB], "s1")
    bcast(sv[0:1, 1:2], 1, [SVB], "s2")
    op("vector", lambda e: e.reciprocal(scal[:, 2:3], scal[:, 0:1]),
       reads=[SCAL], appends=[SCAL], tag="rs1")
    op("vector", lambda e: e.reciprocal(scal[:, 3:4], scal[:, 1:2]),
       reads=[SCAL], appends=[SCAL], tag="rs2")

    def ln_stats(rbuf_ap, rtoks, tag, dump_ap=None, dump_bufs=None):
        op("vector", lambda e: e.tensor_scalar(
            dump_ap, rbuf_ap, 1.0, None, ALU.mult, ALU.add,
            accum_out=stats[:, 0:1]),
           reads=rtoks, writes=[STATS] + dump_bufs, tag=f"{tag}_sum")
        op("vector", lambda e: e.scalar_tensor_tensor(
            dump_ap, rbuf_ap, 1.0, rbuf_ap, ALU.mult, ALU.mult,
            accum_out=stats[:, 1:2]),
           reads=rtoks, writes=dump_bufs, appends=[STATS], tag=f"{tag}_ssq")
        op("vector", lambda e: e.tensor_scalar(stats[:, 2:3], stats[:, 0:1],
                                               1.0 / E, None, ALU.mult),
           reads=[STATS], appends=[STATS], tag=f"{tag}_mu")
        op("vector", lambda e: e.tensor_scalar(stats[:, 3:4], stats[:, 2:3],
                                               stats[:, 2:3], EPS,
                                               ALU.mult, ALU.subtract),
           reads=[STATS], appends=[STATS], tag=f"{tag}_mu2")
        op("vector", lambda e: e.tensor_scalar(stats[:, 4:5], stats[:, 1:2],
                                               1.0 / E, stats[:, 3:4],
                                               ALU.mult, ALU.subtract),
           reads=[STATS], appends=[STATS], tag=f"{tag}_var")
        op("scalar", lambda e: e.activation(stats[:, 5:6], stats[:, 4:5],
                                            AF.Sqrt),
           reads=[STATS], appends=[STATS], tag=f"{tag}_sig")
        op("vector", lambda e: e.reciprocal(stats[:, 6:7], stats[:, 5:6]),
           reads=[STATS], appends=[STATS], tag=f"{tag}_rsig")

    op("sync", lambda e: e.dma_start(out=stage32[:, 0:2048], in_=xr3[0]),
       writes=[STG[0]], tag="ldx0", dma=True)

    for ts in range(NS):
        for tbi in range(4):
            tb = ts * 4 + tbi
            par = tb % 2
            xt = stage32[:, par * 2048:(par + 1) * 2048]
            ln_stats(xt, [STG[par]], f"A{tb}", xh2[:, :], [XHB2])
            op("vector", lambda e, xt=xt: e.tensor_scalar(
                xh[:, :], xt, stats[:, 2:3], stats[:, 6:7],
                ALU.subtract, ALU.mult),
               reads=[STG[par], STATS], writes=[XHB], tag=f"A{tb}_xh")
            if tb + 1 < NT:
                npar = (tb + 1) % 2
                op("sync", lambda e, npar=npar, tb=tb: e.dma_start(
                    out=stage32[:, npar * 2048:(npar + 1) * 2048],
                    in_=xr3[tb + 1]),
                   writes=[STG[npar]], tag=f"ldx{tb + 1}", dma=True)
            for gch in range(4):
                for j in range(4):
                    k = gch * 4 + j
                    op("tensor", lambda e, k=k, j=j: e.transpose(
                        ppbt[0][:, j * 128:(j + 1) * 128],
                        xh[:, k * 128:(k + 1) * 128], idt[:, :]),
                       reads=[XHB, CONST],
                       writes=[PPB[0]] if j == 0 else [],
                       appends=[PPB[0]] if j > 0 else [],
                       tag=f"A{tb}_tp{k}")
                op("scalar", lambda e, gch=gch, tbi=tbi: e.copy(
                    wb_xb[:, gch * 4:(gch + 1) * 4, tbi, :],
                    ppbt[0][:, :]),
                   reads=[PPB[0]], writes=[XB[gch][tbi]],
                   tag=f"A{tb}_xe{gch}")
        xb_all = [XB[g][b] for g in range(4) for b in range(4)]
        for k in range(NC):
            op("tensor", lambda e, k=k: e.matmul(
                pp[4][:, :], wbuf[:, KCW + k * KR:KCW + (k + 1) * KR],
                wbuf[:, k * 512:(k + 1) * 512],
                start=(k == 0), stop=(k == NC - 1)),
               reads=[KCWS[k // 4]] + XB[k // 4],
               writes=[PP[4]] if k == 0 else [],
               appends=[PP[4]] if k > 0 else [], tag=f"A{ts}_kc{k}")
        op("scalar", lambda e, ts=ts: e.copy(kct[:, ts * 512:(ts + 1) * 512],
                                             pp[4][:, :]),
           reads=[PP[4]], appends=[KCT], tag=f"A{ts}_kce")
        for k in range(NC):
            op("tensor", lambda e, k=k: e.matmul(
                pp[5][0:64, :], wqa_b[:, k * QR:(k + 1) * QR],
                wbuf[:, k * 512:(k + 1) * 512],
                start=(k == 0), stop=(k == NC - 1)),
               reads=[WQA] + XB[k // 4],
               writes=[PP[5]] if k == 0 else [],
               appends=[PP[5]] if k > 0 else [], tag=f"A{ts}_tt{k}")
        op("scalar", lambda e, ts=ts: e.copy(tT[:, ts * 512:(ts + 1) * 512],
                                             pp[5][0:64, :]),
           reads=[PP[5]], appends=[TTB], tag=f"A{ts}_tte")
        for vcb in range(4):
            wv = vcb % 2
            wv_off = WV0 if wv == 0 else WV1
            wvch = 4 + 4 * wv
            for j in range(4):
                op("gpsimd", lambda e, vcb=vcb, wv=wv, j=j: e.dma_start(
                    out=wb_wv[wv][:, 4 * j:4 * (j + 1), :],
                    in_=wkva3[4 * j:4 * (j + 1), :,
                              KR + vcb * 512:KR + (vcb + 1) * 512]
                    .rearrange("c p q -> p c q")),
                   writes=[WCH[wvch + j]], tag=f"A{ts}_wv{vcb}_{j}", dma=True)
            for k in range(NC):
                op("vector", lambda e, k=k, wv_off=wv_off: e.tensor_scalar(
                    wbuf[:, wv_off + k * 512:wv_off + (k + 1) * 512],
                    wbuf[:, wv_off + k * 512:wv_off + (k + 1) * 512],
                    gkv_pc[:, k:k + 1], None, ALU.mult),
                   reads=[WCH[wvch + k // 4], GKV],
                   appends=[WCH[wvch + k // 4]],
                   tag=f"A{ts}_gf{vcb}_{k}")
            for k in range(NC):
                for tbi in range(4):
                    op("tensor",
                       lambda e, k=k, tbi=tbi, wv_off=wv_off: e.matmul(
                           pp[tbi][:, :],
                           wbuf[:, k * 512 + tbi * 128:k * 512 + tbi * 128 + 128],
                           wbuf[:, wv_off + k * 512:wv_off + (k + 1) * 512],
                           start=(k == 0), stop=(k == NC - 1)),
                       reads=[XB[k // 4][tbi], WCH[wvch + k // 4]],
                       writes=[PP[tbi]] if k == 0 else [],
                       appends=[PP[tbi]] if k > 0 else [],
                       tag=f"A{ts}_v{vcb}_{k}_{tbi}")
            for tbi in range(4):
                tb = ts * 4 + tbi
                op("scalar", lambda e, tb=tb, vcb=vcb, tbi=tbi: e.copy(
                    arena[:, tb * 2048 + vcb * 512:tb * 2048 + (vcb + 1) * 512],
                    pp[tbi][:, :]),
                   reads=[PP[tbi]], appends=[ARENA, ACH[tb]],
                   tag=f"A{tb}_ve{vcb}")

    for c in range(NC):
        extra = []
        if c < 4:
            extra = XB[c][:]
        if c == 12:
            extra = KCWS[:]
        if c < 2:
            extra = extra + W1S[c][:]
        op("gpsimd", lambda e, c=c: e.dma_start(
            out=wbuf[:, c * 2048:(c + 1) * 2048], in_=wo3[c]),
           writes=[WCH[c]] + extra, tag=f"ld_wo{c}", dma=True)

    w1_jobs = [("mdma", 0), ("mdma", 1)]
    for i in range(62):
        w1_jobs += [("mred", i), ("mdma", i + 2)]
    w1_jobs += [("mred", 62), ("mred", 63), ("fin", 0)]
    w1_jobs += [("qdma", 0), ("qdma", 1), ("qa", 0)]
    for i in range(63):
        w1_jobs.append(("qb", i))
        if i + 2 < 64:
            w1_jobs.append(("qdma", i + 2))
        w1_jobs.append(("qa", i + 1))
    w1_jobs.append(("qb", 63))
    w1_state = {"i": 0}

    def w1_step():
        if w1_state["i"] >= len(w1_jobs):
            return
        kind, i = w1_jobs[w1_state["i"]]
        w1_state["i"] += 1
        par = i % 2
        sl = slice(par * 2048, par * 2048 + 2048)
        if kind in ("mdma", "qdma"):
            op("sync", lambda e, i=i, sl=sl: e.dma_start(
                out=stage32[:, sl],
                in_=w1r[i // 4, :, (i % 4) * 2048:(i % 4 + 1) * 2048]),
               writes=[STG[par]], tag=f"w1l_{kind}{i}", dma=True)
        elif kind == "mred":
            op("vector", lambda e, i=i, sl=sl: e.tensor_reduce(
                wmx[:, i:i + 1], stage32[:, sl], AX.X, ALU.max,
                apply_absolute_value=True),
               reads=[STG[par]], appends=[WMX], tag=f"w1m{i}")
        elif kind == "fin":
            op("vector", lambda e: e.tensor_reduce(
                wmx[:, 64:65], wmx[:, 0:64], AX.X, ALU.max),
               reads=[WMX], appends=[WMX], tag="w1fin")
            op("gpsimd", lambda e: e.tensor_reduce(
                sv[0:1, 4:5], wmx[:, 64:65], AX.C, ALU.max),
               reads=[WMX], appends=[SVB], tag="w1fing")
            bcast(sv[0:1, 4:5], 4, [SVB], "w1max", bank=2)
            op("vector", lambda e: e.tensor_scalar(
                scal[:, 4:5], scal[:, 4:5], 1.0 / 7.0, None, ALU.mult),
               reads=[SCAL], appends=[SCAL], tag="sw1")
            op("vector", lambda e: e.tensor_tensor(
                scal[:, 5:6], scal[:, 0:1], scal[:, 4:5], ALU.mult),
               reads=[SCAL], appends=[SCAL], tag="a1")
            op("vector", lambda e: e.reciprocal(scal[:, 6:7], scal[:, 4:5]),
               reads=[SCAL], appends=[SCAL], tag="rsw1")
        elif kind == "qa":
            op("gpsimd", lambda e, sl=sl: e.tensor_scalar(
                stage32[:, sl], stage32[:, sl], scal[:, 6:7], MAGIC,
                ALU.mult, ALU.add),
               reads=[STG[par], SCAL], appends=[STG[par]], tag=f"w1q1_{i}")
        else:
            dch, q4 = i // 4, i % 4
            for st_ in range(2):
                op("vector", lambda e, sl=sl, st_=st_: e.tensor_scalar(
                    i32t[:, :],
                    stage32[:, sl.start + st_ * 1024:
                            sl.start + (st_ + 1) * 1024],
                    MAGIC, None, ALU.subtract),
                   reads=[STG[par]], writes=[I32H[0], I32H[1]],
                   tag=f"w1q2_{i}_{st_}")
                op("gpsimd", lambda e, st_=st_: e.tensor_scalar(
                    xh[:, st_ * 1024:(st_ + 1) * 1024], i32t[:, :],
                    7.0, -8.0, ALU.min, ALU.max),
                   reads=[I32H[0], I32H[1]], writes=[XHB] if st_ == 0 else [],
                   appends=[XHB] if st_ > 0 else [], tag=f"w1q3_{i}_{st_}")
            for jj in range(2):
                op("gpsimd", lambda e, dch=dch, q4=q4, jj=jj: e.dma_start(
                    out=w1i4[q4 * 16 + 8 * jj:q4 * 16 + 8 * (jj + 1), dch, :, :]
                    .rearrange("c p q -> p c q"),
                    in_=xh[:, 1024 * jj:1024 * (jj + 1)]
                    .rearrange("p (c q) -> p c q", q=128)),
                   reads=[XHB], appends=[SCR["w1i"]], tag=f"w1qs_{i}_{jj}",
                   dma=True)

    def qk_proj(h):
        Qh, Kh = QhA[h % 2], KhA[h % 2]
        QHB, KHB = QHBA[h % 2], KHBA[h % 2]
        for q4 in range(4):
            op("tensor", lambda e, h=h, q4=q4: e.matmul(
                pp[q4][:, :], wqb_b[:, h * 128:(h + 1) * 128],
                tT[:, q4 * 512:(q4 + 1) * 512], start=True, stop=True),
               reads=[WQB, TTB], writes=[PP[q4]], tag=f"B{h}q{q4}")
        for q4 in range(4):
            op("scalar", lambda e, q4=q4, Qh=Qh: e.copy(
                Qh[:, q4 * 512:(q4 + 1) * 512], pp[q4][:, :]),
               reads=[PP[q4]], writes=[QHB[q4]], tag=f"B{h}qe{q4}")
        for q4 in range(4):
            op("tensor", lambda e, h=h, q4=q4: e.matmul(
                pp[q4][:, :], wkvb_b[:, h * 128:(h + 1) * 128],
                kct[:, q4 * 512:(q4 + 1) * 512], start=True, stop=True),
               reads=[WKVB, KCT], writes=[PP[q4]], tag=f"B{h}k{q4}")
        for q4 in range(4):
            op("scalar", lambda e, q4=q4, Kh=Kh: e.copy(
                Kh[:, q4 * 512:(q4 + 1) * 512], pp[q4][:, :]),
               reads=[PP[q4]], writes=[KHB[q4]], tag=f"B{h}ke{q4}")
    def qk_rope(h):
        Qh, Kh = QhA[h % 2], KhA[h % 2]
        QHB, KHB = QHBA[h % 2], KHBA[h % 2]
        for tname, tbuf, TB in (("q", Qh, QHB), ("k", Kh, KHB)):
            for sp in range(4):
                sl = slice(sp * 512, (sp + 1) * 512)
                bk = sp % 4
                pslot = sp % 4
                op("tensor", lambda e, tbuf=tbuf, sl=sl, bk=bk: e.matmul(
                    pp[bk][0:64, :], stm[:, :], tbuf[0:64, sl],
                    start=True, stop=True),
                   reads=[TB[sp], CONST], writes=[PP[bk]],
                   tag=f"B{h}rr{tname}{sp}")
                op("vector", lambda e, sl=sl, bk=bk, pslot=pslot: e.tensor_tensor(
                    mstg[0:64, (pslot % 2) * 512:(pslot % 2 + 1) * 512],
                    pp[bk][0:64, :], rtab[64:64 + 64, sl], ALU.mult),
                   reads=[PP[bk], CONST], writes=[MSTG[pslot % 2]],
                   tag=f"B{h}r1{tname}{sp}")
                op("vector", lambda e, tbuf=tbuf, sl=sl, pslot=pslot:
                   e.scalar_tensor_tensor(
                       xh2[0:64, (pslot % 4) * 512:(pslot % 4 + 1) * 512],
                       tbuf[0:64, sl], 1.0, rtab[0:64, sl], ALU.mult, ALU.mult),
                   reads=[TB[sp], CONST], writes=[XHB2] if sp == 0 and
                   tname == "q" else [],
                   appends=[] if sp == 0 and tname == "q" else [XHB2],
                   tag=f"B{h}r2{tname}{sp}")
                op("vector", lambda e, tbuf=tbuf, sl=sl, pslot=pslot:
                   e.tensor_tensor(
                       tbuf[0:64, sl],
                       mstg[0:64, (pslot % 2) * 512:(pslot % 2 + 1) * 512],
                       xh2[0:64, (pslot % 4) * 512:(pslot % 4 + 1) * 512],
                       ALU.add),
                   reads=[MSTG[pslot % 2], XHB2], appends=[TB[sp]],
                   tag=f"B{h}r3{tname}{sp}")

    def attention_head(h, rope_cb=None, bg_quota=0):
        Qh, Kh = QhA[h % 2], KhA[h % 2]
        QHB, KHB = QHBA[h % 2], KHBA[h % 2]
        st_ctx = {"pend": [], "rope": rope_cb, "boff": 0, "gctr": 0,
                  "bgdone": 0}

        def prc(qt, qs, kt):
            return ((qt % 2) * 4 + kt // 4) if qs > 0 else qt

        def s1a(qt, qti, qs, h=h):
            kext = (qt + 1) * 128
            nkb = (kext + 511) // 512
            sl_ = qt % 4
            banks = [(st_ctx["boff"] + i) % 4 for i in range(nkb)]
            st_ctx["boff"] += nkb
            for kb in range(nkb):
                n = min(512, kext - kb * 512)
                b = banks[kb]
                op("tensor", lambda e, qt=qt, kb=kb, n=n, b=b: e.matmul(
                    pp[b][:, 0:n], Qh[:, qt * 128:(qt + 1) * 128],
                    Kh[:, kb * 512:kb * 512 + n], start=True, stop=True),
                   reads=[QHB[qt // 4], KHB[kb]], writes=[PP[b]],
                   tag=f"B{h}s1_{qt}_{kb}")
            lb = banks[nkb - 1]
            ln_ = min(512, kext - (nkb - 1) * 512)
            op("vector", lambda e, lb=lb, ln_=ln_: e.tensor_tensor(
                pp[lb][:, ln_ - 128:ln_], pp[lb][:, ln_ - 128:ln_],
                amsk[:, :], ALU.add),
               reads=[CONST], writes=[PP[lb]], tag=f"B{h}msk_{qt}")
            for kb in range(nkb):
                n = min(512, kext - kb * 512)
                b = banks[kb]
                if nkb == 1:
                    op("vector", lambda e, b=b, n=n, qt=qt: e.tensor_reduce(
                        mneg[:, qt:qt + 1], pp[b][:, 0:n], AX.X, ALU.max,
                        negate=True),
                       reads=[PP[b]], writes=[MNEG[qt]], tag=f"B{h}s1r_{qt}")
                else:
                    op("vector", lambda e, b=b, n=n, sl_=sl_, kb=kb:
                       e.tensor_reduce(
                        mneg[:, 16 + sl_ * 4 + kb:17 + sl_ * 4 + kb],
                        pp[b][:, 0:n], AX.X, ALU.max, negate=True),
                       reads=[PP[b]], writes=[MNEGP[sl_][kb]],
                       tag=f"B{h}s1r_{qt}_{kb}")
            if nkb > 1:
                op("vector", lambda e, qt=qt, nkb=nkb, sl_=sl_:
                   e.tensor_reduce(
                    mneg[:, qt:qt + 1],
                    mneg[:, 16 + sl_ * 4:16 + sl_ * 4 + nkb], AX.X, ALU.min),
                   reads=MNEGP[sl_][0:nkb], writes=[MNEG[qt]],
                   tag=f"B{h}s1c_{qt}")
            for kb in range(nkb):
                n = min(512, kext - kb * 512)
                b = banks[kb]
                c = prc(qt, qs, kb * 4)
                op("scalar", lambda e, b=b, n=n, qt=qt, c=c, sl_=sl_, kb=kb:
                   e.activation(
                    Pr[:, c * 512:c * 512 + n], pp[b][:, 0:n], AF.Exp,
                    bias=mneg[:, qt:qt + 1],
                    accum_out=stats[:, 8 + sl_ * 4 + kb:9 + sl_ * 4 + kb]),
                   reads=[PP[b], MNEG[qt]],
                   writes=[PRB[c], LPB[sl_]] if kb == 0 else [PRB[c]],
                   appends=[] if kb == 0 else [LPB[sl_]],
                   tag=f"B{h}ex_{qt}_{kb}")

        def s1b(qt, h=h):
            nkb = ((qt + 1) * 128 + 511) // 512
            sl_ = qt % 4
            op("vector", lambda e, nkb=nkb, sl_=sl_: e.tensor_reduce(
                stats[:, 24 + sl_:25 + sl_],
                stats[:, 8 + sl_ * 4:8 + sl_ * 4 + nkb], AX.X, ALU.add),
               reads=[LPB[sl_]], writes=[LB_[sl_]], tag=f"B{h}l_{qt}")
            op("vector", lambda e, sl_=sl_: e.reciprocal(
                stats[:, 28 + sl_:29 + sl_], stats[:, 24 + sl_:25 + sl_]),
               reads=[LB_[sl_]], writes=[RLQ[sl_]], tag=f"B{h}rl_{qt}")
            op("vector", lambda e, sl_=sl_: e.tensor_scalar(
                dgt[:, sl_ * 128:(sl_ + 1) * 128], idt[:, :],
                stats[:, 28 + sl_:29 + sl_], None, ALU.mult),
               reads=[RLQ[sl_], CONST], writes=[DGB[sl_]], tag=f"B{h}dg_{qt}")

        def tav_part(qt, qtl, qs, h=h):
            nkt = (qt + 1)
            sl_ = qt % 4

            def t_g(g, par2, cnt):
                pt_ps = pp[5] if par2 == 0 else ppf
                PT_B = PP[5] if par2 == 0 else PPF
                for j in range(cnt):
                    kt = g * 4 + j
                    c = prc(qt, qs, kt)
                    off = c * 512 + (kt % 4) * 128
                    op("tensor", lambda e, j=j, pt_ps=pt_ps, off=off, sl_=sl_,
                       c=c: e.matmul(
                           pt_ps[:, j * 128:(j + 1) * 128],
                           Pr[:, off:off + 128],
                           dgt[:, sl_ * 128:(sl_ + 1) * 128],
                           start=True, stop=True),
                       reads=[PRB[c], DGB[sl_], CONST],
                       writes=[PT_B] if j == 0 else [],
                       appends=[PT_B] if j > 0 else [],
                       tag=f"B{h}tp_{qt}_{kt}")
                op("vector", lambda e, cnt=cnt, par2=par2, pt_ps=pt_ps:
                   e.tensor_copy(
                    PTt[:, par2 * 512:par2 * 512 + cnt * 128],
                    pt_ps[:, 0:cnt * 128]),
                   reads=[PT_B], writes=[PTRB[par2]],
                   tag=f"B{h}pte_{qt}_{g}")

            def av_g(g, par2, cnt):
                for j in range(cnt):
                    kt = g * 4 + j
                    op("tensor", lambda e, kt=kt, j=j, qtl=qtl, par2=par2,
                       nkt=nkt: e.matmul(
                           pp[4][:, qtl * 128:(qtl + 1) * 128],
                           arena[:, kt * 2048 + h * 128:
                                 kt * 2048 + (h + 1) * 128],
                           PTt[:, par2 * 512 + j * 128:
                               par2 * 512 + (j + 1) * 128],
                           start=(kt == 0), stop=(kt == nkt - 1)),
                       reads=[ACH[kt], PTRB[par2]],
                       writes=[PP[4]] if kt == 0 and qtl == 0 else [],
                       appends=[] if kt == 0 and qtl == 0 else [PP[4]],
                       tag=f"B{h}av_{qt}_{kt}")

            prev = None
            for g in range((nkt + 3) // 4):
                par2 = st_ctx["gctr"] % 2
                st_ctx["gctr"] += 1
                cnt = min(4, nkt - g * 4)
                t_g(g, par2, cnt)
                if prev is not None:
                    av_g(*prev)
                prev = (g, par2, cnt)
            av_g(*prev)

        def y_evac(qs, h=h):
            op("scalar", lambda e, h=h, qs=qs: e.copy(
                ar_y[:, qs * 4:qs * 4 + 4, h, :], pp[4][:, :]),
               reads=[PP[4]],
               writes=[ACH[qs * 4 + i] for i in range(4)],
               appends=[ARENA], tag=f"B{h}ye{qs}")

        point = 0
        for qs in range(NS - 1, -1, -1):
            for qti in range(4):
                qt = qs * 4 + qti
                s1a(qt, qti, qs)
                if st_ctx["rope"] is not None and qti == 3 and qs == NS - 1:
                    cb = st_ctx["rope"]
                    st_ctx["rope"] = None
                    cb()
                for cb in st_ctx["pend"]:
                    cb()
                st_ctx["pend"] = []
                s1b(qt)
                st_ctx["pend"].append(
                    lambda qt=qt, qti=qti, qs=qs: tav_part(qt, qti, qs))
                if qti == 3:
                    st_ctx["pend"].append(lambda qs=qs: y_evac(qs))
                point += 1
                tgt = bg_quota * point // 16
                while st_ctx["bgdone"] < tgt:
                    w1_step()
                    st_ctx["bgdone"] += 1
        for cb in st_ctx["pend"]:
            cb()

    qk_proj(0)
    qk_rope(0)
    bgq = (len(w1_jobs) + H - 1) // H
    for h in range(H):
        if h + 1 < H:
            qk_proj(h + 1)
            attention_head(h, rope_cb=lambda hh=h + 1: qk_rope(hh),
                           bg_quota=bgq)
        else:
            attention_head(h, bg_quota=bgq)
    while w1_state["i"] < len(w1_jobs):
        w1_step()

    def c_transposes(tb):
        xh_t = xh if tb % 2 == 0 else xh2
        XH_T = XHB if tb % 2 == 0 else XHB2
        for gch in range(4):
            half = gch % 2
            for j in range(4):
                k = gch * 4 + j
                op("tensor", lambda e, k=k, j=j, xh_t=xh_t:
                   e.transpose(
                    ppbt[0][:, j * 128:(j + 1) * 128],
                    xh_t[:, k * 128:(k + 1) * 128], idt[:, :]),
                   reads=[XH_T, CONST],
                   writes=[PPB[0]] if j == 0 else [],
                   appends=[PPB[0]] if j > 0 else [], tag=f"C{tb}tp{k}")
            op("vector", lambda e, half=half: e.tensor_copy(
                mstg[:, half * 512:(half + 1) * 512], ppbt[0][:, :]),
               reads=[PPB[0]], writes=[MSTG[half]], tag=f"C{tb}te{gch}")
            op("sync", lambda e, tb=tb, gch=gch, half=half: e.dma_start(
                out=mts4[tb // 4, gch * 4:(gch + 1) * 4, :,
                         (tb % 4) * 128:(tb % 4 + 1) * 128]
                .rearrange("c p q -> p c q"),
                in_=mstg[:, half * 512:(half + 1) * 512]
                .rearrange("p (c q) -> p c q", q=128)),
               reads=[MSTG[half]], appends=[SCR["mts"]], tag=f"C{tb}ts{gch}",
               dma=True)

    for tb in range(NT):
        r = stage32[:, 2048:4096]
        op("sync", lambda e, tb=tb: e.dma_start(out=stage32[:, 0:2048],
                                                in_=xr3[tb]),
           writes=[STG[0]], tag=f"C{tb}x", dma=True)
        for ecb in range(4):
            par = ecb % 2
            for yc in range(NC):
                op("tensor", lambda e, tb=tb, ecb=ecb, yc=yc, par=par: e.matmul(
                    pp[par][:, :],
                    arena[:, tb * 2048 + yc * 128:tb * 2048 + (yc + 1) * 128],
                    wbuf[:, yc * 2048 + ecb * 512:yc * 2048 + (ecb + 1) * 512],
                    start=(yc == 0), stop=(yc == NC - 1)),
                   reads=[ACH[tb], WCH[yc]],
                   writes=[PP[par]] if yc == 0 else [],
                   appends=[PP[par]] if yc > 0 else [],
                   tag=f"C{tb}wo{ecb}_{yc}")
            op("vector", lambda e, ecb=ecb, par=par, r=r: e.tensor_tensor(
                r[:, ecb * 512:(ecb + 1) * 512], pp[par][:, :],
                stage32[:, ecb * 512:(ecb + 1) * 512], ALU.add),
               reads=[PP[par], STG[0]],
               writes=([STG[1], RLBB, RLR] if tb == 0 else [STG[1]])
               if ecb == 0 else [],
               appends=[STG[1]] if ecb > 0 else [], tag=f"C{tb}re{ecb}")
        ln_stats(r, [STG[1]], f"C{tb}", QhA[0][:, :], QHBA[0])
        op("vector", lambda e, r=r: e.tensor_scalar(
            r, r, stats[:, 2:3], stats[:, 6:7], ALU.subtract, ALU.mult),
           reads=[STG[1], STATS], writes=[STG[1]], tag=f"C{tb}x2")
        op("sync", lambda e, tb=tb, r=r: e.dma_start(out=x2s3[tb], in_=r),
           reads=[STG[1]], appends=[SCR["x2s"]], tag=f"C{tb}x2s", dma=True)
        op("vector", lambda e, r=r: e.tensor_scalar(
            r, r, scal[:, 2:3], MAGIC, ALU.mult, ALU.add),
           reads=[STG[1], SCAL], writes=[STG[1]], tag=f"C{tb}mv")
        xh_t = xh if tb % 2 == 0 else xh2
        XH_T = XHB if tb % 2 == 0 else XHB2
        for st_ in range(2):
            op("vector", lambda e, r=r, st_=st_: e.tensor_scalar(
                i32t[:, :], r[:, st_ * 1024:(st_ + 1) * 1024],
                MAGIC, None, ALU.subtract),
               reads=[STG[1]], writes=[I32H[0], I32H[1]], tag=f"C{tb}mi{st_}")
            op("vector", lambda e, xh_t=xh_t, st_=st_: e.tensor_scalar(
                xh_t[:, st_ * 1024:(st_ + 1) * 1024], i32t[:, :],
                7.0, -8.0, ALU.min, ALU.max),
               reads=[I32H[0], I32H[1]], writes=[XH_T] if st_ == 0 else [],
               appends=[XH_T] if st_ > 0 else [], tag=f"C{tb}mc{st_}")
        if tb >= 1:
            c_transposes(tb - 1)
    c_transposes(NT - 1)

    op("sync", lambda e: e.dma_start(
        out=stage32[0:64, 0:128],
        in_=ap["b1"].rearrange("a (c p) -> (a c) p", p=128)),
       writes=[STG[0]], tag="ld_b1", dma=True)
    op("tensor", lambda e: e.transpose(
        pp[5][:, 0:64], stage32[0:64, 0:128], idt32[0:64, 0:64]),
       reads=[STG[0], CONST], writes=[PP[5]], tag="tp_b1")
    op("vector", lambda e: e.tensor_copy(b1qt[:, :], pp[5][:, 0:64]),
       reads=[PP[5]], writes=[B1Q], tag="ev_b1")
    op("vector", lambda e: e.reciprocal(scal[:, 7:8], scal[:, 5:6]),
       reads=[SCAL], appends=[SCAL], tag="ra1")
    op("vector", lambda e: e.tensor_scalar(
        b1qt[:, :], b1qt[:, :], scal[:, 7:8], MAGIC, ALU.mult, ALU.add),
       reads=[B1Q, SCAL], writes=[B1Q], tag="b1q1")
    op("vector", lambda e: e.tensor_scalar(
        i32t[:, 0:64], b1qt[:, :], MAGIC, None, ALU.subtract),
       reads=[B1Q], writes=[I32H[0]], tag="b1q2")
    op("vector", lambda e: e.tensor_scalar(
        b1qt[:, :], i32t[:, 0:64], CLIP32, -CLIP32, ALU.min, ALU.max),
       reads=[I32H[0]], writes=[B1Q], tag="b1q3")
    op("vector", lambda e: e.tensor_scalar(
        b1qt[:, :], b1qt[:, :], scal[:, 5:6], None, ALU.mult),
       reads=[B1Q, SCAL], writes=[B1Q], tag="b1q4")

    w2m_state = {"i": 0}

    def w2m_step():
        i = w2m_state["i"]
        if i > NF:
            return
        w2m_state["i"] += 1
        if i < NF:
            op("gpsimd", lambda e, i=i: e.dma_start(out=stage32[:, 0:2048],
                                                    in_=w2r[i]),
               writes=[STG[0]], tag=f"w2l{i}", dma=True)
            op("vector", lambda e, i=i: e.tensor_reduce(
                wmx[:, i:i + 1], stage32[:, 0:2048], AX.X, ALU.max,
                apply_absolute_value=True),
               reads=[STG[0]], writes=[WMX] if i == 0 else [],
               appends=[WMX] if i > 0 else [], tag=f"w2m{i}")
            return
        op("vector", lambda e: e.tensor_reduce(
            wmx[:, 65:66], wmx[:, 0:64], AX.X, ALU.max),
           reads=[WMX], appends=[WMX], tag="w2fin")
        op("gpsimd", lambda e: e.tensor_reduce(
            sv[0:1, 8:9], wmx[:, 65:66], AX.C, ALU.max),
           reads=[WMX], appends=[SVB], tag="w2fing")
        bcast(sv[0:1, 8:9], 8, [SVB], "w2max")
        op("vector", lambda e: e.tensor_scalar(
            scal[:, 8:9], scal[:, 8:9], 1.0 / 7.0, None, ALU.mult),
           reads=[SCAL], appends=[SCAL], tag="sw2")
        op("vector", lambda e: e.tensor_tensor(
            scal[:, 9:10], scal[:, 1:2], scal[:, 8:9], ALU.mult),
           reads=[SCAL], appends=[SCAL], tag="b2s")
        op("vector", lambda e: e.reciprocal(scal[:, 10:11], scal[:, 8:9]),
           reads=[SCAL], appends=[SCAL], tag="rsw2")

    w2q_state = {"i": 0}

    def w2q_step():
        if w2q_state["i"] >= NF:
            return
        i = w2q_state["i"]
        w2q_state["i"] += 1
        op("gpsimd", lambda e, i=i: e.dma_start(out=stage32[:, 0:2048],
                                                in_=w2r[i]),
           writes=[STG[0]], tag=f"w2ql{i}", dma=True)
        op("vector", lambda e: e.tensor_scalar(
            stage32[:, 0:2048], stage32[:, 0:2048], scal[:, 10:11], MAGIC,
            ALU.mult, ALU.add),
           reads=[STG[0], SCAL], appends=[STG[0]], tag=f"w2q1_{i}")
        for st_ in range(2):
            op("vector", lambda e, st_=st_: e.tensor_scalar(
                i32t[:, :], stage32[:, st_ * 1024:(st_ + 1) * 1024],
                MAGIC, None, ALU.subtract),
               reads=[STG[0]], writes=[I32H[0], I32H[1]], tag=f"w2q2_{i}_{st_}")
            op("vector", lambda e, st_=st_: e.tensor_scalar(
                xh[:, st_ * 1024:(st_ + 1) * 1024], i32t[:, :],
                7.0, -8.0, ALU.min, ALU.max),
               reads=[I32H[0], I32H[1]], writes=[XHB] if st_ == 0 else [],
               appends=[XHB] if st_ > 0 else [], tag=f"w2q3_{i}_{st_}")
        op("gpsimd", lambda e, i=i: e.dma_start(
            out=w2i4[:, i, :, :].rearrange("b p q -> p b q"),
            in_=xh[:, :].rearrange("p (b q) -> p b q", q=512)),
           reads=[XHB], appends=[SCR["w2i"]], tag=f"w2qs_{i}", dma=True)

    for tsup in range(NS):
        for j in range(4):
            op("sync", lambda e, tsup=tsup, j=j: e.dma_start(
                out=wb_mt[:, 4 * j:4 * (j + 1), :],
                in_=mts4[tsup][4 * j:4 * (j + 1)].rearrange("c p q -> p c q")),
               reads=[SCR["mts"]], writes=[WCH[4 + j]],
               tag=f"D{tsup}mtl{j}", dma=True)
        def d_slab_load(fch, tsup=tsup):
            wslot = fch % 2
            for jj in range(2):
                op("sync", lambda e, fch=fch, wslot=wslot, jj=jj: e.dma_start(
                    out=wb_w1[wslot][:, 8 * jj:8 * (jj + 1), :],
                    in_=w1i4[fch][8 * jj:8 * (jj + 1)]
                    .rearrange("c p q -> p c q")),
                   reads=[SCR["w1i"]], writes=[W1S[wslot][jj]],
                   tag=f"D{tsup}_{fch}wl{jj}", dma=True)

        def d_nstore(fch, tsup=tsup):
            t1 = fch % 2
            op("gpsimd", lambda e, tsup=tsup, fch=fch, t1=t1: e.dma_start(
                out=ns4[tsup, fch], in_=mstg[:, t1 * 512:(t1 + 1) * 512]),
               reads=[MSTG[t1]], appends=[SCR["nscr"]], tag=f"D{tsup}_{fch}ns",
               dma=True)

        d_slab_load(0)
        for fch in range(NF):
            wslot = fch % 2
            woff = W1A if wslot == 0 else W1B
            if fch + 1 < NF:
                d_slab_load(fch + 1)
            par = fch % 2
            for dch in range(NC):
                op("tensor", lambda e, dch=dch, woff=woff, par=par: e.matmul(
                    pp[par][:, :],
                    wbuf[:, woff + dch * 128:woff + (dch + 1) * 128],
                    wbuf[:, MTSL + dch * 512:MTSL + (dch + 1) * 512],
                    start=(dch == 0), stop=(dch == NC - 1)),
                   reads=[W1S[wslot][dch // 8], WCH[4 + dch // 4]],
                   writes=[PP[par]] if dch == 0 else [],
                   appends=[PP[par]] if dch > 0 else [],
                   tag=f"D{tsup}_{fch}m{dch}")
            t1 = fch % 2
            t1sl = slice(2048 + t1 * 512, 2048 + (t1 + 1) * 512)
            wrt = [T1B[t1]] + ([STG[1]] if tsup == 0 and fch < 2 else [])
            op("scalar", lambda e, fch=fch, par=par, t1sl=t1sl: e.activation(
                stage32[:, t1sl], pp[par][:, :], AF.Relu,
                bias=b1qt[:, fch:fch + 1], scale=scal[:, 5:6]),
               reads=[PP[par], B1Q, SCAL], writes=wrt,
               tag=f"D{tsup}_{fch}relu")
            op("vector", lambda e, t1sl=t1sl: e.tensor_scalar(
                stage32[:, t1sl], stage32[:, t1sl], scal[:, 3:4], MAGIC,
                ALU.mult, ALU.add),
               reads=[T1B[t1], SCAL], appends=[T1B[t1]], tag=f"D{tsup}_{fch}q1")
            op("vector", lambda e, t1=t1, t1sl=t1sl: e.tensor_scalar(
                i32t[:, t1 * 512:(t1 + 1) * 512], stage32[:, t1sl], MAGIC,
                None, ALU.subtract),
               reads=[T1B[t1]], writes=[I32H[t1]], tag=f"D{tsup}_{fch}q2")
            op("vector", lambda e, t1=t1: e.tensor_scalar(
                mstg[:, t1 * 512:(t1 + 1) * 512],
                i32t[:, t1 * 512:(t1 + 1) * 512], 15.0, 0.0, ALU.min, ALU.max),
               reads=[I32H[t1]], writes=[MSTG[t1]], tag=f"D{tsup}_{fch}q3")
            if fch >= 1:
                d_nstore(fch - 1)
            if tsup < 2:
                w2m_step()
            elif fch % 2 == 1:
                w2q_step()
        d_nstore(NF - 1)

    op("sync", lambda e: e.dma_start(out=r2sup[0:1, 0:2048], in_=ap["b2"]),
       writes=[R2T[0]], tag="ld_b2", dma=True)
    op("vector", lambda e: e.tensor_scalar(
        r2sup[0:1, 0:2048], r2sup[0:1, 0:2048], scal[0:1, 10:11], MAGIC,
        ALU.mult, ALU.add),
       reads=[R2T[0], SCAL], writes=[R2T[0]], tag="b2q1")
    for st_ in range(2):
        op("vector", lambda e, st_=st_: e.tensor_scalar(
            i32t[0:1, :], r2sup[0:1, st_ * 1024:(st_ + 1) * 1024],
            MAGIC, None, ALU.subtract),
           reads=[R2T[0]], writes=[I32H[0], I32H[1]], tag=f"b2q2_{st_}")
        op("vector", lambda e, st_=st_: e.tensor_scalar(
            r2sup[0:1, st_ * 1024:(st_ + 1) * 1024], i32t[0:1, :],
            CLIP32, -CLIP32, ALU.min, ALU.max),
           reads=[I32H[0], I32H[1]], writes=[R2T[0]], tag=f"b2q3_{st_}")
    op("vector", lambda e: e.tensor_scalar(
        r2sup[0:1, 0:2048], r2sup[0:1, 0:2048], scal[0:1, 9:10], None,
        ALU.mult),
       reads=[R2T[0], SCAL], writes=[R2T[0]], tag="b2q4")
    for q4 in range(4):
        op("tensor", lambda e, q4=q4: e.matmul(
            pp[5][:, :], one32[:, :], r2sup[0:1, q4 * 512:(q4 + 1) * 512],
            start=True, stop=True),
           reads=[CONST, R2T[0]], writes=[PP[5]], tag=f"b2b{q4}")
        op("vector", lambda e, q4=q4: e.tensor_copy(
            b2qb[:, q4 * 512:(q4 + 1) * 512], pp[5][:, :]),
           reads=[PP[5]], writes=[B2QB] if q4 == 0 else [],
           appends=[B2QB] if q4 > 0 else [], tag=f"b2be{q4}")

    for tsup in range(NS):
        for c in range(NT):
            op("gpsimd", lambda e, tsup=tsup, c=c: e.dma_start(
                out=ar_n[:, 4 * c:4 * (c + 1), :],
                in_=ns4[tsup][4 * c:4 * (c + 1)].rearrange("f p q -> p f q")),
               reads=[SCR["nscr"]], writes=[ACH[c]], tag=f"E{tsup}nl{c}",
               dma=True)
        for ecb in range(4):
            for c in range(NC):
                extra = []
                if tsup == 0 and ecb == 0 and c < 2:
                    extra = W1S[c][:]
                op("sync", lambda e, ecb=ecb, c=c: e.dma_start(
                    out=wb_w2[:, 4 * c:4 * (c + 1), :],
                    in_=w2i4[ecb][4 * c:4 * (c + 1)]
                    .rearrange("f p q -> p f q")),
                   reads=[SCR["w2i"]], writes=[WCH[c]] + extra,
                   tag=f"E{tsup}_{ecb}wl{c}", dma=True)
            for tbi in range(4):
                par = tbi % 2 + (2 if ecb % 2 else 0)
                for fch in range(NF):
                    op("tensor",
                       lambda e, tbi=tbi, fch=fch, par=par: e.matmul(
                           pp[par][:, :],
                           arena[:, fch * 512 + tbi * 128:
                                 fch * 512 + tbi * 128 + 128],
                           wbuf[:, fch * 512:(fch + 1) * 512],
                           start=(fch == 0), stop=(fch == NF - 1)),
                       reads=[ACH[fch // 4], WCH[fch // 4]],
                       writes=[PP[par]] if fch == 0 else [],
                       appends=[PP[par]] if fch > 0 else [],
                       tag=f"E{tsup}_{ecb}_{tbi}m{fch}")
                op("vector", lambda e, tbi=tbi, ecb=ecb, par=par:
                   e.tensor_scalar(
                       r2sup[:, tbi * 2048 + ecb * 512:
                             tbi * 2048 + (ecb + 1) * 512],
                       pp[par][:, :], scal[:, 9:10], None, ALU.mult),
                   reads=[PP[par], SCAL],
                   writes=[R2T[tbi]] if ecb == 0 else [],
                   appends=[R2T[tbi]] if ecb > 0 else [],
                   tag=f"E{tsup}_{ecb}_{tbi}ev")
        for tbi in range(4):
            tb = tsup * 4 + tbi
            r2 = r2sup[:, tbi * 2048:(tbi + 1) * 2048]
            op("vector", lambda e, r2=r2: e.tensor_tensor(
                r2, r2, b2qb[:, :], ALU.add),
               reads=[R2T[tbi], B2QB], writes=[R2T[tbi]], tag=f"E{tb}badd")
            op("sync", lambda e, tb=tb: e.dma_start(
                out=stage32[:, 0:2048], in_=x2s3[tb]),
               reads=[SCR["x2s"]], writes=[STG[0]], tag=f"E{tb}x2l", dma=True)
            op("vector", lambda e, r2=r2: e.tensor_tensor(
                r2, r2, stage32[:, 0:2048], ALU.add),
               reads=[R2T[tbi], STG[0]], writes=[R2T[tbi]], tag=f"E{tb}radd")
            ln_stats(r2, [R2T[tbi]], f"E{tb}",
                     stage32[:, 2048:4096], [STG[1], T1B[0], T1B[1]])
            op("vector", lambda e, r2=r2: e.tensor_scalar(
                r2, r2, stats[:, 2:3], stats[:, 6:7], ALU.subtract, ALU.mult),
               reads=[R2T[tbi], STATS], writes=[R2T[tbi]], tag=f"E{tb}x3")
            op("sync", lambda e, tb=tb, r2=r2: e.dma_start(out=out3[tb],
                                                           in_=r2),
               reads=[R2T[tbi]], tag=f"E{tb}out", dma=True)

    t.finalize()

    with nc.Block() as block:
        @block.sync
        def _(eng):
            t.emit("sync", eng)
            for q in ("hw", "sw"):
                for i in range(NLANES):
                    lane = f"{q}{i}"
                    if t.totals[lane]:
                        eng.wait_ge(t.sems[lane], t.totals[lane])

        @block.gpsimd
        def _(eng):
            t.emit("gpsimd", eng)
            for i in range(NLANES):
                lane = f"sw{i}"
                if t.totals[lane]:
                    eng.wait_ge(t.sems[lane], t.totals[lane])

        @block.tensor
        def _(eng):
            t.emit("tensor", eng)

        @block.vector
        def _(eng):
            t.emit("vector", eng)

        @block.scalar
        def _(eng):
            t.emit("scalar", eng)

    return nc


_cached = {}


def kernel(**inputs):
    if "nc" not in _cached:
        _cached["nc"] = build()
    nc = _cached["nc"]
    f32 = np.float32
    x = np.ascontiguousarray(np.asarray(inputs["L0_attn_x"], dtype=f32))
    wkva = np.asarray(inputs["L0_attn_w_kva"], dtype=f32)
    kva_used = np.ascontiguousarray(np.concatenate(
        [wkva[:, 0:KR]] +
        [wkva[:, KR + h * 2 * D:KR + h * 2 * D + D] for h in range(H)], axis=1))
    base = {
        "gq": np.ascontiguousarray(np.asarray(inputs["L0_attn_g_q"], f32)[None, :]),
        "gkv": np.ascontiguousarray(np.asarray(inputs["L0_attn_g_kv"], f32)[None, :]),
        "wqa": np.ascontiguousarray(np.asarray(inputs["L0_attn_w_qa"], f32)),
        "wqb": np.ascontiguousarray(np.asarray(inputs["L0_attn_w_qb"], f32)),
        "wkva": kva_used,
        "wkvb": np.ascontiguousarray(np.asarray(inputs["L0_attn_w_kvb"], f32)),
        "wo": np.ascontiguousarray(np.asarray(inputs["L0_attn_w_o"], f32)),
        "w1t": np.ascontiguousarray(np.asarray(inputs["L0_ffn_W1"], f32).T),
        "b1": np.ascontiguousarray(np.asarray(inputs["L0_ffn_b1"], f32)[None, :]),
        "w2t": np.ascontiguousarray(np.asarray(inputs["L0_ffn_W2"], f32).T),
        "b2": np.ascontiguousarray(np.asarray(inputs["L0_ffn_b2"], f32)[None, :]),
        "s1": np.ascontiguousarray(np.asarray(inputs["L0_ffn_s_act1"], f32)[:, None]),
        "s2": np.ascontiguousarray(np.asarray(inputs["L0_ffn_s_act2"], f32)[:, None]),
    }
    in_maps = []
    for c in range(8):
        m = dict(base)
        m["x"] = np.ascontiguousarray(x[c])
        in_maps.append(m)
    res = run_bass_kernel_spmd(nc, in_maps, core_ids=list(range(8)))
    return np.stack([np.asarray(res.results[c]["out"]) for c in range(8)],
                    axis=0)


# version 16
# speedup vs baseline: 1.3590x; 1.0892x over previous
import numpy as np
import ml_dtypes
import concourse.bass as bass
import concourse.mybir as mybir
from concourse import dve_ops
from concourse.bass_utils import run_bass_kernel_spmd

F32 = mybir.dt.float32
BF16 = mybir.dt.bfloat16
I32 = mybir.dt.int32
AF = mybir.ActivationFunctionType
ALU = mybir.AluOpType
AX = mybir.AxisListType

MAGIC = 12582912.0
CLIP32 = 2147483520.0
S, E, H, D, QR, KR, RD, F = 2048, 2048, 16, 128, 64, 128, 64, 8192
EPS = 1e-5
ISD = 0.08838834764831845
NT, NS, NC, NF = S // 128, S // 512, E // 128, F // 128
VCOLS = H * D
WKVA_C = KR + VCOLS

ENGINES = ("sync", "gpsimd", "tensor", "vector", "scalar")
ENG_SEM = {"tensor": "pe", "vector": "dv", "scalar": "ac", "gpsimd": "gp",
           "sync": "sy"}
DMA_QUEUE = {"sync": "hw", "gpsimd": "sw"}
NLANES = 8


class Buf:
    __slots__ = ("name", "writers", "readers")

    def __init__(self, name):
        self.name = name
        self.writers = []
        self.readers = []


class Op:
    __slots__ = ("eng", "fn", "deps", "tag", "dma", "inc_sem", "inc_val",
                 "_clk", "_wait")

    def __init__(self, eng, fn, tag, dma):
        self.eng = eng
        self.fn = fn
        self.deps = []
        self.tag = tag
        self.dma = dma
        self.inc_sem = None
        self.inc_val = 0
        self._clk = {}
        self._wait = []


class Trk:
    def __init__(self, nc):
        self.nc = nc
        self.ops = {e: [] for e in ENGINES}
        names = ["pe", "dv", "ac", "gp", "sy"]
        for q in ("hw", "sw"):
            names += [f"{q}{i}" for i in range(NLANES)]
        self.sems = {s: nc.alloc_semaphore(s) for s in names}
        self.dma_count = {"hw": 0, "sw": 0}
        self.lane_last = {f"{q}{i}": None for q in ("hw", "sw")
                          for i in range(NLANES)}

    def op(self, eng, fn, reads=(), writes=(), appends=(), tag="", dma=False):
        o = Op(eng, fn, tag, dma)
        deps = set()
        for b in reads:
            deps.update(b.writers)
        for b in writes:
            deps.update(b.writers)
            deps.update(b.readers)
        if dma:
            q = DMA_QUEUE[eng]
            lane = f"{q}{self.dma_count[q] % NLANES}"
            self.dma_count[q] += 1
            o.inc_sem = lane
            prev = self.lane_last[lane]
            if prev is not None:
                deps.add(prev)
            self.lane_last[lane] = o
        o.deps = list(deps)
        self.ops[eng].append(o)
        for b in reads:
            b.readers.append(o)
        for b in writes:
            b.writers = [o]
            b.readers = []
        for b in appends:
            b.writers.append(o)
        return o

    def finalize(self):
        needs = set()
        for e in ENGINES:
            for o in self.ops[e]:
                needs.update(o.deps)
        cnt = {s: 0 for s in self.sems}
        for e in ENGINES:
            for o in self.ops[e]:
                if o.dma:
                    cnt[o.inc_sem] += 16
                    o.inc_val = cnt[o.inc_sem]
                elif o in needs:
                    s = ENG_SEM[e]
                    cnt[s] += 1
                    o.inc_sem, o.inc_val = s, cnt[s]
        idx = {e: 0 for e in ENGINES}
        clock = {e: {} for e in ENGINES}
        done = set()
        n_total = sum(len(self.ops[e]) for e in ENGINES)
        n_done = 0
        progress = True
        while progress:
            progress = False
            for e in ENGINES:
                while idx[e] < len(self.ops[e]):
                    o = self.ops[e][idx[e]]
                    if not all(d in done for d in o.deps):
                        break
                    ck = clock[e]
                    w = []
                    for dd in sorted(o.deps,
                                     key=lambda d: (d.inc_sem, d.inc_val)):
                        s, v = dd.inc_sem, dd.inc_val
                        if ck.get(s, 0) < v:
                            w.append((s, v))
                            ck[s] = v
                            for s2, v2 in dd._clk.items():
                                if ck.get(s2, 0) < v2:
                                    ck[s2] = v2
                    o._wait = w
                    o._clk = dict(ck)
                    if o.inc_sem:
                        o._clk[o.inc_sem] = max(o._clk.get(o.inc_sem, 0),
                                                o.inc_val)
                    done.add(o)
                    idx[e] += 1
                    n_done += 1
                    progress = True
        if n_done != n_total:
            stuck = {e: self.ops[e][idx[e]].tag for e in ENGINES
                     if idx[e] < len(self.ops[e])}
            raise RuntimeError(f"build-order deadlock: {stuck}")
        self.totals = dict(cnt)

    def emit(self, engine, eng):
        for o in self.ops[engine]:
            for s, v in o._wait[1:]:
                eng.wait_ge(self.sems[s], v)
            ins = o.fn(eng)
            if o._wait:
                s, v = o._wait[0]
                ins._wait_ge(self.sems[s], v)
            if o.inc_sem:
                ins.then_inc(self.sems[o.inc_sem], 16 if o.dma else 1)


def _consts():
    c = {}
    c["ident"] = np.eye(128, dtype=ml_dtypes.bfloat16)
    c["ident32"] = np.eye(128, dtype=np.float32)
    Smat = np.zeros((RD, RD), dtype=np.float32)
    half = RD // 2
    for i in range(half):
        Smat[i, i + half] = -1.0
        Smat[i + half, i] = 1.0
    c["stmat"] = np.ascontiguousarray(Smat.T.astype(ml_dtypes.bfloat16))
    c["trimask"] = np.ascontiguousarray(
        np.triu(np.ones((128, 128), np.float32)).astype(ml_dtypes.bfloat16))
    am = np.triu(np.full((128, 128), -1e30, np.float32), 1)
    c["addmask"] = np.ascontiguousarray(am.astype(ml_dtypes.bfloat16))
    c["ones_col"] = np.ones((128, 1), dtype=ml_dtypes.bfloat16)
    c["ones_row"] = np.ones((1, 128), dtype=ml_dtypes.bfloat16)
    c["ones32"] = np.ones((1, 128), dtype=np.float32)
    inv_freq = 1.0 / (10000.0 ** (np.arange(0, RD, 2, dtype=np.float64) / RD))
    freqs = np.arange(S, dtype=np.float64)[:, None] * inv_freq[None, :]
    cs, sn = np.cos(freqs).T, np.sin(freqs).T
    c["costab"] = np.ascontiguousarray(
        np.concatenate([cs, cs], 0).astype(ml_dtypes.bfloat16))
    c["sintab"] = np.ascontiguousarray(
        np.concatenate([sn, sn], 0).astype(ml_dtypes.bfloat16))
    return c


def build():
    nc = bass.Bass("TRN2", target_bir_lowering=False)
    t = Trk(nc)

    ap = {}
    for name, shape in (("x", [S, E]), ("gq", [1, E]), ("gkv", [1, E]),
                        ("wqa", [E, QR]), ("wqb", [QR, E]),
                        ("wkva", [E, WKVA_C]), ("wkvb", [KR, E]),
                        ("wo", [E, E]), ("w1t", [E, F]), ("b1", [1, F]),
                        ("w2t", [F, E]), ("b2", [1, E]), ("s1", [1, 1]),
                        ("s2", [1, 1])):
        ap[name] = nc.dram_tensor(name, shape, F32, kind="ExternalInput").ap()
    out = nc.dram_tensor("out", [S, E], F32, kind="ExternalOutput").ap()
    for k, v in _consts().items():
        ap[k] = nc.inline_tensor(v, name=f"c_{k}").ap()

    x2s = nc.dram_tensor("x2s", [S, E], F32, kind="Internal").ap()
    mts = nc.dram_tensor("mts", [NS * 128, NC * 512], BF16, kind="Internal").ap()
    w1i = nc.dram_tensor("w1i", [NF * 128, NC * 128], BF16, kind="Internal").ap()
    w2i = nc.dram_tensor("w2i", [4 * NF * 128, 512], BF16, kind="Internal").ap()
    nscr = nc.dram_tensor("nscr", [NS * NF * 128, 512], BF16, kind="Internal").ap()

    AL = nc.alloc_sbuf_tensor
    arena = AL("arena", [128, NT * 2048], BF16)
    wbuf = AL("wbuf", [128, 32768], BF16)
    stage32 = AL("stage32", [128, 4096], F32)
    xh = AL("xh", [128, 2048], BF16)
    i32t = AL("i32t", [128, 1024], I32)
    xh2 = AL("xh2", [128, 2048], BF16)
    mstg = AL("mstg", [128, 1024], BF16)
    scal = AL("scal", [128, 16], F32)
    stats = AL("stats", [128, 32], F32)
    sv = AL("svals", [1, 16], F32)
    wmx = AL("wmx", [128, 66], F32)
    idt = AL("idt", [128, 128], BF16)
    idt32 = AL("idt32", [128, 128], F32)
    stm = AL("stm", [64, 64], BF16)
    trim = AL("trim", [128, 128], BF16)
    amsk = AL("amsk", [128, 128], BF16)
    onec = AL("onec", [128, 1], BF16)
    oner = AL("oner", [1, 128], BF16)
    one32 = AL("one32", [1, 128], F32)
    guards = []

    def tmp_alloc(name, shape, dt):
        g = nc.sbuf_tensor(name, shape, dt)
        guards.append(g)
        return g.__enter__()
    kct = tmp_alloc("kct", [128, 2048], BF16)
    tT = tmp_alloc("tTq", [64, 2048], BF16)
    wqb_b = tmp_alloc("wqb_b", [64, 2048], BF16)
    wkvb_b = tmp_alloc("wkvb_b", [128, 2048], BF16)
    ga = nc.sbuf_tensor("wqa_b", [128, NC * QR], BF16)
    wqa_b = ga.__enter__()
    gb = nc.sbuf_tensor("gq_pc", [128, NC], F32)
    gq_pc = gb.__enter__()
    gc = nc.sbuf_tensor("gkv_pc", [128, NC], F32)
    gkv_pc = gc.__enter__()
    gc.__exit__(None, None, None)
    gb.__exit__(None, None, None)
    ga.__exit__(None, None, None)
    QhA = [tmp_alloc("Qh0", [128, 2048], BF16), tmp_alloc("Qh1", [128, 2048], BF16)]
    KhA = [tmp_alloc("Kh0", [128, 2048], BF16), tmp_alloc("Kh1", [128, 2048], BF16)]
    Pr = tmp_alloc("Pr", [128, 4096], BF16)
    PTt = tmp_alloc("PTt", [128, 1024], BF16)
    dgt = tmp_alloc("dgt", [128, 512], BF16)
    rtab = tmp_alloc("rtab", [128, S], BF16)
    mneg = tmp_alloc("mneg", [128, 32], F32)
    for g in reversed(guards):
        g.__exit__(None, None, None)
    rlb = stage32[:, 3584:4096]
    r2sup = AL("r2sup", [128, 8192], F32)
    b2qb = AL("b2qb", [128, 2048], BF16)
    b1qt = AL("b1qt", [128, 64], F32)

    WV0, WV1, KCW = 8192, 16384, 24576
    W1A, W1B, MTSL = 0, 2048, 8192

    pp = [nc.alloc_psum_tensor(f"pp{i}", [128, 512], F32) for i in range(6)]
    ppbt = [nc.alloc_psum_tensor("ppb0", [128, 512], BF16)]
    ppf = nc.alloc_psum_tensor("ppf", [128, 512], F32)
    p1 = pp[5][0:1, 0:512]

    PP = [Buf(f"pp{i}") for i in range(6)]
    PPB = [Buf("ppb0")]
    PPF = Buf("ppf")
    P1B = Buf("p1")
    CONST = Buf("const")
    ARENA = Buf("arena")
    STG = [Buf("stg0"), Buf("stg1")]
    T1B = [Buf("t1a"), Buf("t1b")]
    XHB = Buf("xh")
    XHB2 = Buf("xh2")
    I32H = [Buf("i32a"), Buf("i32b")]
    MSTG = [Buf("mstg0"), Buf("mstg1")]
    STATS = Buf("stats")
    SCAL = Buf("scal")
    SVB = Buf("sv")
    WMX = Buf("wmx")
    KCT = Buf("kct")
    TTB = Buf("tT")
    WQA = Buf("wqa_b")
    WQB = Buf("wqb_b")
    WKVB = Buf("wkvb_b")
    GQ = Buf("gq")
    GKV = Buf("gkv")
    QHBA = [[Buf(f"Qh{i}_{j}") for j in range(4)] for i in range(2)]
    KHBA = [[Buf(f"Kh{i}_{j}") for j in range(4)] for i in range(2)]
    PRB = [Buf(f"Pr{i}") for i in range(8)]
    PTRB = [Buf("PT0"), Buf("PT1")]
    DGB = [Buf(f"dg{i}") for i in range(4)]
    MNEG = [Buf(f"mneg{i}") for i in range(16)]
    MNEGP = [[Buf(f"mnegp{s_}_{i}") for i in range(4)] for s_ in range(4)]
    LPB = [Buf(f"lp{i}") for i in range(4)]
    LB_ = [Buf(f"lb{i}") for i in range(4)]
    RLQ = [Buf(f"rlq{i}") for i in range(4)]
    RLR = Buf("rlrow")
    RLBB = Buf("rlb")
    XB = [[Buf(f"xb{g}_{b}") for b in range(4)] for g in range(4)]
    WCH = [Buf(f"wch{i}") for i in range(16)]
    ACH = [Buf(f"ach{i}") for i in range(NT)]
    KCWS = [Buf(f"kcws{i}") for i in range(4)]
    W1S = [[Buf(f"w1s{s_}_{j}") for j in range(2)] for s_ in range(8)]
    W2ST = [Buf(f"w2st{i}") for i in range(4)]
    B1Q = Buf("b1q")
    B2QB = Buf("b2qb")
    R2T = [Buf(f"r2_{i}") for i in range(4)]
    SCR = {n: Buf("scr_" + n) for n in ("x2s", "mts", "w1i", "w2i", "nscr")}

    op = t.op

    xr3 = ap["x"].rearrange("(t p) e -> t p e", p=128)
    wkva3 = ap["wkva"].rearrange("(c p) q -> c p q", p=128)
    wo3 = ap["wo"].rearrange("(c p) e -> c p e", p=128)
    w1r = ap["w1t"].rearrange("(c p) f -> c p f", p=128)
    w2r = ap["w2t"].rearrange("(c p) d -> c p d", p=128)
    w1i4 = w1i.rearrange("(f p) (c q) -> f p c q", p=128, q=128)
    w2i4 = w2i.rearrange("(b f p) q -> b f p q", f=NF, p=128)
    mts4 = mts.rearrange("(s p) (c q) -> s p c q", p=128, q=512)
    mts2 = mts.rearrange("(s p) w -> s p w", p=128)
    ns4 = nscr.rearrange("(s f p) q -> s f p q", f=NF, p=128)
    x2s3 = x2s.rearrange("(t p) e -> t p e", p=128)
    out3 = out.rearrange("(t p) e -> t p e", p=128)
    wb_xb = wbuf[:, 0:8192].rearrange("p (k b q) -> p k b q", b=4, q=128)
    wb_wv = [wbuf[:, WV0:WV0 + 8192].rearrange("p (k q) -> p k q", q=512),
             wbuf[:, WV1:WV1 + 8192].rearrange("p (k q) -> p k q", q=512)]
    wb_kcw = wbuf[:, KCW:KCW + 2048].rearrange("p (k q) -> p k q", q=128)
    wb_wo = wbuf[:, :].rearrange("p (c e) -> p c e", e=2048)
    wb_w2 = wbuf[:, :].rearrange("p (f q) -> p f q", q=512)
    ar_n = arena[:, 0:NF * 512].rearrange("p (f q) -> p f q", q=512)
    ar_y = arena[:, :].rearrange("p (c h q) -> p c h q", h=H, q=128)

    for src, dst in (("ident", idt), ("ident32", idt32), ("addmask", amsk),
                     ("stmat", stm), ("trimask", trim),
                     ("ones_col", onec), ("ones_row", oner), ("ones32", one32),
                     ("costab", rtab[0:64, :]),
                     ("sintab", rtab[64:128, :])):
        op("sync", lambda e, s=src, d=dst: e.dma_start(out=d[:, :], in_=ap[s]),
           appends=[CONST], tag=f"ld_{src}", dma=True)
    for nm, coff in (("gq", 1024), ("gkv", 1152)):
        op("sync", lambda e, nm=nm, coff=coff: e.dma_start(
            out=stage32[0:16, coff:coff + 128],
            in_=ap[nm].rearrange("a (c p) -> (a c) p", p=128)),
           appends=[STG[0]], tag=f"ld_{nm}", dma=True)
    op("tensor", lambda e: e.transpose(
        pp[5][:, 0:16], stage32[0:16, 1024:1152], idt32[0:16, 0:16]),
       reads=[STG[0], CONST], writes=[PP[5]], tag="tp_gq")
    op("vector", lambda e: e.tensor_copy(gq_pc[:, :], pp[5][:, 0:16]),
       reads=[PP[5]], writes=[GQ], tag="ev_gq")
    op("tensor", lambda e: e.transpose(
        pp[5][:, 0:16], stage32[0:16, 1152:1280], idt32[0:16, 0:16]),
       reads=[STG[0], CONST], writes=[PP[5]], tag="tp_gkv")
    op("vector", lambda e: e.tensor_copy(gkv_pc[:, :], pp[5][:, 0:16]),
       reads=[PP[5]], writes=[GKV], tag="ev_gkv")
    op("sync", lambda e: e.dma_start(out=sv[0:1, 0:1], in_=ap["s1"]),
       appends=[SVB], tag="ld_s1", dma=True)
    op("sync", lambda e: e.dma_start(out=sv[0:1, 1:2], in_=ap["s2"]),
       appends=[SVB], tag="ld_s2", dma=True)
    op("sync", lambda e: e.dma_start(
        out=stage32[:, 0:NC * QR].rearrange("p (c q) -> p c q", q=QR),
        in_=ap["wqa"].rearrange("(c p) q -> c p q", p=128).rearrange(
            "c p q -> p c q")),
       writes=[STG[0]], tag="ld_wqa", dma=True)
    op("sync", lambda e: e.dma_start(out=stage32[0:64, 2048:2048 + E],
                                     in_=ap["wqb"]),
       writes=[STG[1]], tag="ld_wqb", dma=True)
    op("gpsimd", lambda e: e.dma_start(out=wkvb_b[:, :], in_=ap["wkvb"]),
       writes=[WKVB], tag="ld_wkvb", dma=True)
    for j in range(4):
        op("gpsimd", lambda e, j=j: e.dma_start(
            out=wb_kcw[:, 4 * j:4 * (j + 1), :],
            in_=wkva3[4 * j:4 * (j + 1), :, 0:KR].rearrange("c p q -> p c q")),
           writes=[KCWS[j]], tag=f"ld_kcw{j}", dma=True)

    for c in range(NC):
        op("vector", lambda e, c=c: e.tensor_scalar(
            wqa_b[:, c * QR:(c + 1) * QR], stage32[:, c * QR:(c + 1) * QR],
            gq_pc[:, c:c + 1], None, ALU.mult),
           reads=[STG[0], GQ], appends=[WQA], tag=f"f_wqa{c}")
    op("vector", lambda e: e.tensor_scalar(
        wqb_b[:, :], stage32[0:64, 2048:2048 + E], ISD, None, ALU.mult),
       reads=[STG[1]], writes=[WQB], tag="f_wqb")

    def bcast(src_slice, col, reads, tag, bank=5):
        op("tensor", lambda e, s=src_slice, c=col, bank=bank: e.matmul(
            pp[bank][:, c:c + 1], one32[:, :], s, start=True, stop=True),
           reads=[CONST] + reads, writes=[PP[bank]], tag=f"bcm_{tag}")
        op("vector", lambda e, c=col, bank=bank: e.tensor_copy(
            scal[:, c:c + 1], pp[bank][:, c:c + 1]),
           reads=[PP[bank]], appends=[SCAL], tag=f"bcc_{tag}")

    bcast(sv[0:1, 0:1], 0, [SVB], "s1")
    bcast(sv[0:1, 1:2], 1, [SVB], "s2")
    op("vector", lambda e: e.reciprocal(scal[:, 2:3], scal[:, 0:1]),
       reads=[SCAL], appends=[SCAL], tag="rs1")
    op("vector", lambda e: e.reciprocal(scal[:, 3:4], scal[:, 1:2]),
       reads=[SCAL], appends=[SCAL], tag="rs2")

    def ln_stats(rbuf_ap, rtoks, tag, dump_ap=None, dump_bufs=None):
        op("vector", lambda e: e.tensor_scalar(
            dump_ap, rbuf_ap, 1.0, None, ALU.mult, ALU.add,
            accum_out=stats[:, 0:1]),
           reads=rtoks, writes=[STATS] + dump_bufs, tag=f"{tag}_sum")
        op("vector", lambda e: e.scalar_tensor_tensor(
            dump_ap, rbuf_ap, 1.0, rbuf_ap, ALU.mult, ALU.mult,
            accum_out=stats[:, 1:2]),
           reads=rtoks, writes=dump_bufs, appends=[STATS], tag=f"{tag}_ssq")
        op("vector", lambda e: e.tensor_scalar(stats[:, 2:3], stats[:, 0:1],
                                               1.0 / E, None, ALU.mult),
           reads=[STATS], appends=[STATS], tag=f"{tag}_mu")
        op("vector", lambda e: e.tensor_scalar(stats[:, 3:4], stats[:, 2:3],
                                               stats[:, 2:3], EPS,
                                               ALU.mult, ALU.subtract),
           reads=[STATS], appends=[STATS], tag=f"{tag}_mu2")
        op("vector", lambda e: e.tensor_scalar(stats[:, 4:5], stats[:, 1:2],
                                               1.0 / E, stats[:, 3:4],
                                               ALU.mult, ALU.subtract),
           reads=[STATS], appends=[STATS], tag=f"{tag}_var")
        op("scalar", lambda e: e.activation(stats[:, 5:6], stats[:, 4:5],
                                            AF.Sqrt),
           reads=[STATS], appends=[STATS], tag=f"{tag}_sig")
        op("vector", lambda e: e.reciprocal(stats[:, 6:7], stats[:, 5:6]),
           reads=[STATS], appends=[STATS], tag=f"{tag}_rsig")

    op("sync", lambda e: e.dma_start(out=stage32[:, 0:2048], in_=xr3[0]),
       writes=[STG[0]], tag="ldx0", dma=True)

    for ts in range(NS):
        for tbi in range(4):
            tb = ts * 4 + tbi
            par = tb % 2
            xt = stage32[:, par * 2048:(par + 1) * 2048]
            ln_stats(xt, [STG[par]], f"A{tb}", xh2[:, :], [XHB2])
            op("vector", lambda e, xt=xt: e.tensor_scalar(
                xh[:, :], xt, stats[:, 2:3], stats[:, 6:7],
                ALU.subtract, ALU.mult),
               reads=[STG[par], STATS], writes=[XHB], tag=f"A{tb}_xh")
            if tb + 1 < NT:
                npar = (tb + 1) % 2
                op("sync", lambda e, npar=npar, tb=tb: e.dma_start(
                    out=stage32[:, npar * 2048:(npar + 1) * 2048],
                    in_=xr3[tb + 1]),
                   writes=[STG[npar]], tag=f"ldx{tb + 1}", dma=True)
            for gch in range(4):
                for j in range(4):
                    k = gch * 4 + j
                    op("tensor", lambda e, k=k, j=j: e.transpose(
                        ppbt[0][:, j * 128:(j + 1) * 128],
                        xh[:, k * 128:(k + 1) * 128], idt[:, :]),
                       reads=[XHB, CONST],
                       writes=[PPB[0]] if j == 0 else [],
                       appends=[PPB[0]] if j > 0 else [],
                       tag=f"A{tb}_tp{k}")
                op("scalar", lambda e, gch=gch, tbi=tbi: e.copy(
                    wb_xb[:, gch * 4:(gch + 1) * 4, tbi, :],
                    ppbt[0][:, :]),
                   reads=[PPB[0]], writes=[XB[gch][tbi]],
                   tag=f"A{tb}_xe{gch}")
        xb_all = [XB[g][b] for g in range(4) for b in range(4)]
        for k in range(NC):
            op("tensor", lambda e, k=k: e.matmul(
                pp[4][:, :], wbuf[:, KCW + k * KR:KCW + (k + 1) * KR],
                wbuf[:, k * 512:(k + 1) * 512],
                start=(k == 0), stop=(k == NC - 1)),
               reads=[KCWS[k // 4]] + XB[k // 4],
               writes=[PP[4]] if k == 0 else [],
               appends=[PP[4]] if k > 0 else [], tag=f"A{ts}_kc{k}")
        op("scalar", lambda e, ts=ts: e.copy(kct[:, ts * 512:(ts + 1) * 512],
                                             pp[4][:, :]),
           reads=[PP[4]], appends=[KCT], tag=f"A{ts}_kce")
        for k in range(NC):
            op("tensor", lambda e, k=k: e.matmul(
                pp[5][0:64, :], wqa_b[:, k * QR:(k + 1) * QR],
                wbuf[:, k * 512:(k + 1) * 512],
                start=(k == 0), stop=(k == NC - 1)),
               reads=[WQA] + XB[k // 4],
               writes=[PP[5]] if k == 0 else [],
               appends=[PP[5]] if k > 0 else [], tag=f"A{ts}_tt{k}")
        op("scalar", lambda e, ts=ts: e.copy(tT[:, ts * 512:(ts + 1) * 512],
                                             pp[5][0:64, :]),
           reads=[PP[5]], appends=[TTB], tag=f"A{ts}_tte")
        for vcb in range(4):
            wv = vcb % 2
            wv_off = WV0 if wv == 0 else WV1
            wvch = 4 + 4 * wv
            for j in range(4):
                op("gpsimd", lambda e, vcb=vcb, wv=wv, j=j: e.dma_start(
                    out=wb_wv[wv][:, 4 * j:4 * (j + 1), :],
                    in_=wkva3[4 * j:4 * (j + 1), :,
                              KR + vcb * 512:KR + (vcb + 1) * 512]
                    .rearrange("c p q -> p c q")),
                   writes=[WCH[wvch + j]], tag=f"A{ts}_wv{vcb}_{j}", dma=True)
            for k in range(NC):
                op("vector", lambda e, k=k, wv_off=wv_off: e.tensor_scalar(
                    wbuf[:, wv_off + k * 512:wv_off + (k + 1) * 512],
                    wbuf[:, wv_off + k * 512:wv_off + (k + 1) * 512],
                    gkv_pc[:, k:k + 1], None, ALU.mult),
                   reads=[WCH[wvch + k // 4], GKV],
                   appends=[WCH[wvch + k // 4]],
                   tag=f"A{ts}_gf{vcb}_{k}")
            for k in range(NC):
                for tbi in range(4):
                    op("tensor",
                       lambda e, k=k, tbi=tbi, wv_off=wv_off: e.matmul(
                           pp[tbi][:, :],
                           wbuf[:, k * 512 + tbi * 128:k * 512 + tbi * 128 + 128],
                           wbuf[:, wv_off + k * 512:wv_off + (k + 1) * 512],
                           start=(k == 0), stop=(k == NC - 1)),
                       reads=[XB[k // 4][tbi], WCH[wvch + k // 4]],
                       writes=[PP[tbi]] if k == 0 else [],
                       appends=[PP[tbi]] if k > 0 else [],
                       tag=f"A{ts}_v{vcb}_{k}_{tbi}")
            for tbi in range(4):
                tb = ts * 4 + tbi
                op("scalar", lambda e, tb=tb, vcb=vcb, tbi=tbi: e.copy(
                    arena[:, tb * 2048 + vcb * 512:tb * 2048 + (vcb + 1) * 512],
                    pp[tbi][:, :]),
                   reads=[PP[tbi]], appends=[ARENA, ACH[tb]],
                   tag=f"A{tb}_ve{vcb}")

    for c in range(NC):
        extra = []
        if c < 4:
            extra = XB[c][:]
        if c == 12:
            extra = KCWS[:]
        if c < 8:
            extra = extra + W1S[c][:]
        op("gpsimd", lambda e, c=c: e.dma_start(
            out=wbuf[:, c * 2048:(c + 1) * 2048], in_=wo3[c]),
           writes=[WCH[c]] + extra, tag=f"ld_wo{c}", dma=True)

    w1_jobs = [("mdma", 0), ("mdma", 1)]
    for i in range(62):
        w1_jobs += [("mred", i), ("mdma", i + 2)]
    w1_jobs += [("mred", 62), ("mred", 63), ("fin", 0)]
    w1_jobs += [("qdma", 0), ("qdma", 1), ("qa", 0)]
    for i in range(63):
        w1_jobs.append(("qb", i))
        if i + 2 < 64:
            w1_jobs.append(("qdma", i + 2))
        w1_jobs.append(("qa", i + 1))
    w1_jobs.append(("qb", 63))
    w1_state = {"i": 0}

    def w1_step():
        if w1_state["i"] >= len(w1_jobs):
            return
        kind, i = w1_jobs[w1_state["i"]]
        w1_state["i"] += 1
        par = i % 2
        sl = slice(par * 2048, par * 2048 + 2048)
        if kind in ("mdma", "qdma"):
            op("sync", lambda e, i=i, sl=sl: e.dma_start(
                out=stage32[:, sl],
                in_=w1r[i // 4, :, (i % 4) * 2048:(i % 4 + 1) * 2048]),
               writes=[STG[par]], tag=f"w1l_{kind}{i}", dma=True)
        elif kind == "mred":
            op("vector", lambda e, i=i, sl=sl: e.tensor_reduce(
                wmx[:, i:i + 1], stage32[:, sl], AX.X, ALU.max,
                apply_absolute_value=True),
               reads=[STG[par]], appends=[WMX], tag=f"w1m{i}")
        elif kind == "fin":
            op("vector", lambda e: e.tensor_reduce(
                wmx[:, 64:65], wmx[:, 0:64], AX.X, ALU.max),
               reads=[WMX], appends=[WMX], tag="w1fin")
            op("gpsimd", lambda e: e.tensor_reduce(
                sv[0:1, 4:5], wmx[:, 64:65], AX.C, ALU.max),
               reads=[WMX], appends=[SVB], tag="w1fing")
            bcast(sv[0:1, 4:5], 4, [SVB], "w1max", bank=2)
            op("vector", lambda e: e.tensor_scalar(
                scal[:, 4:5], scal[:, 4:5], 1.0 / 7.0, None, ALU.mult),
               reads=[SCAL], appends=[SCAL], tag="sw1")
            op("vector", lambda e: e.tensor_tensor(
                scal[:, 5:6], scal[:, 0:1], scal[:, 4:5], ALU.mult),
               reads=[SCAL], appends=[SCAL], tag="a1")
            op("vector", lambda e: e.reciprocal(scal[:, 6:7], scal[:, 4:5]),
               reads=[SCAL], appends=[SCAL], tag="rsw1")
        elif kind == "qa":
            op("gpsimd", lambda e, sl=sl: e.tensor_scalar(
                stage32[:, sl], stage32[:, sl], scal[:, 6:7], MAGIC,
                ALU.mult, ALU.add),
               reads=[STG[par], SCAL], appends=[STG[par]], tag=f"w1q1_{i}")
        else:
            dch, q4 = i // 4, i % 4
            for st_ in range(2):
                op("vector", lambda e, sl=sl, st_=st_: e.tensor_scalar(
                    i32t[:, :],
                    stage32[:, sl.start + st_ * 1024:
                            sl.start + (st_ + 1) * 1024],
                    MAGIC, None, ALU.subtract),
                   reads=[STG[par]], writes=[I32H[0], I32H[1]],
                   tag=f"w1q2_{i}_{st_}")
                op("gpsimd", lambda e, st_=st_: e.tensor_scalar(
                    xh[:, st_ * 1024:(st_ + 1) * 1024], i32t[:, :],
                    7.0, -8.0, ALU.min, ALU.max),
                   reads=[I32H[0], I32H[1]], writes=[XHB] if st_ == 0 else [],
                   appends=[XHB] if st_ > 0 else [], tag=f"w1q3_{i}_{st_}")
            for jj in range(2):
                op("gpsimd", lambda e, dch=dch, q4=q4, jj=jj: e.dma_start(
                    out=w1i4[q4 * 16 + 8 * jj:q4 * 16 + 8 * (jj + 1), :, dch, :]
                    .rearrange("c p q -> p c q"),
                    in_=xh[:, 1024 * jj:1024 * (jj + 1)]
                    .rearrange("p (c q) -> p c q", q=128)),
                   reads=[XHB], appends=[SCR["w1i"]], tag=f"w1qs_{i}_{jj}",
                   dma=True)

    def qk_proj(h):
        Qh, Kh = QhA[h % 2], KhA[h % 2]
        QHB, KHB = QHBA[h % 2], KHBA[h % 2]
        for q4 in range(4):
            op("tensor", lambda e, h=h, q4=q4: e.matmul(
                pp[q4][:, :], wqb_b[:, h * 128:(h + 1) * 128],
                tT[:, q4 * 512:(q4 + 1) * 512], start=True, stop=True),
               reads=[WQB, TTB], writes=[PP[q4]], tag=f"B{h}q{q4}")
        for q4 in range(4):
            op("scalar", lambda e, q4=q4, Qh=Qh: e.copy(
                Qh[:, q4 * 512:(q4 + 1) * 512], pp[q4][:, :]),
               reads=[PP[q4]], writes=[QHB[q4]], tag=f"B{h}qe{q4}")
        for q4 in range(4):
            op("tensor", lambda e, h=h, q4=q4: e.matmul(
                pp[q4][:, :], wkvb_b[:, h * 128:(h + 1) * 128],
                kct[:, q4 * 512:(q4 + 1) * 512], start=True, stop=True),
               reads=[WKVB, KCT], writes=[PP[q4]], tag=f"B{h}k{q4}")
        for q4 in range(4):
            op("scalar", lambda e, q4=q4, Kh=Kh: e.copy(
                Kh[:, q4 * 512:(q4 + 1) * 512], pp[q4][:, :]),
               reads=[PP[q4]], writes=[KHB[q4]], tag=f"B{h}ke{q4}")
    def qk_rope(h):
        Qh, Kh = QhA[h % 2], KhA[h % 2]
        QHB, KHB = QHBA[h % 2], KHBA[h % 2]
        for tname, tbuf, TB in (("q", Qh, QHB), ("k", Kh, KHB)):
            for sp in range(4):
                sl = slice(sp * 512, (sp + 1) * 512)
                bk = sp % 4
                pslot = sp % 4
                op("tensor", lambda e, tbuf=tbuf, sl=sl, bk=bk: e.matmul(
                    pp[bk][0:64, :], stm[:, :], tbuf[0:64, sl],
                    start=True, stop=True),
                   reads=[TB[sp], CONST], writes=[PP[bk]],
                   tag=f"B{h}rr{tname}{sp}")
                op("vector", lambda e, sl=sl, bk=bk, pslot=pslot: e.tensor_tensor(
                    mstg[0:64, (pslot % 2) * 512:(pslot % 2 + 1) * 512],
                    pp[bk][0:64, :], rtab[64:64 + 64, sl], ALU.mult),
                   reads=[PP[bk], CONST], writes=[MSTG[pslot % 2]],
                   tag=f"B{h}r1{tname}{sp}")
                op("vector", lambda e, tbuf=tbuf, sl=sl, pslot=pslot:
                   e.scalar_tensor_tensor(
                       xh2[0:64, (pslot % 4) * 512:(pslot % 4 + 1) * 512],
                       tbuf[0:64, sl], 1.0, rtab[0:64, sl], ALU.mult, ALU.mult),
                   reads=[TB[sp], CONST], writes=[XHB2] if sp == 0 and
                   tname == "q" else [],
                   appends=[] if sp == 0 and tname == "q" else [XHB2],
                   tag=f"B{h}r2{tname}{sp}")
                op("vector", lambda e, tbuf=tbuf, sl=sl, pslot=pslot:
                   e.tensor_tensor(
                       tbuf[0:64, sl],
                       mstg[0:64, (pslot % 2) * 512:(pslot % 2 + 1) * 512],
                       xh2[0:64, (pslot % 4) * 512:(pslot % 4 + 1) * 512],
                       ALU.add),
                   reads=[MSTG[pslot % 2], XHB2], appends=[TB[sp]],
                   tag=f"B{h}r3{tname}{sp}")

    def attention_head(h, rope_cb=None, bg_quota=0):
        Qh, Kh = QhA[h % 2], KhA[h % 2]
        QHB, KHB = QHBA[h % 2], KHBA[h % 2]
        st_ctx = {"pend": [], "rope": rope_cb, "boff": 0, "gctr": 0,
                  "bgdone": 0}

        def prc(qt, qs, kt):
            return ((qt % 2) * 4 + kt // 4) if qs > 0 else qt

        def s1a(qt, qti, qs, h=h):
            kext = (qt + 1) * 128
            nkb = (kext + 511) // 512
            sl_ = qt % 4
            banks = [(st_ctx["boff"] + i) % 4 for i in range(nkb)]
            st_ctx["boff"] += nkb
            for kb in range(nkb):
                n = min(512, kext - kb * 512)
                b = banks[kb]
                op("tensor", lambda e, qt=qt, kb=kb, n=n, b=b: e.matmul(
                    pp[b][:, 0:n], Qh[:, qt * 128:(qt + 1) * 128],
                    Kh[:, kb * 512:kb * 512 + n], start=True, stop=True),
                   reads=[QHB[qt // 4], KHB[kb]], writes=[PP[b]],
                   tag=f"B{h}s1_{qt}_{kb}")
            lb = banks[nkb - 1]
            ln_ = min(512, kext - (nkb - 1) * 512)
            op("vector", lambda e, lb=lb, ln_=ln_: e.tensor_tensor(
                pp[lb][:, ln_ - 128:ln_], pp[lb][:, ln_ - 128:ln_],
                amsk[:, :], ALU.add),
               reads=[CONST], writes=[PP[lb]], tag=f"B{h}msk_{qt}")
            for kb in range(nkb):
                n = min(512, kext - kb * 512)
                b = banks[kb]
                if nkb == 1:
                    op("vector", lambda e, b=b, n=n, qt=qt: e.tensor_reduce(
                        mneg[:, qt:qt + 1], pp[b][:, 0:n], AX.X, ALU.max,
                        negate=True),
                       reads=[PP[b]], writes=[MNEG[qt]], tag=f"B{h}s1r_{qt}")
                else:
                    op("vector", lambda e, b=b, n=n, sl_=sl_, kb=kb:
                       e.tensor_reduce(
                        mneg[:, 16 + sl_ * 4 + kb:17 + sl_ * 4 + kb],
                        pp[b][:, 0:n], AX.X, ALU.max, negate=True),
                       reads=[PP[b]], writes=[MNEGP[sl_][kb]],
                       tag=f"B{h}s1r_{qt}_{kb}")
            if nkb > 1:
                op("vector", lambda e, qt=qt, nkb=nkb, sl_=sl_:
                   e.tensor_reduce(
                    mneg[:, qt:qt + 1],
                    mneg[:, 16 + sl_ * 4:16 + sl_ * 4 + nkb], AX.X, ALU.min),
                   reads=MNEGP[sl_][0:nkb], writes=[MNEG[qt]],
                   tag=f"B{h}s1c_{qt}")
            for kb in range(nkb):
                n = min(512, kext - kb * 512)
                b = banks[kb]
                c = prc(qt, qs, kb * 4)
                op("scalar", lambda e, b=b, n=n, qt=qt, c=c, sl_=sl_, kb=kb:
                   e.activation(
                    Pr[:, c * 512:c * 512 + n], pp[b][:, 0:n], AF.Exp,
                    bias=mneg[:, qt:qt + 1],
                    accum_out=stats[:, 8 + sl_ * 4 + kb:9 + sl_ * 4 + kb]),
                   reads=[PP[b], MNEG[qt]],
                   writes=[PRB[c], LPB[sl_]] if kb == 0 else [PRB[c]],
                   appends=[] if kb == 0 else [LPB[sl_]],
                   tag=f"B{h}ex_{qt}_{kb}")

        def s1b(qt, h=h):
            nkb = ((qt + 1) * 128 + 511) // 512
            sl_ = qt % 4
            op("vector", lambda e, nkb=nkb, sl_=sl_: e.tensor_reduce(
                stats[:, 24 + sl_:25 + sl_],
                stats[:, 8 + sl_ * 4:8 + sl_ * 4 + nkb], AX.X, ALU.add),
               reads=[LPB[sl_]], writes=[LB_[sl_]], tag=f"B{h}l_{qt}")
            op("vector", lambda e, sl_=sl_: e.reciprocal(
                stats[:, 28 + sl_:29 + sl_], stats[:, 24 + sl_:25 + sl_]),
               reads=[LB_[sl_]], writes=[RLQ[sl_]], tag=f"B{h}rl_{qt}")
            op("vector", lambda e, sl_=sl_: e.tensor_scalar(
                dgt[:, sl_ * 128:(sl_ + 1) * 128], idt[:, :],
                stats[:, 28 + sl_:29 + sl_], None, ALU.mult),
               reads=[RLQ[sl_], CONST], writes=[DGB[sl_]], tag=f"B{h}dg_{qt}")

        def tav_part(qt, qtl, qs, h=h):
            nkt = (qt + 1)
            sl_ = qt % 4

            def t_g(g, par2, cnt):
                pt_ps = pp[5] if par2 == 0 else ppf
                PT_B = PP[5] if par2 == 0 else PPF
                for j in range(cnt):
                    kt = g * 4 + j
                    c = prc(qt, qs, kt)
                    off = c * 512 + (kt % 4) * 128
                    op("tensor", lambda e, j=j, pt_ps=pt_ps, off=off, sl_=sl_,
                       c=c: e.matmul(
                           pt_ps[:, j * 128:(j + 1) * 128],
                           Pr[:, off:off + 128],
                           dgt[:, sl_ * 128:(sl_ + 1) * 128],
                           start=True, stop=True),
                       reads=[PRB[c], DGB[sl_], CONST],
                       writes=[PT_B] if j == 0 else [],
                       appends=[PT_B] if j > 0 else [],
                       tag=f"B{h}tp_{qt}_{kt}")
                op("vector", lambda e, cnt=cnt, par2=par2, pt_ps=pt_ps:
                   e.tensor_copy(
                    PTt[:, par2 * 512:par2 * 512 + cnt * 128],
                    pt_ps[:, 0:cnt * 128]),
                   reads=[PT_B], writes=[PTRB[par2]],
                   tag=f"B{h}pte_{qt}_{g}")

            def av_g(g, par2, cnt):
                for j in range(cnt):
                    kt = g * 4 + j
                    op("tensor", lambda e, kt=kt, j=j, qtl=qtl, par2=par2,
                       nkt=nkt: e.matmul(
                           pp[4][:, qtl * 128:(qtl + 1) * 128],
                           arena[:, kt * 2048 + h * 128:
                                 kt * 2048 + (h + 1) * 128],
                           PTt[:, par2 * 512 + j * 128:
                               par2 * 512 + (j + 1) * 128],
                           start=(kt == 0), stop=(kt == nkt - 1)),
                       reads=[ACH[kt], PTRB[par2]],
                       writes=[PP[4]] if kt == 0 and qtl == 0 else [],
                       appends=[] if kt == 0 and qtl == 0 else [PP[4]],
                       tag=f"B{h}av_{qt}_{kt}")

            prev = None
            for g in range((nkt + 3) // 4):
                par2 = st_ctx["gctr"] % 2
                st_ctx["gctr"] += 1
                cnt = min(4, nkt - g * 4)
                t_g(g, par2, cnt)
                if prev is not None:
                    av_g(*prev)
                prev = (g, par2, cnt)
            av_g(*prev)

        def y_evac(qs, h=h):
            op("scalar", lambda e, h=h, qs=qs: e.copy(
                ar_y[:, qs * 4:qs * 4 + 4, h, :], pp[4][:, :]),
               reads=[PP[4]],
               writes=[ACH[qs * 4 + i] for i in range(4)],
               appends=[ARENA], tag=f"B{h}ye{qs}")

        point = 0
        for qs in range(NS - 1, -1, -1):
            for qti in range(4):
                qt = qs * 4 + qti
                s1a(qt, qti, qs)
                if st_ctx["rope"] is not None and qti == 3 and qs == NS - 1:
                    cb = st_ctx["rope"]
                    st_ctx["rope"] = None
                    cb()
                for cb in st_ctx["pend"]:
                    cb()
                st_ctx["pend"] = []
                s1b(qt)
                st_ctx["pend"].append(
                    lambda qt=qt, qti=qti, qs=qs: tav_part(qt, qti, qs))
                if qti == 3:
                    st_ctx["pend"].append(lambda qs=qs: y_evac(qs))
                point += 1
                tgt = bg_quota * point // 16
                while st_ctx["bgdone"] < tgt:
                    w1_step()
                    st_ctx["bgdone"] += 1
        for cb in st_ctx["pend"]:
            cb()

    qk_proj(0)
    qk_rope(0)
    bgq = (len(w1_jobs) + H - 1) // H
    for h in range(H):
        if h + 1 < H:
            qk_proj(h + 1)
            attention_head(h, rope_cb=lambda hh=h + 1: qk_rope(hh),
                           bg_quota=bgq)
        else:
            attention_head(h, bg_quota=bgq)
    while w1_state["i"] < len(w1_jobs):
        w1_step()

    def c_transposes(tb):
        xh_t = xh if tb % 2 == 0 else xh2
        XH_T = XHB if tb % 2 == 0 else XHB2
        for gch in range(4):
            half = gch % 2
            tp_ps = pp[2] if half == 0 else pp[3]
            TP_B = PP[2] if half == 0 else PP[3]
            for j in range(4):
                k = gch * 4 + j
                op("tensor", lambda e, k=k, j=j, xh_t=xh_t, tp_ps=tp_ps:
                   e.matmul(
                    tp_ps[:, j * 128:(j + 1) * 128],
                    xh_t[:, k * 128:(k + 1) * 128], idt[:, :],
                    start=True, stop=True),
                   reads=[XH_T, CONST],
                   writes=[TP_B] if j == 0 else [],
                   appends=[TP_B] if j > 0 else [], tag=f"C{tb}tp{k}")
            op("vector", lambda e, half=half, tp_ps=tp_ps: e.tensor_copy(
                mstg[:, half * 512:(half + 1) * 512], tp_ps[:, :]),
               reads=[TP_B], writes=[MSTG[half]], tag=f"C{tb}te{gch}")
            op("sync", lambda e, tb=tb, gch=gch, half=half: e.dma_start(
                out=mts4[tb // 4, :, gch * 4:(gch + 1) * 4,
                         (tb % 4) * 128:(tb % 4 + 1) * 128],
                in_=mstg[:, half * 512:(half + 1) * 512]
                .rearrange("p (c q) -> p c q", q=128)),
               reads=[MSTG[half]], appends=[SCR["mts"]], tag=f"C{tb}ts{gch}",
               dma=True)

    for tb in range(NT):
        r = stage32[:, 2048:4096]
        op("sync", lambda e, tb=tb: e.dma_start(out=stage32[:, 0:2048],
                                                in_=xr3[tb]),
           writes=[STG[0]], tag=f"C{tb}x", dma=True)
        for ecb in range(4):
            par = ecb % 2
            for yc in range(NC):
                op("tensor", lambda e, tb=tb, ecb=ecb, yc=yc, par=par: e.matmul(
                    pp[par][:, :],
                    arena[:, tb * 2048 + yc * 128:tb * 2048 + (yc + 1) * 128],
                    wbuf[:, yc * 2048 + ecb * 512:yc * 2048 + (ecb + 1) * 512],
                    start=(yc == 0), stop=(yc == NC - 1)),
                   reads=[ACH[tb], WCH[yc]],
                   writes=[PP[par]] if yc == 0 else [],
                   appends=[PP[par]] if yc > 0 else [],
                   tag=f"C{tb}wo{ecb}_{yc}")
            op("vector", lambda e, ecb=ecb, par=par, r=r: e.tensor_tensor(
                r[:, ecb * 512:(ecb + 1) * 512], pp[par][:, :],
                stage32[:, ecb * 512:(ecb + 1) * 512], ALU.add),
               reads=[PP[par], STG[0]],
               writes=([STG[1], RLBB, RLR] if tb == 0 else [STG[1]])
               if ecb == 0 else [],
               appends=[STG[1]] if ecb > 0 else [], tag=f"C{tb}re{ecb}")
        ln_stats(r, [STG[1]], f"C{tb}", QhA[0][:, :], QHBA[0])
        op("vector", lambda e, r=r: e.tensor_scalar(
            r, r, stats[:, 2:3], stats[:, 6:7], ALU.subtract, ALU.mult),
           reads=[STG[1], STATS], writes=[STG[1]], tag=f"C{tb}x2")
        op("sync", lambda e, tb=tb, r=r: e.dma_start(out=x2s3[tb], in_=r),
           reads=[STG[1]], appends=[SCR["x2s"]], tag=f"C{tb}x2s", dma=True)
        op("vector", lambda e, r=r: e.tensor_scalar(
            r, r, scal[:, 2:3], MAGIC, ALU.mult, ALU.add),
           reads=[STG[1], SCAL], writes=[STG[1]], tag=f"C{tb}mv")
        xh_t = xh if tb % 2 == 0 else xh2
        XH_T = XHB if tb % 2 == 0 else XHB2
        for st_ in range(2):
            op("vector", lambda e, r=r, st_=st_: e.tensor_scalar(
                i32t[:, :], r[:, st_ * 1024:(st_ + 1) * 1024],
                MAGIC, None, ALU.subtract),
               reads=[STG[1]], writes=[I32H[0], I32H[1]], tag=f"C{tb}mi{st_}")
            op("vector", lambda e, xh_t=xh_t, st_=st_: e.tensor_scalar(
                xh_t[:, st_ * 1024:(st_ + 1) * 1024], i32t[:, :],
                7.0, -8.0, ALU.min, ALU.max),
               reads=[I32H[0], I32H[1]], writes=[XH_T] if st_ == 0 else [],
               appends=[XH_T] if st_ > 0 else [], tag=f"C{tb}mc{st_}")
        if tb >= 1:
            c_transposes(tb - 1)
    c_transposes(NT - 1)

    op("sync", lambda e: e.dma_start(
        out=stage32[0:64, 0:128],
        in_=ap["b1"].rearrange("a (c p) -> (a c) p", p=128)),
       writes=[STG[0]], tag="ld_b1", dma=True)
    op("tensor", lambda e: e.transpose(
        pp[5][:, 0:64], stage32[0:64, 0:128], idt32[0:64, 0:64]),
       reads=[STG[0], CONST], writes=[PP[5]], tag="tp_b1")
    op("vector", lambda e: e.tensor_copy(b1qt[:, :], pp[5][:, 0:64]),
       reads=[PP[5], SCR["mts"]], writes=[B1Q], tag="ev_b1")
    op("vector", lambda e: e.reciprocal(scal[:, 7:8], scal[:, 5:6]),
       reads=[SCAL], appends=[SCAL], tag="ra1")
    op("vector", lambda e: e.tensor_scalar(
        b1qt[:, :], b1qt[:, :], scal[:, 7:8], MAGIC, ALU.mult, ALU.add),
       reads=[B1Q, SCAL], writes=[B1Q], tag="b1q1")
    op("vector", lambda e: e.tensor_scalar(
        i32t[:, 0:64], b1qt[:, :], MAGIC, None, ALU.subtract),
       reads=[B1Q], writes=[I32H[0]], tag="b1q2")
    op("vector", lambda e: e.tensor_scalar(
        b1qt[:, :], i32t[:, 0:64], CLIP32, -CLIP32, ALU.min, ALU.max),
       reads=[I32H[0]], writes=[B1Q], tag="b1q3")
    op("vector", lambda e: e.tensor_scalar(
        b1qt[:, :], b1qt[:, :], scal[:, 5:6], None, ALU.mult),
       reads=[B1Q, SCAL], writes=[B1Q], tag="b1q4")

    w2_jobs = [("mdma", i) for i in range(4)]
    for i in range(60):
        w2_jobs += [("mred", i), ("mdma", i + 4)]
    w2_jobs += [("mred", i) for i in range(60, 64)]
    w2_jobs.append(("fin", 0))
    w2_jobs += [("qdma", i) for i in range(4)]
    for i in range(64):
        w2_jobs += [("qa", i), ("qb", i)]
        if i + 4 < 64:
            w2_jobs.append(("qdma", i + 4))
    w2_state = {"i": 0}

    def w2_step():
        if w2_state["i"] >= len(w2_jobs):
            return
        kind, i = w2_jobs[w2_state["i"]]
        w2_state["i"] += 1
        slt = i % 4
        so = slt * 2048
        if kind == "mdma":
            op("sync", lambda e, i=i, so=so: e.dma_start(
                out=r2sup[:, so:so + 2048], in_=w2r[i]),
               reads=[SCR["mts"]] if i < 4 else [],
               writes=[W2ST[slt]], tag=f"w2l{i}", dma=True)
        elif kind == "mred":
            op("vector", lambda e, i=i, so=so: e.tensor_reduce(
                wmx[:, i:i + 1], r2sup[:, so:so + 2048], AX.X, ALU.max,
                apply_absolute_value=True),
               reads=[W2ST[slt]], writes=[WMX] if i == 0 else [],
               appends=[WMX] if i > 0 else [], tag=f"w2m{i}")
        elif kind == "fin":
            op("vector", lambda e: e.tensor_reduce(
                wmx[:, 65:66], wmx[:, 0:64], AX.X, ALU.max),
               reads=[WMX], appends=[WMX], tag="w2fin")
            op("gpsimd", lambda e: e.tensor_reduce(
                sv[0:1, 8:9], wmx[:, 65:66], AX.C, ALU.max),
               reads=[WMX], appends=[SVB], tag="w2fing")
            bcast(sv[0:1, 8:9], 8, [SVB], "w2max")
            op("vector", lambda e: e.tensor_scalar(
                scal[:, 8:9], scal[:, 8:9], 1.0 / 7.0, None, ALU.mult),
               reads=[SCAL], appends=[SCAL], tag="sw2")
            op("vector", lambda e: e.tensor_tensor(
                scal[:, 9:10], scal[:, 1:2], scal[:, 8:9], ALU.mult),
               reads=[SCAL], appends=[SCAL], tag="b2s")
            op("vector", lambda e: e.reciprocal(scal[:, 10:11], scal[:, 8:9]),
               reads=[SCAL], appends=[SCAL], tag="rsw2")
        elif kind == "qdma":
            op("gpsimd", lambda e, i=i, so=so: e.dma_start(
                out=r2sup[:, so:so + 2048], in_=w2r[i]),
               reads=[SCR["mts"]] if i < 4 else [],
               writes=[W2ST[slt]], tag=f"w2ql{i}", dma=True)
        elif kind == "qa":
            op("vector", lambda e, so=so: e.tensor_scalar(
                r2sup[:, so:so + 2048], r2sup[:, so:so + 2048],
                scal[:, 10:11], MAGIC, ALU.mult, ALU.add),
               reads=[W2ST[slt], SCAL], appends=[W2ST[slt]], tag=f"w2q1_{i}")
        else:
            for st_ in range(2):
                op("vector", lambda e, st_=st_, so=so: e.tensor_scalar(
                    i32t[:, :],
                    r2sup[:, so + st_ * 1024:so + (st_ + 1) * 1024],
                    MAGIC, None, ALU.subtract),
                   reads=[W2ST[slt]], writes=[I32H[0], I32H[1]],
                   tag=f"w2q2_{i}_{st_}")
                op("vector", lambda e, st_=st_: e.tensor_scalar(
                    xh[:, st_ * 1024:(st_ + 1) * 1024], i32t[:, :],
                    7.0, -8.0, ALU.min, ALU.max),
                   reads=[I32H[0], I32H[1]], writes=[XHB] if st_ == 0 else [],
                   appends=[XHB] if st_ > 0 else [], tag=f"w2q3_{i}_{st_}")
            op("gpsimd", lambda e, i=i: e.dma_start(
                out=w2i4[:, i, :, :].rearrange("b p q -> p b q"),
                in_=xh[:, :].rearrange("p (b q) -> p b q", q=512)),
               reads=[XHB], appends=[SCR["w2i"]], tag=f"w2qs_{i}", dma=True)

    GSZ = 8
    NG = NF // GSZ
    MT0 = 16384

    def mt_load(g, tsup):
        base = MT0 + (tsup % 2) * 8192
        wc = 8 + (tsup % 2) * 4
        op("sync", lambda e, tsup=tsup, base=base: e.dma_start(
            out=wbuf[:, base:base + 8192], in_=mts2[tsup]),
           reads=[SCR["mts"]], writes=[WCH[wc + j] for j in range(4)],
           tag=f"D{g}_{tsup}mtl", dma=True)

    def slab_load(g, fi, first=False):
        fch = g * GSZ + fi
        for jj in range(2):
            op("sync", lambda e, fch=fch, fi=fi, jj=jj: e.dma_start(
                out=wbuf[:, fi * 2048 + jj * 1024:
                         fi * 2048 + (jj + 1) * 1024],
                in_=w1i[fch * 128:(fch + 1) * 128,
                        jj * 1024:(jj + 1) * 1024]),
               reads=[SCR["w1i"]],
               writes=[W1S[fi][jj]] + ([WCH[fi]] if first else []),
               tag=f"D{g}_{fi}wl{jj}", dma=True)

    def d_nstore(tsup, fch):
        t1 = fch % 2
        op("gpsimd", lambda e, tsup=tsup, fch=fch, t1=t1: e.dma_start(
            out=ns4[tsup, fch], in_=mstg[:, t1 * 512:(t1 + 1) * 512]),
           reads=[MSTG[t1]], appends=[SCR["nscr"]], tag=f"D{tsup}_{fch}ns",
           dma=True)

    mt_load(0, 0)
    for fi in range(GSZ):
        slab_load(0, fi, first=True)
    d_it = 0
    for g in range(NG):
        for tsup in range(NS):
            if tsup + 1 < NS:
                mt_load(g, tsup + 1)
            elif g + 1 < NG:
                mt_load(g + 1, 0)
            mbase = MT0 + (tsup % 2) * 8192
            wc = 8 + (tsup % 2) * 4
            for fi in range(GSZ):
                fch = g * GSZ + fi
                par = fi % 2
                for dch in range(NC):
                    op("tensor", lambda e, dch=dch, fi=fi, par=par,
                       mbase=mbase: e.matmul(
                        pp[par][:, :],
                        wbuf[:, fi * 2048 + dch * 128:
                             fi * 2048 + (dch + 1) * 128],
                        wbuf[:, mbase + dch * 512:mbase + (dch + 1) * 512],
                        start=(dch == 0), stop=(dch == NC - 1)),
                       reads=[W1S[fi][dch // 8], WCH[wc + dch // 4]],
                       writes=[PP[par]] if dch == 0 else [],
                       appends=[PP[par]] if dch > 0 else [],
                       tag=f"D{g}_{tsup}_{fi}m{dch}")
                t1 = fi % 2
                t1sl = slice(2048 + t1 * 512, 2048 + (t1 + 1) * 512)
                wrt = [T1B[t1]] + ([STG[1]] if g == 0 and tsup == 0 and
                                   fi < 2 else [])
                op("scalar", lambda e, fch=fch, par=par, t1sl=t1sl:
                   e.activation(
                    stage32[:, t1sl], pp[par][:, :], AF.Relu,
                    bias=b1qt[:, fch:fch + 1], scale=scal[:, 5:6]),
                   reads=[PP[par], B1Q, SCAL], writes=wrt,
                   tag=f"D{g}_{tsup}_{fi}relu")
                op("vector", lambda e, t1sl=t1sl: e.tensor_scalar(
                    stage32[:, t1sl], stage32[:, t1sl], scal[:, 3:4], MAGIC,
                    ALU.mult, ALU.add),
                   reads=[T1B[t1], SCAL], appends=[T1B[t1]],
                   tag=f"D{g}_{tsup}_{fi}q1")
                op("vector", lambda e, t1=t1, t1sl=t1sl: e.tensor_scalar(
                    i32t[:, t1 * 512:(t1 + 1) * 512], stage32[:, t1sl], MAGIC,
                    None, ALU.subtract),
                   reads=[T1B[t1]], writes=[I32H[t1]],
                   tag=f"D{g}_{tsup}_{fi}q2")
                op("vector", lambda e, t1=t1: e.tensor_scalar(
                    mstg[:, t1 * 512:(t1 + 1) * 512],
                    i32t[:, t1 * 512:(t1 + 1) * 512], 15.0, 0.0,
                    ALU.min, ALU.max),
                   reads=[I32H[t1]], writes=[MSTG[t1]],
                   tag=f"D{g}_{tsup}_{fi}q3")
                if fi >= 1:
                    d_nstore(tsup, fch - 1)
                if g + 1 < NG and tsup == NS - 1:
                    slab_load(g + 1, fi)
                d_it += 1
                if d_it % 2 == 0:
                    w2_step()
            d_nstore(tsup, g * GSZ + GSZ - 1)
    while w2_state["i"] < len(w2_jobs):
        w2_step()

    op("sync", lambda e: e.dma_start(out=r2sup[0:1, 0:2048], in_=ap["b2"]),
       writes=[R2T[0], W2ST[0]], tag="ld_b2", dma=True)
    op("vector", lambda e: e.tensor_scalar(
        r2sup[0:1, 0:2048], r2sup[0:1, 0:2048], scal[0:1, 10:11], MAGIC,
        ALU.mult, ALU.add),
       reads=[R2T[0], SCAL], writes=[R2T[0]], tag="b2q1")
    for st_ in range(2):
        op("vector", lambda e, st_=st_: e.tensor_scalar(
            i32t[0:1, :], r2sup[0:1, st_ * 1024:(st_ + 1) * 1024],
            MAGIC, None, ALU.subtract),
           reads=[R2T[0]], writes=[I32H[0], I32H[1]], tag=f"b2q2_{st_}")
        op("vector", lambda e, st_=st_: e.tensor_scalar(
            r2sup[0:1, st_ * 1024:(st_ + 1) * 1024], i32t[0:1, :],
            CLIP32, -CLIP32, ALU.min, ALU.max),
           reads=[I32H[0], I32H[1]], writes=[R2T[0]], tag=f"b2q3_{st_}")
    op("vector", lambda e: e.tensor_scalar(
        r2sup[0:1, 0:2048], r2sup[0:1, 0:2048], scal[0:1, 9:10], None,
        ALU.mult),
       reads=[R2T[0], SCAL], writes=[R2T[0]], tag="b2q4")
    for q4 in range(4):
        op("tensor", lambda e, q4=q4: e.matmul(
            pp[5][:, :], one32[:, :], r2sup[0:1, q4 * 512:(q4 + 1) * 512],
            start=True, stop=True),
           reads=[CONST, R2T[0]], writes=[PP[5]], tag=f"b2b{q4}")
        op("vector", lambda e, q4=q4: e.tensor_copy(
            b2qb[:, q4 * 512:(q4 + 1) * 512], pp[5][:, :]),
           reads=[PP[5]], writes=[B2QB] if q4 == 0 else [],
           appends=[B2QB] if q4 > 0 else [], tag=f"b2be{q4}")

    for tsup in range(NS):
        for c in range(NT):
            op("gpsimd", lambda e, tsup=tsup, c=c: e.dma_start(
                out=ar_n[:, 4 * c:4 * (c + 1), :],
                in_=ns4[tsup][4 * c:4 * (c + 1)].rearrange("f p q -> p f q")),
               reads=[SCR["nscr"]], writes=[ACH[c]], tag=f"E{tsup}nl{c}",
               dma=True)
        for ecb in range(4):
            for c in range(NC):
                extra = []
                if tsup == 0 and ecb == 0 and c < 8:
                    extra = W1S[c][:]
                op("sync", lambda e, ecb=ecb, c=c: e.dma_start(
                    out=wb_w2[:, 4 * c:4 * (c + 1), :],
                    in_=w2i4[ecb][4 * c:4 * (c + 1)]
                    .rearrange("f p q -> p f q")),
                   reads=[SCR["w2i"]], writes=[WCH[c]] + extra,
                   tag=f"E{tsup}_{ecb}wl{c}", dma=True)
            for tbi in range(4):
                par = tbi % 2 + (2 if ecb % 2 else 0)
                for fch in range(NF):
                    op("tensor",
                       lambda e, tbi=tbi, fch=fch, par=par: e.matmul(
                           pp[par][:, :],
                           arena[:, fch * 512 + tbi * 128:
                                 fch * 512 + tbi * 128 + 128],
                           wbuf[:, fch * 512:(fch + 1) * 512],
                           start=(fch == 0), stop=(fch == NF - 1)),
                       reads=[ACH[fch // 4], WCH[fch // 4]],
                       writes=[PP[par]] if fch == 0 else [],
                       appends=[PP[par]] if fch > 0 else [],
                       tag=f"E{tsup}_{ecb}_{tbi}m{fch}")
                op("vector", lambda e, tbi=tbi, ecb=ecb, par=par:
                   e.tensor_scalar(
                       r2sup[:, tbi * 2048 + ecb * 512:
                             tbi * 2048 + (ecb + 1) * 512],
                       pp[par][:, :], scal[:, 9:10], None, ALU.mult),
                   reads=[PP[par], SCAL],
                   writes=([R2T[tbi], W2ST[tbi]] if tsup == 0
                           else [R2T[tbi]]) if ecb == 0 else [],
                   appends=[R2T[tbi]] if ecb > 0 else [],
                   tag=f"E{tsup}_{ecb}_{tbi}ev")
        for tbi in range(4):
            tb = tsup * 4 + tbi
            r2 = r2sup[:, tbi * 2048:(tbi + 1) * 2048]
            op("vector", lambda e, r2=r2: e.tensor_tensor(
                r2, r2, b2qb[:, :], ALU.add),
               reads=[R2T[tbi], B2QB], writes=[R2T[tbi]], tag=f"E{tb}badd")
            op("sync", lambda e, tb=tb: e.dma_start(
                out=stage32[:, 0:2048], in_=x2s3[tb]),
               reads=[SCR["x2s"]], writes=[STG[0]], tag=f"E{tb}x2l", dma=True)
            op("vector", lambda e, r2=r2: e.tensor_tensor(
                r2, r2, stage32[:, 0:2048], ALU.add),
               reads=[R2T[tbi], STG[0]], writes=[R2T[tbi]], tag=f"E{tb}radd")
            ln_stats(r2, [R2T[tbi]], f"E{tb}",
                     stage32[:, 2048:4096], [STG[1], T1B[0], T1B[1]])
            op("vector", lambda e, r2=r2: e.tensor_scalar(
                r2, r2, stats[:, 2:3], stats[:, 6:7], ALU.subtract, ALU.mult),
               reads=[R2T[tbi], STATS], writes=[R2T[tbi]], tag=f"E{tb}x3")
            op("sync", lambda e, tb=tb, r2=r2: e.dma_start(out=out3[tb],
                                                           in_=r2),
               reads=[R2T[tbi]], tag=f"E{tb}out", dma=True)

    t.finalize()

    with nc.Block() as block:
        @block.sync
        def _(eng):
            t.emit("sync", eng)
            for q in ("hw", "sw"):
                for i in range(NLANES):
                    lane = f"{q}{i}"
                    if t.totals[lane]:
                        eng.wait_ge(t.sems[lane], t.totals[lane])

        @block.gpsimd
        def _(eng):
            t.emit("gpsimd", eng)
            for i in range(NLANES):
                lane = f"sw{i}"
                if t.totals[lane]:
                    eng.wait_ge(t.sems[lane], t.totals[lane])

        @block.tensor
        def _(eng):
            t.emit("tensor", eng)

        @block.vector
        def _(eng):
            t.emit("vector", eng)

        @block.scalar
        def _(eng):
            t.emit("scalar", eng)

    return nc


_cached = {}


def kernel(**inputs):
    if "nc" not in _cached:
        _cached["nc"] = build()
    nc = _cached["nc"]
    f32 = np.float32
    x = np.ascontiguousarray(np.asarray(inputs["L0_attn_x"], dtype=f32))
    wkva = np.asarray(inputs["L0_attn_w_kva"], dtype=f32)
    kva_used = np.ascontiguousarray(np.concatenate(
        [wkva[:, 0:KR]] +
        [wkva[:, KR + h * 2 * D:KR + h * 2 * D + D] for h in range(H)], axis=1))
    base = {
        "gq": np.ascontiguousarray(np.asarray(inputs["L0_attn_g_q"], f32)[None, :]),
        "gkv": np.ascontiguousarray(np.asarray(inputs["L0_attn_g_kv"], f32)[None, :]),
        "wqa": np.ascontiguousarray(np.asarray(inputs["L0_attn_w_qa"], f32)),
        "wqb": np.ascontiguousarray(np.asarray(inputs["L0_attn_w_qb"], f32)),
        "wkva": kva_used,
        "wkvb": np.ascontiguousarray(np.asarray(inputs["L0_attn_w_kvb"], f32)),
        "wo": np.ascontiguousarray(np.asarray(inputs["L0_attn_w_o"], f32)),
        "w1t": np.ascontiguousarray(np.asarray(inputs["L0_ffn_W1"], f32).T),
        "b1": np.ascontiguousarray(np.asarray(inputs["L0_ffn_b1"], f32)[None, :]),
        "w2t": np.ascontiguousarray(np.asarray(inputs["L0_ffn_W2"], f32).T),
        "b2": np.ascontiguousarray(np.asarray(inputs["L0_ffn_b2"], f32)[None, :]),
        "s1": np.ascontiguousarray(np.asarray(inputs["L0_ffn_s_act1"], f32)[:, None]),
        "s2": np.ascontiguousarray(np.asarray(inputs["L0_ffn_s_act2"], f32)[:, None]),
    }
    in_maps = []
    for c in range(8):
        m = dict(base)
        m["x"] = np.ascontiguousarray(x[c])
        in_maps.append(m)
    res = run_bass_kernel_spmd(nc, in_maps, core_ids=list(range(8)))
    return np.stack([np.asarray(res.results[c]["out"]) for c in range(8)],
                    axis=0)


# version 17
# speedup vs baseline: 1.4324x; 1.0540x over previous
import numpy as np
import ml_dtypes
import concourse.bass as bass
import concourse.mybir as mybir
from concourse import dve_ops
from concourse.bass_utils import run_bass_kernel_spmd

F32 = mybir.dt.float32
BF16 = mybir.dt.bfloat16
INT8 = mybir.dt.int8
I32 = mybir.dt.int32
AF = mybir.ActivationFunctionType
ALU = mybir.AluOpType
AX = mybir.AxisListType

MAGIC = 12582912.0
CLIP32 = 2147483520.0
S, E, H, D, QR, KR, RD, F = 2048, 2048, 16, 128, 64, 128, 64, 8192
EPS = 1e-5
ISD = 0.08838834764831845
NT, NS, NC, NF = S // 128, S // 512, E // 128, F // 128
VCOLS = H * D
WKVA_C = KR + VCOLS

ENGINES = ("sync", "gpsimd", "tensor", "vector", "scalar")
ENG_SEM = {"tensor": "pe", "vector": "dv", "scalar": "ac", "gpsimd": "gp",
           "sync": "sy"}
DMA_QUEUE = {"sync": "hw", "gpsimd": "sw"}
NLANES = 8


class Buf:
    __slots__ = ("name", "writers", "readers")

    def __init__(self, name):
        self.name = name
        self.writers = []
        self.readers = []


class Op:
    __slots__ = ("eng", "fn", "deps", "tag", "dma", "inc_sem", "inc_val",
                 "_clk", "_wait")

    def __init__(self, eng, fn, tag, dma):
        self.eng = eng
        self.fn = fn
        self.deps = []
        self.tag = tag
        self.dma = dma
        self.inc_sem = None
        self.inc_val = 0
        self._clk = {}
        self._wait = []


class Trk:
    def __init__(self, nc):
        self.nc = nc
        self.ops = {e: [] for e in ENGINES}
        names = ["pe", "dv", "ac", "gp", "sy"]
        for q in ("hw", "sw"):
            names += [f"{q}{i}" for i in range(NLANES)]
        self.sems = {s: nc.alloc_semaphore(s) for s in names}
        self.dma_count = {"hw": 0, "sw": 0}
        self.lane_last = {f"{q}{i}": None for q in ("hw", "sw")
                          for i in range(NLANES)}

    def op(self, eng, fn, reads=(), writes=(), appends=(), tag="", dma=False):
        o = Op(eng, fn, tag, dma)
        deps = set()
        for b in reads:
            deps.update(b.writers)
        for b in writes:
            deps.update(b.writers)
            deps.update(b.readers)
        if dma:
            q = DMA_QUEUE[eng]
            lane = f"{q}{self.dma_count[q] % NLANES}"
            self.dma_count[q] += 1
            o.inc_sem = lane
            prev = self.lane_last[lane]
            if prev is not None:
                deps.add(prev)
            self.lane_last[lane] = o
        o.deps = list(deps)
        self.ops[eng].append(o)
        for b in reads:
            b.readers.append(o)
        for b in writes:
            b.writers = [o]
            b.readers = []
        for b in appends:
            b.writers.append(o)
        return o

    def finalize(self):
        needs = set()
        for e in ENGINES:
            for o in self.ops[e]:
                needs.update(o.deps)
        cnt = {s: 0 for s in self.sems}
        for e in ENGINES:
            for o in self.ops[e]:
                if o.dma:
                    cnt[o.inc_sem] += 16
                    o.inc_val = cnt[o.inc_sem]
                elif o in needs:
                    s = ENG_SEM[e]
                    cnt[s] += 1
                    o.inc_sem, o.inc_val = s, cnt[s]
        idx = {e: 0 for e in ENGINES}
        clock = {e: {} for e in ENGINES}
        done = set()
        n_total = sum(len(self.ops[e]) for e in ENGINES)
        n_done = 0
        progress = True
        while progress:
            progress = False
            for e in ENGINES:
                while idx[e] < len(self.ops[e]):
                    o = self.ops[e][idx[e]]
                    if not all(d in done for d in o.deps):
                        break
                    ck = clock[e]
                    w = []
                    for dd in sorted(o.deps,
                                     key=lambda d: (d.inc_sem, d.inc_val)):
                        s, v = dd.inc_sem, dd.inc_val
                        if ck.get(s, 0) < v:
                            w.append((s, v))
                            ck[s] = v
                            for s2, v2 in dd._clk.items():
                                if ck.get(s2, 0) < v2:
                                    ck[s2] = v2
                    o._wait = w
                    o._clk = dict(ck)
                    if o.inc_sem:
                        o._clk[o.inc_sem] = max(o._clk.get(o.inc_sem, 0),
                                                o.inc_val)
                    done.add(o)
                    idx[e] += 1
                    n_done += 1
                    progress = True
        if n_done != n_total:
            stuck = {e: self.ops[e][idx[e]].tag for e in ENGINES
                     if idx[e] < len(self.ops[e])}
            raise RuntimeError(f"build-order deadlock: {stuck}")
        self.totals = dict(cnt)

    def emit(self, engine, eng):
        for o in self.ops[engine]:
            for s, v in o._wait[1:]:
                eng.wait_ge(self.sems[s], v)
            ins = o.fn(eng)
            if o._wait:
                s, v = o._wait[0]
                ins._wait_ge(self.sems[s], v)
            if o.inc_sem:
                ins.then_inc(self.sems[o.inc_sem], 16 if o.dma else 1)


def _consts():
    c = {}
    c["ident"] = np.eye(128, dtype=ml_dtypes.bfloat16)
    c["ident32"] = np.eye(128, dtype=np.float32)
    Smat = np.zeros((RD, RD), dtype=np.float32)
    half = RD // 2
    for i in range(half):
        Smat[i, i + half] = -1.0
        Smat[i + half, i] = 1.0
    c["stmat"] = np.ascontiguousarray(Smat.T.astype(ml_dtypes.bfloat16))
    c["trimask"] = np.ascontiguousarray(
        np.triu(np.ones((128, 128), np.float32)).astype(ml_dtypes.bfloat16))
    am = np.triu(np.full((128, 128), -1e30, np.float32), 1)
    c["addmask"] = np.ascontiguousarray(am.astype(ml_dtypes.bfloat16))
    c["ones_col"] = np.ones((128, 1), dtype=ml_dtypes.bfloat16)
    c["ones_row"] = np.ones((1, 128), dtype=ml_dtypes.bfloat16)
    c["ones32"] = np.ones((1, 128), dtype=np.float32)
    inv_freq = 1.0 / (10000.0 ** (np.arange(0, RD, 2, dtype=np.float64) / RD))
    freqs = np.arange(S, dtype=np.float64)[:, None] * inv_freq[None, :]
    cs, sn = np.cos(freqs).T, np.sin(freqs).T
    c["costab"] = np.ascontiguousarray(
        np.concatenate([cs, cs], 0).astype(ml_dtypes.bfloat16))
    c["sintab"] = np.ascontiguousarray(
        np.concatenate([sn, sn], 0).astype(ml_dtypes.bfloat16))
    return c


def build():
    nc = bass.Bass("TRN2", target_bir_lowering=False)
    t = Trk(nc)

    ap = {}
    for name, shape in (("x", [S, E]), ("gq", [1, E]), ("gkv", [1, E]),
                        ("wqa", [E, QR]), ("wqb", [QR, E]),
                        ("wkva", [E, WKVA_C]), ("wkvb", [KR, E]),
                        ("wo", [E, E]), ("w1t", [E, F]), ("b1", [1, F]),
                        ("w2t", [F, E]), ("b2", [1, E]), ("s1", [1, 1]),
                        ("s2", [1, 1])):
        ap[name] = nc.dram_tensor(name, shape, F32, kind="ExternalInput").ap()
    out = nc.dram_tensor("out", [S, E], F32, kind="ExternalOutput").ap()
    for k, v in _consts().items():
        ap[k] = nc.inline_tensor(v, name=f"c_{k}").ap()

    x2s = nc.dram_tensor("x2s", [S, E], F32, kind="Internal").ap()
    mts = nc.dram_tensor("mts", [NS * 128, NC * 512], BF16, kind="Internal").ap()
    w1i = nc.dram_tensor("w1i", [NF * 128, NC * 128], BF16, kind="Internal").ap()
    w2i = nc.dram_tensor("w2i", [4 * NF * 128, 512], INT8, kind="Internal").ap()
    nscr = nc.dram_tensor("nscr", [NS * NF * 128, 512], INT8, kind="Internal").ap()

    AL = nc.alloc_sbuf_tensor
    arena = AL("arena", [128, NT * 2048], BF16)
    wbuf = AL("wbuf", [128, 32768], BF16)
    stage32 = AL("stage32", [128, 4096], F32)
    xh = AL("xh", [128, 2048], BF16)
    i32t = AL("i32t", [128, 1024], I32)
    xh2 = AL("xh2", [128, 2048], BF16)
    mstg = AL("mstg", [128, 1024], BF16)
    scal = AL("scal", [128, 16], F32)
    stats = AL("stats", [128, 32], F32)
    sv = AL("svals", [1, 16], F32)
    wmx = AL("wmx", [128, 66], F32)
    idt = AL("idt", [128, 128], BF16)
    idt32 = AL("idt32", [128, 128], F32)
    stm = AL("stm", [64, 64], BF16)
    trim = AL("trim", [128, 128], BF16)
    amsk = AL("amsk", [128, 128], BF16)
    onec = AL("onec", [128, 1], BF16)
    oner = AL("oner", [1, 128], BF16)
    one32 = AL("one32", [1, 128], F32)
    guards = []

    def tmp_alloc(name, shape, dt):
        g = nc.sbuf_tensor(name, shape, dt)
        guards.append(g)
        return g.__enter__()
    kct = tmp_alloc("kct", [128, 2048], BF16)
    tT = tmp_alloc("tTq", [64, 2048], BF16)
    wqb_b = tmp_alloc("wqb_b", [64, 2048], BF16)
    wkvb_b = tmp_alloc("wkvb_b", [128, 2048], BF16)
    ga = nc.sbuf_tensor("wqa_b", [128, NC * QR], BF16)
    wqa_b = ga.__enter__()
    gb = nc.sbuf_tensor("gq_pc", [128, NC], F32)
    gq_pc = gb.__enter__()
    gc = nc.sbuf_tensor("gkv_pc", [128, NC], F32)
    gkv_pc = gc.__enter__()
    gc.__exit__(None, None, None)
    gb.__exit__(None, None, None)
    ga.__exit__(None, None, None)
    QhA = [tmp_alloc("Qh0", [128, 2048], BF16), tmp_alloc("Qh1", [128, 2048], BF16)]
    KhA = [tmp_alloc("Kh0", [128, 2048], BF16), tmp_alloc("Kh1", [128, 2048], BF16)]
    Pr = tmp_alloc("Pr", [128, 4096], BF16)
    PTt = tmp_alloc("PTt", [128, 1024], BF16)
    dgt = tmp_alloc("dgt", [128, 512], BF16)
    rtab = tmp_alloc("rtab", [128, S], BF16)
    mneg = tmp_alloc("mneg", [128, 32], F32)
    for g in reversed(guards):
        g.__exit__(None, None, None)
    rlb = stage32[:, 3584:4096]
    r2sup = AL("r2sup", [128, 8192], F32)
    b2qb = AL("b2qb", [128, 2048], BF16)
    b1qt = AL("b1qt", [128, 64], F32)

    WV0, WV1, KCW = 8192, 16384, 24576
    W1A, W1B, MTSL = 0, 2048, 8192

    pp = [nc.alloc_psum_tensor(f"pp{i}", [128, 512], F32) for i in range(6)]
    ppbt = [nc.alloc_psum_tensor("ppb0", [128, 512], BF16)]
    ppf = nc.alloc_psum_tensor("ppf", [128, 512], F32)
    p1 = pp[5][0:1, 0:512]

    PP = [Buf(f"pp{i}") for i in range(6)]
    PPB = [Buf("ppb0")]
    PPF = Buf("ppf")
    P1B = Buf("p1")
    CONST = Buf("const")
    ARENA = Buf("arena")
    STG = [Buf("stg0"), Buf("stg1")]
    T1B = [Buf("t1a"), Buf("t1b")]
    XHB = Buf("xh")
    XHB2 = Buf("xh2")
    I32H = [Buf("i32a"), Buf("i32b")]
    MSTG = [Buf("mstg0"), Buf("mstg1")]
    STATS = Buf("stats")
    SCAL = Buf("scal")
    SVB = Buf("sv")
    WMX = Buf("wmx")
    KCT = Buf("kct")
    TTB = Buf("tT")
    WQA = Buf("wqa_b")
    WQB = Buf("wqb_b")
    WKVB = Buf("wkvb_b")
    GQ = Buf("gq")
    GKV = Buf("gkv")
    QHBA = [[Buf(f"Qh{i}_{j}") for j in range(4)] for i in range(2)]
    KHBA = [[Buf(f"Kh{i}_{j}") for j in range(4)] for i in range(2)]
    PRB = [Buf(f"Pr{i}") for i in range(8)]
    PTRB = [Buf("PT0"), Buf("PT1")]
    DGB = [Buf(f"dg{i}") for i in range(4)]
    MNEG = [Buf(f"mneg{i}") for i in range(16)]
    MNEGP = [[Buf(f"mnegp{s_}_{i}") for i in range(4)] for s_ in range(4)]
    LPB = [Buf(f"lp{i}") for i in range(4)]
    LB_ = [Buf(f"lb{i}") for i in range(4)]
    RLQ = [Buf(f"rlq{i}") for i in range(4)]
    RLR = Buf("rlrow")
    RLBB = Buf("rlb")
    XB = [[Buf(f"xb{g}_{b}") for b in range(4)] for g in range(4)]
    WCH = [Buf(f"wch{i}") for i in range(16)]
    ACH = [Buf(f"ach{i}") for i in range(NT)]
    KCWS = [Buf(f"kcws{i}") for i in range(4)]
    W1S = [[Buf(f"w1s{s_}_{j}") for j in range(2)] for s_ in range(8)]
    W2ST = [Buf(f"w2st{i}") for i in range(4)]
    B1Q = Buf("b1q")
    B2QB = Buf("b2qb")
    R2T = [Buf(f"r2_{i}") for i in range(4)]
    SCR = {n: Buf("scr_" + n) for n in ("x2s", "mts", "w1i", "w2i", "nscr")}

    op = t.op

    xr3 = ap["x"].rearrange("(t p) e -> t p e", p=128)
    wkva3 = ap["wkva"].rearrange("(c p) q -> c p q", p=128)
    wo3 = ap["wo"].rearrange("(c p) e -> c p e", p=128)
    w1r = ap["w1t"].rearrange("(c p) f -> c p f", p=128)
    w2r = ap["w2t"].rearrange("(c p) d -> c p d", p=128)
    w1i4 = w1i.rearrange("(f p) (c q) -> f p c q", p=128, q=128)
    w2i4 = w2i.rearrange("(b f p) q -> b f p q", f=NF, p=128)
    mts4 = mts.rearrange("(s p) (c q) -> s p c q", p=128, q=512)
    mts2 = mts.rearrange("(s p) w -> s p w", p=128)
    ns4 = nscr.rearrange("(s f p) q -> s f p q", f=NF, p=128)
    x2s3 = x2s.rearrange("(t p) e -> t p e", p=128)
    out3 = out.rearrange("(t p) e -> t p e", p=128)
    wb_xb = wbuf[:, 0:8192].rearrange("p (k b q) -> p k b q", b=4, q=128)
    wb_wv = [wbuf[:, WV0:WV0 + 8192].rearrange("p (k q) -> p k q", q=512),
             wbuf[:, WV1:WV1 + 8192].rearrange("p (k q) -> p k q", q=512)]
    wb_kcw = wbuf[:, KCW:KCW + 2048].rearrange("p (k q) -> p k q", q=128)
    wb_wo = wbuf[:, :].rearrange("p (c e) -> p c e", e=2048)
    wb_w2 = wbuf[:, :].rearrange("p (f q) -> p f q", q=512)
    ar_n = arena[:, 0:NF * 512].rearrange("p (f q) -> p f q", q=512)
    ar_y = arena[:, :].rearrange("p (c h q) -> p c h q", h=H, q=128)

    for src, dst in (("ident", idt), ("ident32", idt32), ("addmask", amsk),
                     ("stmat", stm), ("trimask", trim),
                     ("ones_col", onec), ("ones_row", oner), ("ones32", one32),
                     ("costab", rtab[0:64, :]),
                     ("sintab", rtab[64:128, :])):
        op("sync", lambda e, s=src, d=dst: e.dma_start(out=d[:, :], in_=ap[s]),
           appends=[CONST], tag=f"ld_{src}", dma=True)
    for nm, coff in (("gq", 1024), ("gkv", 1152)):
        op("sync", lambda e, nm=nm, coff=coff: e.dma_start(
            out=stage32[0:16, coff:coff + 128],
            in_=ap[nm].rearrange("a (c p) -> (a c) p", p=128)),
           appends=[STG[0]], tag=f"ld_{nm}", dma=True)
    op("tensor", lambda e: e.transpose(
        pp[5][:, 0:16], stage32[0:16, 1024:1152], idt32[0:16, 0:16]),
       reads=[STG[0], CONST], writes=[PP[5]], tag="tp_gq")
    op("vector", lambda e: e.tensor_copy(gq_pc[:, :], pp[5][:, 0:16]),
       reads=[PP[5]], writes=[GQ], tag="ev_gq")
    op("tensor", lambda e: e.transpose(
        pp[5][:, 0:16], stage32[0:16, 1152:1280], idt32[0:16, 0:16]),
       reads=[STG[0], CONST], writes=[PP[5]], tag="tp_gkv")
    op("vector", lambda e: e.tensor_copy(gkv_pc[:, :], pp[5][:, 0:16]),
       reads=[PP[5]], writes=[GKV], tag="ev_gkv")
    op("sync", lambda e: e.dma_start(out=sv[0:1, 0:1], in_=ap["s1"]),
       appends=[SVB], tag="ld_s1", dma=True)
    op("sync", lambda e: e.dma_start(out=sv[0:1, 1:2], in_=ap["s2"]),
       appends=[SVB], tag="ld_s2", dma=True)
    op("sync", lambda e: e.dma_start(
        out=stage32[:, 0:NC * QR].rearrange("p (c q) -> p c q", q=QR),
        in_=ap["wqa"].rearrange("(c p) q -> c p q", p=128).rearrange(
            "c p q -> p c q")),
       writes=[STG[0]], tag="ld_wqa", dma=True)
    op("sync", lambda e: e.dma_start(out=stage32[0:64, 2048:2048 + E],
                                     in_=ap["wqb"]),
       writes=[STG[1]], tag="ld_wqb", dma=True)
    op("gpsimd", lambda e: e.dma_start(out=wkvb_b[:, :], in_=ap["wkvb"]),
       writes=[WKVB], tag="ld_wkvb", dma=True)
    for j in range(4):
        op("gpsimd", lambda e, j=j: e.dma_start(
            out=wb_kcw[:, 4 * j:4 * (j + 1), :],
            in_=wkva3[4 * j:4 * (j + 1), :, 0:KR].rearrange("c p q -> p c q")),
           writes=[KCWS[j]], tag=f"ld_kcw{j}", dma=True)

    for c in range(NC):
        op("vector", lambda e, c=c: e.tensor_scalar(
            wqa_b[:, c * QR:(c + 1) * QR], stage32[:, c * QR:(c + 1) * QR],
            gq_pc[:, c:c + 1], None, ALU.mult),
           reads=[STG[0], GQ], appends=[WQA], tag=f"f_wqa{c}")
    op("vector", lambda e: e.tensor_scalar(
        wqb_b[:, :], stage32[0:64, 2048:2048 + E], ISD, None, ALU.mult),
       reads=[STG[1]], writes=[WQB], tag="f_wqb")

    def bcast(src_slice, col, reads, tag, bank=5):
        op("tensor", lambda e, s=src_slice, c=col, bank=bank: e.matmul(
            pp[bank][:, c:c + 1], one32[:, :], s, start=True, stop=True),
           reads=[CONST] + reads, writes=[PP[bank]], tag=f"bcm_{tag}")
        op("vector", lambda e, c=col, bank=bank: e.tensor_copy(
            scal[:, c:c + 1], pp[bank][:, c:c + 1]),
           reads=[PP[bank]], appends=[SCAL], tag=f"bcc_{tag}")

    bcast(sv[0:1, 0:1], 0, [SVB], "s1")
    bcast(sv[0:1, 1:2], 1, [SVB], "s2")
    op("vector", lambda e: e.reciprocal(scal[:, 2:3], scal[:, 0:1]),
       reads=[SCAL], appends=[SCAL], tag="rs1")
    op("vector", lambda e: e.reciprocal(scal[:, 3:4], scal[:, 1:2]),
       reads=[SCAL], appends=[SCAL], tag="rs2")

    def ln_stats(rbuf_ap, rtoks, tag, dump_ap=None, dump_bufs=None):
        op("vector", lambda e: e.tensor_scalar(
            dump_ap, rbuf_ap, 1.0, None, ALU.mult, ALU.add,
            accum_out=stats[:, 0:1]),
           reads=rtoks, writes=[STATS] + dump_bufs, tag=f"{tag}_sum")
        op("vector", lambda e: e.scalar_tensor_tensor(
            dump_ap, rbuf_ap, 1.0, rbuf_ap, ALU.mult, ALU.mult,
            accum_out=stats[:, 1:2]),
           reads=rtoks, writes=dump_bufs, appends=[STATS], tag=f"{tag}_ssq")
        op("vector", lambda e: e.tensor_scalar(stats[:, 2:3], stats[:, 0:1],
                                               1.0 / E, None, ALU.mult),
           reads=[STATS], appends=[STATS], tag=f"{tag}_mu")
        op("vector", lambda e: e.tensor_scalar(stats[:, 3:4], stats[:, 2:3],
                                               stats[:, 2:3], EPS,
                                               ALU.mult, ALU.subtract),
           reads=[STATS], appends=[STATS], tag=f"{tag}_mu2")
        op("vector", lambda e: e.tensor_scalar(stats[:, 4:5], stats[:, 1:2],
                                               1.0 / E, stats[:, 3:4],
                                               ALU.mult, ALU.subtract),
           reads=[STATS], appends=[STATS], tag=f"{tag}_var")
        op("scalar", lambda e: e.activation(stats[:, 5:6], stats[:, 4:5],
                                            AF.Sqrt),
           reads=[STATS], appends=[STATS], tag=f"{tag}_sig")
        op("vector", lambda e: e.reciprocal(stats[:, 6:7], stats[:, 5:6]),
           reads=[STATS], appends=[STATS], tag=f"{tag}_rsig")

    op("sync", lambda e: e.dma_start(out=stage32[:, 0:2048], in_=xr3[0]),
       writes=[STG[0]], tag="ldx0", dma=True)

    for ts in range(NS):
        for tbi in range(4):
            tb = ts * 4 + tbi
            par = tb % 2
            xt = stage32[:, par * 2048:(par + 1) * 2048]
            ln_stats(xt, [STG[par]], f"A{tb}", xh2[:, :], [XHB2])
            op("vector", lambda e, xt=xt: e.tensor_scalar(
                xh[:, :], xt, stats[:, 2:3], stats[:, 6:7],
                ALU.subtract, ALU.mult),
               reads=[STG[par], STATS], writes=[XHB], tag=f"A{tb}_xh")
            if tb + 1 < NT:
                npar = (tb + 1) % 2
                op("sync", lambda e, npar=npar, tb=tb: e.dma_start(
                    out=stage32[:, npar * 2048:(npar + 1) * 2048],
                    in_=xr3[tb + 1]),
                   writes=[STG[npar]], tag=f"ldx{tb + 1}", dma=True)
            for gch in range(4):
                for j in range(4):
                    k = gch * 4 + j
                    op("tensor", lambda e, k=k, j=j: e.transpose(
                        ppbt[0][:, j * 128:(j + 1) * 128],
                        xh[:, k * 128:(k + 1) * 128], idt[:, :]),
                       reads=[XHB, CONST],
                       writes=[PPB[0]] if j == 0 else [],
                       appends=[PPB[0]] if j > 0 else [],
                       tag=f"A{tb}_tp{k}")
                op("scalar", lambda e, gch=gch, tbi=tbi: e.copy(
                    wb_xb[:, gch * 4:(gch + 1) * 4, tbi, :],
                    ppbt[0][:, :]),
                   reads=[PPB[0]], writes=[XB[gch][tbi]],
                   tag=f"A{tb}_xe{gch}")
        xb_all = [XB[g][b] for g in range(4) for b in range(4)]
        for k in range(NC):
            op("tensor", lambda e, k=k: e.matmul(
                pp[4][:, :], wbuf[:, KCW + k * KR:KCW + (k + 1) * KR],
                wbuf[:, k * 512:(k + 1) * 512],
                start=(k == 0), stop=(k == NC - 1)),
               reads=[KCWS[k // 4]] + XB[k // 4],
               writes=[PP[4]] if k == 0 else [],
               appends=[PP[4]] if k > 0 else [], tag=f"A{ts}_kc{k}")
        op("scalar", lambda e, ts=ts: e.copy(kct[:, ts * 512:(ts + 1) * 512],
                                             pp[4][:, :]),
           reads=[PP[4]], appends=[KCT], tag=f"A{ts}_kce")
        for k in range(NC):
            op("tensor", lambda e, k=k: e.matmul(
                pp[5][0:64, :], wqa_b[:, k * QR:(k + 1) * QR],
                wbuf[:, k * 512:(k + 1) * 512],
                start=(k == 0), stop=(k == NC - 1)),
               reads=[WQA] + XB[k // 4],
               writes=[PP[5]] if k == 0 else [],
               appends=[PP[5]] if k > 0 else [], tag=f"A{ts}_tt{k}")
        op("scalar", lambda e, ts=ts: e.copy(tT[:, ts * 512:(ts + 1) * 512],
                                             pp[5][0:64, :]),
           reads=[PP[5]], appends=[TTB], tag=f"A{ts}_tte")
        for vcb in range(4):
            wv = vcb % 2
            wv_off = WV0 if wv == 0 else WV1
            wvch = 4 + 4 * wv
            for j in range(4):
                op("gpsimd", lambda e, vcb=vcb, wv=wv, j=j: e.dma_start(
                    out=wb_wv[wv][:, 4 * j:4 * (j + 1), :],
                    in_=wkva3[4 * j:4 * (j + 1), :,
                              KR + vcb * 512:KR + (vcb + 1) * 512]
                    .rearrange("c p q -> p c q")),
                   writes=[WCH[wvch + j]], tag=f"A{ts}_wv{vcb}_{j}", dma=True)
            for k in range(NC):
                op("vector", lambda e, k=k, wv_off=wv_off: e.tensor_scalar(
                    wbuf[:, wv_off + k * 512:wv_off + (k + 1) * 512],
                    wbuf[:, wv_off + k * 512:wv_off + (k + 1) * 512],
                    gkv_pc[:, k:k + 1], None, ALU.mult),
                   reads=[WCH[wvch + k // 4], GKV],
                   appends=[WCH[wvch + k // 4]],
                   tag=f"A{ts}_gf{vcb}_{k}")
            for k in range(NC):
                for tbi in range(4):
                    op("tensor",
                       lambda e, k=k, tbi=tbi, wv_off=wv_off: e.matmul(
                           pp[tbi][:, :],
                           wbuf[:, k * 512 + tbi * 128:k * 512 + tbi * 128 + 128],
                           wbuf[:, wv_off + k * 512:wv_off + (k + 1) * 512],
                           start=(k == 0), stop=(k == NC - 1)),
                       reads=[XB[k // 4][tbi], WCH[wvch + k // 4]],
                       writes=[PP[tbi]] if k == 0 else [],
                       appends=[PP[tbi]] if k > 0 else [],
                       tag=f"A{ts}_v{vcb}_{k}_{tbi}")
            for tbi in range(4):
                tb = ts * 4 + tbi
                op("scalar", lambda e, tb=tb, vcb=vcb, tbi=tbi: e.copy(
                    arena[:, tb * 2048 + vcb * 512:tb * 2048 + (vcb + 1) * 512],
                    pp[tbi][:, :]),
                   reads=[PP[tbi]], appends=[ARENA, ACH[tb]],
                   tag=f"A{tb}_ve{vcb}")

    for c in range(NC):
        extra = []
        if c < 4:
            extra = XB[c][:]
        if c == 12:
            extra = KCWS[:]
        if c < 8:
            extra = extra + W1S[c][:]
        op("gpsimd", lambda e, c=c: e.dma_start(
            out=wbuf[:, c * 2048:(c + 1) * 2048], in_=wo3[c]),
           writes=[WCH[c]] + extra, tag=f"ld_wo{c}", dma=True)

    w1_jobs = [("mdma", 0), ("mdma", 1)]
    for i in range(62):
        w1_jobs += [("mred", i), ("mdma", i + 2)]
    w1_jobs += [("mred", 62), ("mred", 63), ("fin", 0)]
    w1_jobs += [("qdma", 0), ("qdma", 1), ("qa", 0)]
    for i in range(63):
        w1_jobs.append(("qb", i))
        if i + 2 < 64:
            w1_jobs.append(("qdma", i + 2))
        w1_jobs.append(("qa", i + 1))
    w1_jobs.append(("qb", 63))
    w1_state = {"i": 0}

    def w1_step():
        if w1_state["i"] >= len(w1_jobs):
            return
        kind, i = w1_jobs[w1_state["i"]]
        w1_state["i"] += 1
        par = i % 2
        sl = slice(par * 2048, par * 2048 + 2048)
        if kind in ("mdma", "qdma"):
            op("sync", lambda e, i=i, sl=sl: e.dma_start(
                out=stage32[:, sl],
                in_=w1r[i // 4, :, (i % 4) * 2048:(i % 4 + 1) * 2048]),
               writes=[STG[par]], tag=f"w1l_{kind}{i}", dma=True)
        elif kind == "mred":
            op("vector", lambda e, i=i, sl=sl: e.tensor_reduce(
                wmx[:, i:i + 1], stage32[:, sl], AX.X, ALU.max,
                apply_absolute_value=True),
               reads=[STG[par]], appends=[WMX], tag=f"w1m{i}")
        elif kind == "fin":
            op("vector", lambda e: e.tensor_reduce(
                wmx[:, 64:65], wmx[:, 0:64], AX.X, ALU.max),
               reads=[WMX], appends=[WMX], tag="w1fin")
            op("gpsimd", lambda e: e.tensor_reduce(
                sv[0:1, 4:5], wmx[:, 64:65], AX.C, ALU.max),
               reads=[WMX], appends=[SVB], tag="w1fing")
            bcast(sv[0:1, 4:5], 4, [SVB], "w1max", bank=2)
            op("vector", lambda e: e.tensor_scalar(
                scal[:, 4:5], scal[:, 4:5], 1.0 / 7.0, None, ALU.mult),
               reads=[SCAL], appends=[SCAL], tag="sw1")
            op("vector", lambda e: e.tensor_tensor(
                scal[:, 5:6], scal[:, 0:1], scal[:, 4:5], ALU.mult),
               reads=[SCAL], appends=[SCAL], tag="a1")
            op("vector", lambda e: e.reciprocal(scal[:, 6:7], scal[:, 4:5]),
               reads=[SCAL], appends=[SCAL], tag="rsw1")
        elif kind == "qa":
            op("gpsimd", lambda e, sl=sl: e.tensor_scalar(
                stage32[:, sl], stage32[:, sl], scal[:, 6:7], MAGIC,
                ALU.mult, ALU.add),
               reads=[STG[par], SCAL], appends=[STG[par]], tag=f"w1q1_{i}")
        else:
            dch, q4 = i // 4, i % 4
            for st_ in range(2):
                op("vector", lambda e, sl=sl, st_=st_: e.tensor_scalar(
                    i32t[:, :],
                    stage32[:, sl.start + st_ * 1024:
                            sl.start + (st_ + 1) * 1024],
                    MAGIC, None, ALU.subtract),
                   reads=[STG[par]], writes=[I32H[0], I32H[1]],
                   tag=f"w1q2_{i}_{st_}")
                op("gpsimd", lambda e, st_=st_: e.tensor_scalar(
                    xh[:, st_ * 1024:(st_ + 1) * 1024], i32t[:, :],
                    7.0, -8.0, ALU.min, ALU.max),
                   reads=[I32H[0], I32H[1]], writes=[XHB] if st_ == 0 else [],
                   appends=[XHB] if st_ > 0 else [], tag=f"w1q3_{i}_{st_}")
            for jj in range(2):
                op("gpsimd", lambda e, dch=dch, q4=q4, jj=jj: e.dma_start(
                    out=w1i4[q4 * 16 + 8 * jj:q4 * 16 + 8 * (jj + 1), :, dch, :]
                    .rearrange("c p q -> p c q"),
                    in_=xh[:, 1024 * jj:1024 * (jj + 1)]
                    .rearrange("p (c q) -> p c q", q=128)),
                   reads=[XHB], appends=[SCR["w1i"]], tag=f"w1qs_{i}_{jj}",
                   dma=True)

    def qk_proj(h):
        Qh, Kh = QhA[h % 2], KhA[h % 2]
        QHB, KHB = QHBA[h % 2], KHBA[h % 2]
        for q4 in range(4):
            op("tensor", lambda e, h=h, q4=q4: e.matmul(
                pp[q4][:, :], wqb_b[:, h * 128:(h + 1) * 128],
                tT[:, q4 * 512:(q4 + 1) * 512], start=True, stop=True),
               reads=[WQB, TTB], writes=[PP[q4]], tag=f"B{h}q{q4}")
        for q4 in range(4):
            op("scalar", lambda e, q4=q4, Qh=Qh: e.copy(
                Qh[:, q4 * 512:(q4 + 1) * 512], pp[q4][:, :]),
               reads=[PP[q4]], writes=[QHB[q4]], tag=f"B{h}qe{q4}")
        for q4 in range(4):
            op("tensor", lambda e, h=h, q4=q4: e.matmul(
                pp[q4][:, :], wkvb_b[:, h * 128:(h + 1) * 128],
                kct[:, q4 * 512:(q4 + 1) * 512], start=True, stop=True),
               reads=[WKVB, KCT], writes=[PP[q4]], tag=f"B{h}k{q4}")
        for q4 in range(4):
            op("scalar", lambda e, q4=q4, Kh=Kh: e.copy(
                Kh[:, q4 * 512:(q4 + 1) * 512], pp[q4][:, :]),
               reads=[PP[q4]], writes=[KHB[q4]], tag=f"B{h}ke{q4}")
    def qk_rope(h):
        Qh, Kh = QhA[h % 2], KhA[h % 2]
        QHB, KHB = QHBA[h % 2], KHBA[h % 2]
        for tname, tbuf, TB in (("q", Qh, QHB), ("k", Kh, KHB)):
            for sp in range(4):
                sl = slice(sp * 512, (sp + 1) * 512)
                bk = sp % 4
                pslot = sp % 4
                op("tensor", lambda e, tbuf=tbuf, sl=sl, bk=bk: e.matmul(
                    pp[bk][0:64, :], stm[:, :], tbuf[0:64, sl],
                    start=True, stop=True),
                   reads=[TB[sp], CONST], writes=[PP[bk]],
                   tag=f"B{h}rr{tname}{sp}")
                op("vector", lambda e, sl=sl, bk=bk, pslot=pslot: e.tensor_tensor(
                    mstg[0:64, (pslot % 2) * 512:(pslot % 2 + 1) * 512],
                    pp[bk][0:64, :], rtab[64:64 + 64, sl], ALU.mult),
                   reads=[PP[bk], CONST], writes=[MSTG[pslot % 2]],
                   tag=f"B{h}r1{tname}{sp}")
                op("vector", lambda e, tbuf=tbuf, sl=sl, pslot=pslot:
                   e.scalar_tensor_tensor(
                       xh2[0:64, (pslot % 4) * 512:(pslot % 4 + 1) * 512],
                       tbuf[0:64, sl], 1.0, rtab[0:64, sl], ALU.mult, ALU.mult),
                   reads=[TB[sp], CONST], writes=[XHB2] if sp == 0 and
                   tname == "q" else [],
                   appends=[] if sp == 0 and tname == "q" else [XHB2],
                   tag=f"B{h}r2{tname}{sp}")
                op("vector", lambda e, tbuf=tbuf, sl=sl, pslot=pslot:
                   e.tensor_tensor(
                       tbuf[0:64, sl],
                       mstg[0:64, (pslot % 2) * 512:(pslot % 2 + 1) * 512],
                       xh2[0:64, (pslot % 4) * 512:(pslot % 4 + 1) * 512],
                       ALU.add),
                   reads=[MSTG[pslot % 2], XHB2], appends=[TB[sp]],
                   tag=f"B{h}r3{tname}{sp}")

    def attention_head(h, rope_cb=None, bg_quota=0):
        Qh, Kh = QhA[h % 2], KhA[h % 2]
        QHB, KHB = QHBA[h % 2], KHBA[h % 2]
        st_ctx = {"pend": [], "rope": rope_cb, "boff": 0, "gctr": 0,
                  "bgdone": 0}

        def prc(qt, qs, kt):
            return ((qt % 2) * 4 + kt // 4) if qs > 0 else qt

        def s1a(qt, qti, qs, h=h):
            kext = (qt + 1) * 128
            nkb = (kext + 511) // 512
            sl_ = qt % 4
            banks = [(st_ctx["boff"] + i) % 4 for i in range(nkb)]
            st_ctx["boff"] += nkb
            for kb in range(nkb):
                n = min(512, kext - kb * 512)
                b = banks[kb]
                op("tensor", lambda e, qt=qt, kb=kb, n=n, b=b: e.matmul(
                    pp[b][:, 0:n], Qh[:, qt * 128:(qt + 1) * 128],
                    Kh[:, kb * 512:kb * 512 + n], start=True, stop=True),
                   reads=[QHB[qt // 4], KHB[kb]], writes=[PP[b]],
                   tag=f"B{h}s1_{qt}_{kb}")
            lb = banks[nkb - 1]
            ln_ = min(512, kext - (nkb - 1) * 512)
            op("vector", lambda e, lb=lb, ln_=ln_: e.tensor_tensor(
                pp[lb][:, ln_ - 128:ln_], pp[lb][:, ln_ - 128:ln_],
                amsk[:, :], ALU.add),
               reads=[CONST], writes=[PP[lb]], tag=f"B{h}msk_{qt}")
            for kb in range(nkb):
                n = min(512, kext - kb * 512)
                b = banks[kb]
                if nkb == 1:
                    op("vector", lambda e, b=b, n=n, qt=qt: e.tensor_reduce(
                        mneg[:, qt:qt + 1], pp[b][:, 0:n], AX.X, ALU.max,
                        negate=True),
                       reads=[PP[b]], writes=[MNEG[qt]], tag=f"B{h}s1r_{qt}")
                else:
                    op("vector", lambda e, b=b, n=n, sl_=sl_, kb=kb:
                       e.tensor_reduce(
                        mneg[:, 16 + sl_ * 4 + kb:17 + sl_ * 4 + kb],
                        pp[b][:, 0:n], AX.X, ALU.max, negate=True),
                       reads=[PP[b]], writes=[MNEGP[sl_][kb]],
                       tag=f"B{h}s1r_{qt}_{kb}")
            if nkb > 1:
                op("vector", lambda e, qt=qt, nkb=nkb, sl_=sl_:
                   e.tensor_reduce(
                    mneg[:, qt:qt + 1],
                    mneg[:, 16 + sl_ * 4:16 + sl_ * 4 + nkb], AX.X, ALU.min),
                   reads=MNEGP[sl_][0:nkb], writes=[MNEG[qt]],
                   tag=f"B{h}s1c_{qt}")
            for kb in range(nkb):
                n = min(512, kext - kb * 512)
                b = banks[kb]
                c = prc(qt, qs, kb * 4)
                op("scalar", lambda e, b=b, n=n, qt=qt, c=c, sl_=sl_, kb=kb:
                   e.activation(
                    Pr[:, c * 512:c * 512 + n], pp[b][:, 0:n], AF.Exp,
                    bias=mneg[:, qt:qt + 1],
                    accum_out=stats[:, 8 + sl_ * 4 + kb:9 + sl_ * 4 + kb]),
                   reads=[PP[b], MNEG[qt]],
                   writes=[PRB[c], LPB[sl_]] if kb == 0 else [PRB[c]],
                   appends=[] if kb == 0 else [LPB[sl_]],
                   tag=f"B{h}ex_{qt}_{kb}")

        def s1b(qt, h=h):
            nkb = ((qt + 1) * 128 + 511) // 512
            sl_ = qt % 4
            op("vector", lambda e, nkb=nkb, sl_=sl_: e.tensor_reduce(
                stats[:, 24 + sl_:25 + sl_],
                stats[:, 8 + sl_ * 4:8 + sl_ * 4 + nkb], AX.X, ALU.add),
               reads=[LPB[sl_]], writes=[LB_[sl_]], tag=f"B{h}l_{qt}")
            op("vector", lambda e, sl_=sl_: e.reciprocal(
                stats[:, 28 + sl_:29 + sl_], stats[:, 24 + sl_:25 + sl_]),
               reads=[LB_[sl_]], writes=[RLQ[sl_]], tag=f"B{h}rl_{qt}")
            op("vector", lambda e, sl_=sl_: e.tensor_scalar(
                dgt[:, sl_ * 128:(sl_ + 1) * 128], idt[:, :],
                stats[:, 28 + sl_:29 + sl_], None, ALU.mult),
               reads=[RLQ[sl_], CONST], writes=[DGB[sl_]], tag=f"B{h}dg_{qt}")

        def tav_part(qt, qtl, qs, h=h):
            nkt = (qt + 1)
            sl_ = qt % 4

            def t_g(g, par2, cnt):
                pt_ps = pp[5] if par2 == 0 else ppf
                PT_B = PP[5] if par2 == 0 else PPF
                for j in range(cnt):
                    kt = g * 4 + j
                    c = prc(qt, qs, kt)
                    off = c * 512 + (kt % 4) * 128
                    op("tensor", lambda e, j=j, pt_ps=pt_ps, off=off, sl_=sl_,
                       c=c: e.matmul(
                           pt_ps[:, j * 128:(j + 1) * 128],
                           Pr[:, off:off + 128],
                           dgt[:, sl_ * 128:(sl_ + 1) * 128],
                           start=True, stop=True),
                       reads=[PRB[c], DGB[sl_], CONST],
                       writes=[PT_B] if j == 0 else [],
                       appends=[PT_B] if j > 0 else [],
                       tag=f"B{h}tp_{qt}_{kt}")
                op("vector", lambda e, cnt=cnt, par2=par2, pt_ps=pt_ps:
                   e.tensor_copy(
                    PTt[:, par2 * 512:par2 * 512 + cnt * 128],
                    pt_ps[:, 0:cnt * 128]),
                   reads=[PT_B], writes=[PTRB[par2]],
                   tag=f"B{h}pte_{qt}_{g}")

            def av_g(g, par2, cnt):
                for j in range(cnt):
                    kt = g * 4 + j
                    op("tensor", lambda e, kt=kt, j=j, qtl=qtl, par2=par2,
                       nkt=nkt: e.matmul(
                           pp[4][:, qtl * 128:(qtl + 1) * 128],
                           arena[:, kt * 2048 + h * 128:
                                 kt * 2048 + (h + 1) * 128],
                           PTt[:, par2 * 512 + j * 128:
                               par2 * 512 + (j + 1) * 128],
                           start=(kt == 0), stop=(kt == nkt - 1)),
                       reads=[ACH[kt], PTRB[par2]],
                       writes=[PP[4]] if kt == 0 and qtl == 0 else [],
                       appends=[] if kt == 0 and qtl == 0 else [PP[4]],
                       tag=f"B{h}av_{qt}_{kt}")

            prev = None
            for g in range((nkt + 3) // 4):
                par2 = st_ctx["gctr"] % 2
                st_ctx["gctr"] += 1
                cnt = min(4, nkt - g * 4)
                t_g(g, par2, cnt)
                if prev is not None:
                    av_g(*prev)
                prev = (g, par2, cnt)
            av_g(*prev)

        def y_evac(qs, h=h):
            op("scalar", lambda e, h=h, qs=qs: e.copy(
                ar_y[:, qs * 4:qs * 4 + 4, h, :], pp[4][:, :]),
               reads=[PP[4]],
               writes=[ACH[qs * 4 + i] for i in range(4)],
               appends=[ARENA], tag=f"B{h}ye{qs}")

        point = 0
        for qs in range(NS - 1, -1, -1):
            for qti in range(4):
                qt = qs * 4 + qti
                s1a(qt, qti, qs)
                if st_ctx["rope"] is not None and qti == 3 and qs == NS - 1:
                    cb = st_ctx["rope"]
                    st_ctx["rope"] = None
                    cb()
                for cb in st_ctx["pend"]:
                    cb()
                st_ctx["pend"] = []
                s1b(qt)
                st_ctx["pend"].append(
                    lambda qt=qt, qti=qti, qs=qs: tav_part(qt, qti, qs))
                if qti == 3:
                    st_ctx["pend"].append(lambda qs=qs: y_evac(qs))
                point += 1
                tgt = bg_quota * point // 16
                while st_ctx["bgdone"] < tgt:
                    w1_step()
                    st_ctx["bgdone"] += 1
        for cb in st_ctx["pend"]:
            cb()

    qk_proj(0)
    qk_rope(0)
    bgq = (len(w1_jobs) + H - 1) // H
    for h in range(H):
        if h + 1 < H:
            qk_proj(h + 1)
            attention_head(h, rope_cb=lambda hh=h + 1: qk_rope(hh),
                           bg_quota=bgq)
        else:
            attention_head(h, bg_quota=bgq)
    while w1_state["i"] < len(w1_jobs):
        w1_step()

    def c_transposes(tb):
        xh_t = xh if tb % 2 == 0 else xh2
        XH_T = XHB if tb % 2 == 0 else XHB2
        for gch in range(4):
            half = gch % 2
            tp_ps = pp[2] if half == 0 else pp[3]
            TP_B = PP[2] if half == 0 else PP[3]
            for j in range(4):
                k = gch * 4 + j
                op("tensor", lambda e, k=k, j=j, xh_t=xh_t, tp_ps=tp_ps:
                   e.matmul(
                    tp_ps[:, j * 128:(j + 1) * 128],
                    xh_t[:, k * 128:(k + 1) * 128], idt[:, :],
                    start=True, stop=True),
                   reads=[XH_T, CONST],
                   writes=[TP_B] if j == 0 else [],
                   appends=[TP_B] if j > 0 else [], tag=f"C{tb}tp{k}")
            op("vector", lambda e, half=half, tp_ps=tp_ps: e.tensor_copy(
                mstg[:, half * 512:(half + 1) * 512], tp_ps[:, :]),
               reads=[TP_B], writes=[MSTG[half]], tag=f"C{tb}te{gch}")
            op("sync", lambda e, tb=tb, gch=gch, half=half: e.dma_start(
                out=mts4[tb // 4, :, gch * 4:(gch + 1) * 4,
                         (tb % 4) * 128:(tb % 4 + 1) * 128],
                in_=mstg[:, half * 512:(half + 1) * 512]
                .rearrange("p (c q) -> p c q", q=128)),
               reads=[MSTG[half]], appends=[SCR["mts"]], tag=f"C{tb}ts{gch}",
               dma=True)

    for tb in range(NT):
        r = stage32[:, 2048:4096]
        op("sync", lambda e, tb=tb: e.dma_start(out=stage32[:, 0:2048],
                                                in_=xr3[tb]),
           writes=[STG[0]], tag=f"C{tb}x", dma=True)
        for ecb in range(4):
            par = ecb % 2
            for yc in range(NC):
                op("tensor", lambda e, tb=tb, ecb=ecb, yc=yc, par=par: e.matmul(
                    pp[par][:, :],
                    arena[:, tb * 2048 + yc * 128:tb * 2048 + (yc + 1) * 128],
                    wbuf[:, yc * 2048 + ecb * 512:yc * 2048 + (ecb + 1) * 512],
                    start=(yc == 0), stop=(yc == NC - 1)),
                   reads=[ACH[tb], WCH[yc]],
                   writes=[PP[par]] if yc == 0 else [],
                   appends=[PP[par]] if yc > 0 else [],
                   tag=f"C{tb}wo{ecb}_{yc}")
            op("vector", lambda e, ecb=ecb, par=par, r=r: e.tensor_tensor(
                r[:, ecb * 512:(ecb + 1) * 512], pp[par][:, :],
                stage32[:, ecb * 512:(ecb + 1) * 512], ALU.add),
               reads=[PP[par], STG[0]],
               writes=([STG[1], RLBB, RLR] if tb == 0 else [STG[1]])
               if ecb == 0 else [],
               appends=[STG[1]] if ecb > 0 else [], tag=f"C{tb}re{ecb}")
        ln_stats(r, [STG[1]], f"C{tb}", QhA[0][:, :], QHBA[0])
        op("vector", lambda e, r=r: e.tensor_scalar(
            r, r, stats[:, 2:3], stats[:, 6:7], ALU.subtract, ALU.mult),
           reads=[STG[1], STATS], writes=[STG[1]], tag=f"C{tb}x2")
        op("sync", lambda e, tb=tb, r=r: e.dma_start(out=x2s3[tb], in_=r),
           reads=[STG[1]], appends=[SCR["x2s"]], tag=f"C{tb}x2s", dma=True)
        op("vector", lambda e, r=r: e.tensor_scalar(
            r, r, scal[:, 2:3], MAGIC, ALU.mult, ALU.add),
           reads=[STG[1], SCAL], writes=[STG[1]], tag=f"C{tb}mv")
        xh_t = xh if tb % 2 == 0 else xh2
        XH_T = XHB if tb % 2 == 0 else XHB2
        for st_ in range(2):
            op("vector", lambda e, r=r, st_=st_: e.tensor_scalar(
                i32t[:, :], r[:, st_ * 1024:(st_ + 1) * 1024],
                MAGIC, None, ALU.subtract),
               reads=[STG[1]], writes=[I32H[0], I32H[1]], tag=f"C{tb}mi{st_}")
            op("vector", lambda e, xh_t=xh_t, st_=st_: e.tensor_scalar(
                xh_t[:, st_ * 1024:(st_ + 1) * 1024], i32t[:, :],
                7.0, -8.0, ALU.min, ALU.max),
               reads=[I32H[0], I32H[1]], writes=[XH_T] if st_ == 0 else [],
               appends=[XH_T] if st_ > 0 else [], tag=f"C{tb}mc{st_}")
        if tb >= 1:
            c_transposes(tb - 1)
    c_transposes(NT - 1)

    op("sync", lambda e: e.dma_start(
        out=stage32[0:64, 0:128],
        in_=ap["b1"].rearrange("a (c p) -> (a c) p", p=128)),
       writes=[STG[0]], tag="ld_b1", dma=True)
    op("tensor", lambda e: e.transpose(
        pp[5][:, 0:64], stage32[0:64, 0:128], idt32[0:64, 0:64]),
       reads=[STG[0], CONST], writes=[PP[5]], tag="tp_b1")
    op("vector", lambda e: e.tensor_copy(b1qt[:, :], pp[5][:, 0:64]),
       reads=[PP[5], SCR["mts"]], writes=[B1Q], tag="ev_b1")
    op("vector", lambda e: e.reciprocal(scal[:, 7:8], scal[:, 5:6]),
       reads=[SCAL], appends=[SCAL], tag="ra1")
    op("vector", lambda e: e.tensor_scalar(
        b1qt[:, :], b1qt[:, :], scal[:, 7:8], MAGIC, ALU.mult, ALU.add),
       reads=[B1Q, SCAL], writes=[B1Q], tag="b1q1")
    op("vector", lambda e: e.tensor_scalar(
        i32t[:, 0:64], b1qt[:, :], MAGIC, None, ALU.subtract),
       reads=[B1Q], writes=[I32H[0]], tag="b1q2")
    op("vector", lambda e: e.tensor_scalar(
        b1qt[:, :], i32t[:, 0:64], CLIP32, -CLIP32, ALU.min, ALU.max),
       reads=[I32H[0]], writes=[B1Q], tag="b1q3")
    op("vector", lambda e: e.tensor_scalar(
        b1qt[:, :], b1qt[:, :], scal[:, 5:6], None, ALU.mult),
       reads=[B1Q, SCAL], writes=[B1Q], tag="b1q4")

    w2_jobs = [("mdma", i) for i in range(4)]
    for i in range(60):
        w2_jobs += [("mred", i), ("mdma", i + 4)]
    w2_jobs += [("mred", i) for i in range(60, 64)]
    w2_jobs.append(("fin", 0))
    w2_jobs += [("qdma", i) for i in range(4)]
    for i in range(64):
        w2_jobs += [("qa", i), ("qb", i)]
        if i + 4 < 64:
            w2_jobs.append(("qdma", i + 4))
    w2_state = {"i": 0}

    def w2_step():
        if w2_state["i"] >= len(w2_jobs):
            return
        kind, i = w2_jobs[w2_state["i"]]
        w2_state["i"] += 1
        slt = i % 4
        so = slt * 2048
        if kind == "mdma":
            op("sync", lambda e, i=i, so=so: e.dma_start(
                out=r2sup[:, so:so + 2048], in_=w2r[i]),
               reads=[SCR["mts"]] if i < 4 else [],
               writes=[W2ST[slt]], tag=f"w2l{i}", dma=True)
        elif kind == "mred":
            op("vector", lambda e, i=i, so=so: e.tensor_reduce(
                wmx[:, i:i + 1], r2sup[:, so:so + 2048], AX.X, ALU.max,
                apply_absolute_value=True),
               reads=[W2ST[slt]], writes=[WMX] if i == 0 else [],
               appends=[WMX] if i > 0 else [], tag=f"w2m{i}")
        elif kind == "fin":
            op("vector", lambda e: e.tensor_reduce(
                wmx[:, 65:66], wmx[:, 0:64], AX.X, ALU.max),
               reads=[WMX], appends=[WMX], tag="w2fin")
            op("gpsimd", lambda e: e.tensor_reduce(
                sv[0:1, 8:9], wmx[:, 65:66], AX.C, ALU.max),
               reads=[WMX], appends=[SVB], tag="w2fing")
            bcast(sv[0:1, 8:9], 8, [SVB], "w2max")
            op("vector", lambda e: e.tensor_scalar(
                scal[:, 8:9], scal[:, 8:9], 1.0 / 7.0, None, ALU.mult),
               reads=[SCAL], appends=[SCAL], tag="sw2")
            op("vector", lambda e: e.tensor_tensor(
                scal[:, 9:10], scal[:, 1:2], scal[:, 8:9], ALU.mult),
               reads=[SCAL], appends=[SCAL], tag="b2s")
            op("vector", lambda e: e.reciprocal(scal[:, 10:11], scal[:, 8:9]),
               reads=[SCAL], appends=[SCAL], tag="rsw2")
        elif kind == "qdma":
            op("gpsimd", lambda e, i=i, so=so: e.dma_start(
                out=r2sup[:, so:so + 2048], in_=w2r[i]),
               reads=[SCR["mts"]] if i < 4 else [],
               writes=[W2ST[slt]], tag=f"w2ql{i}", dma=True)
        elif kind == "qa":
            op("vector", lambda e, so=so: e.tensor_scalar(
                r2sup[:, so:so + 2048], r2sup[:, so:so + 2048],
                scal[:, 10:11], MAGIC, ALU.mult, ALU.add),
               reads=[W2ST[slt], SCAL], appends=[W2ST[slt]], tag=f"w2q1_{i}")
        else:
            for st_ in range(2):
                op("vector", lambda e, st_=st_, so=so: e.tensor_scalar(
                    i32t[:, :],
                    r2sup[:, so + st_ * 1024:so + (st_ + 1) * 1024],
                    MAGIC, None, ALU.subtract),
                   reads=[W2ST[slt]], writes=[I32H[0], I32H[1]],
                   tag=f"w2q2_{i}_{st_}")
                op("vector", lambda e, st_=st_: e.tensor_scalar(
                    xh[:, st_ * 1024:(st_ + 1) * 1024], i32t[:, :],
                    7.0, -8.0, ALU.min, ALU.max),
                   reads=[I32H[0], I32H[1]], writes=[XHB] if st_ == 0 else [],
                   appends=[XHB] if st_ > 0 else [], tag=f"w2q3_{i}_{st_}")
            op("gpsimd", lambda e, i=i: e.dma_start(
                out=w2i4[:, i, :, :].rearrange("b p q -> p b q"),
                in_=xh[:, :].rearrange("p (b q) -> p b q", q=512)),
               reads=[XHB], appends=[SCR["w2i"]], tag=f"w2qs_{i}", dma=True)

    GSZ = 8
    NG = NF // GSZ
    MT0 = 16384

    def mt_load(g, tsup):
        base = MT0 + (tsup % 2) * 8192
        wc = 8 + (tsup % 2) * 4
        op("sync", lambda e, tsup=tsup, base=base: e.dma_start(
            out=wbuf[:, base:base + 8192], in_=mts2[tsup]),
           reads=[SCR["mts"]], writes=[WCH[wc + j] for j in range(4)],
           tag=f"D{g}_{tsup}mtl", dma=True)

    def slab_load(g, fi, first=False):
        fch = g * GSZ + fi
        for jj in range(2):
            op("sync", lambda e, fch=fch, fi=fi, jj=jj: e.dma_start(
                out=wbuf[:, fi * 2048 + jj * 1024:
                         fi * 2048 + (jj + 1) * 1024],
                in_=w1i[fch * 128:(fch + 1) * 128,
                        jj * 1024:(jj + 1) * 1024]),
               reads=[SCR["w1i"]],
               writes=[W1S[fi][jj]] + ([WCH[fi]] if first else []),
               tag=f"D{g}_{fi}wl{jj}", dma=True)

    def d_nstore(tsup, fch):
        t1 = fch % 2
        op("gpsimd", lambda e, tsup=tsup, fch=fch, t1=t1: e.dma_start(
            out=ns4[tsup, fch], in_=mstg[:, t1 * 512:(t1 + 1) * 512]),
           reads=[MSTG[t1]], appends=[SCR["nscr"]], tag=f"D{tsup}_{fch}ns",
           dma=True)

    mt_load(0, 0)
    for fi in range(GSZ):
        slab_load(0, fi, first=True)
    d_it = 0
    for g in range(NG):
        for tsup in range(NS):
            if tsup + 1 < NS:
                mt_load(g, tsup + 1)
            elif g + 1 < NG:
                mt_load(g + 1, 0)
            mbase = MT0 + (tsup % 2) * 8192
            wc = 8 + (tsup % 2) * 4
            for fi in range(GSZ):
                fch = g * GSZ + fi
                par = fi % 2
                for dch in range(NC):
                    op("tensor", lambda e, dch=dch, fi=fi, par=par,
                       mbase=mbase: e.matmul(
                        pp[par][:, :],
                        wbuf[:, fi * 2048 + dch * 128:
                             fi * 2048 + (dch + 1) * 128],
                        wbuf[:, mbase + dch * 512:mbase + (dch + 1) * 512],
                        start=(dch == 0), stop=(dch == NC - 1)),
                       reads=[W1S[fi][dch // 8], WCH[wc + dch // 4]],
                       writes=[PP[par]] if dch == 0 else [],
                       appends=[PP[par]] if dch > 0 else [],
                       tag=f"D{g}_{tsup}_{fi}m{dch}")
                t1 = fi % 2
                t1sl = slice(2048 + t1 * 512, 2048 + (t1 + 1) * 512)
                wrt = [T1B[t1]] + ([STG[1]] if g == 0 and tsup == 0 and
                                   fi < 2 else [])
                op("scalar", lambda e, fch=fch, par=par, t1sl=t1sl:
                   e.activation(
                    stage32[:, t1sl], pp[par][:, :], AF.Relu,
                    bias=b1qt[:, fch:fch + 1], scale=scal[:, 5:6]),
                   reads=[PP[par], B1Q, SCAL], writes=wrt,
                   tag=f"D{g}_{tsup}_{fi}relu")
                op("vector", lambda e, t1sl=t1sl: e.tensor_scalar(
                    stage32[:, t1sl], stage32[:, t1sl], scal[:, 3:4], MAGIC,
                    ALU.mult, ALU.add),
                   reads=[T1B[t1], SCAL], appends=[T1B[t1]],
                   tag=f"D{g}_{tsup}_{fi}q1")
                op("vector", lambda e, t1=t1, t1sl=t1sl: e.tensor_scalar(
                    i32t[:, t1 * 512:(t1 + 1) * 512], stage32[:, t1sl], MAGIC,
                    None, ALU.subtract),
                   reads=[T1B[t1]], writes=[I32H[t1]],
                   tag=f"D{g}_{tsup}_{fi}q2")
                op("vector", lambda e, t1=t1: e.tensor_scalar(
                    mstg[:, t1 * 512:(t1 + 1) * 512],
                    i32t[:, t1 * 512:(t1 + 1) * 512], 15.0, 0.0,
                    ALU.min, ALU.max),
                   reads=[I32H[t1]], writes=[MSTG[t1]],
                   tag=f"D{g}_{tsup}_{fi}q3")
                if fi >= 1:
                    d_nstore(tsup, fch - 1)
                if g + 1 < NG and tsup == NS - 1:
                    slab_load(g + 1, fi)
                d_it += 1
                while w2_state["i"] < len(w2_jobs) * d_it // (NG * NS * GSZ):
                    w2_step()
            d_nstore(tsup, g * GSZ + GSZ - 1)
    while w2_state["i"] < len(w2_jobs):
        w2_step()

    op("sync", lambda e: e.dma_start(out=r2sup[0:1, 0:2048], in_=ap["b2"]),
       writes=[R2T[0], W2ST[0]], tag="ld_b2", dma=True)
    op("vector", lambda e: e.tensor_scalar(
        r2sup[0:1, 0:2048], r2sup[0:1, 0:2048], scal[0:1, 10:11], MAGIC,
        ALU.mult, ALU.add),
       reads=[R2T[0], SCAL], writes=[R2T[0]], tag="b2q1")
    for st_ in range(2):
        op("vector", lambda e, st_=st_: e.tensor_scalar(
            i32t[0:1, :], r2sup[0:1, st_ * 1024:(st_ + 1) * 1024],
            MAGIC, None, ALU.subtract),
           reads=[R2T[0]], writes=[I32H[0], I32H[1]], tag=f"b2q2_{st_}")
        op("vector", lambda e, st_=st_: e.tensor_scalar(
            r2sup[0:1, st_ * 1024:(st_ + 1) * 1024], i32t[0:1, :],
            CLIP32, -CLIP32, ALU.min, ALU.max),
           reads=[I32H[0], I32H[1]], writes=[R2T[0]], tag=f"b2q3_{st_}")
    op("vector", lambda e: e.tensor_scalar(
        r2sup[0:1, 0:2048], r2sup[0:1, 0:2048], scal[0:1, 9:10], None,
        ALU.mult),
       reads=[R2T[0], SCAL], writes=[R2T[0]], tag="b2q4")
    for q4 in range(4):
        op("tensor", lambda e, q4=q4: e.matmul(
            pp[5][:, :], one32[:, :], r2sup[0:1, q4 * 512:(q4 + 1) * 512],
            start=True, stop=True),
           reads=[CONST, R2T[0]], writes=[PP[5]], tag=f"b2b{q4}")
        op("vector", lambda e, q4=q4: e.tensor_copy(
            b2qb[:, q4 * 512:(q4 + 1) * 512], pp[5][:, :]),
           reads=[PP[5]], writes=[B2QB] if q4 == 0 else [],
           appends=[B2QB] if q4 > 0 else [], tag=f"b2be{q4}")

    def e_w2_load(tsup, ecb, c, first=False):
        extra = (W1S[c][:] if c < 8 else []) if first else []
        op("gpsimd", lambda e, ecb=ecb, c=c: e.dma_start(
            out=wb_w2[:, 4 * c:4 * (c + 1), :],
            in_=w2i4[ecb][4 * c:4 * (c + 1)].rearrange("f p q -> p f q")),
           reads=[SCR["w2i"]], writes=[WCH[c]] + extra,
           tag=f"E{tsup}_{ecb}wl{c}", dma=True)

    def e_n_load(tsup, c):
        op("gpsimd", lambda e, tsup=tsup, c=c: e.dma_start(
            out=ar_n[:, 4 * c:4 * (c + 1), :],
            in_=ns4[tsup][4 * c:4 * (c + 1)].rearrange("f p q -> p f q")),
           reads=[SCR["nscr"]], writes=[ACH[c]], tag=f"E{tsup}nl{c}", dma=True)

    for c in range(NT):
        e_n_load(0, c)
    for c in range(NC):
        e_w2_load(0, 0, c, first=True)
    for tsup in range(NS):
        for ecb in range(4):
            for fch in range(NF):
                for tbi in range(4):
                    op("tensor",
                       lambda e, tbi=tbi, fch=fch: e.matmul(
                           pp[tbi][:, :],
                           arena[:, fch * 512 + tbi * 128:
                                 fch * 512 + tbi * 128 + 128],
                           wbuf[:, fch * 512:(fch + 1) * 512],
                           start=(fch == 0), stop=(fch == NF - 1)),
                       reads=[ACH[fch // 4], WCH[fch // 4]],
                       writes=[PP[tbi]] if fch == 0 else [],
                       appends=[PP[tbi]] if fch > 0 else [],
                       tag=f"E{tsup}_{ecb}_{tbi}m{fch}")
                if fch % 4 == 3:
                    c = fch // 4
                    if ecb < 3:
                        e_w2_load(tsup, ecb + 1, c)
                    elif tsup + 1 < NS:
                        e_w2_load(tsup + 1, 0, c)
                        e_n_load(tsup + 1, c)
            for tbi in range(4):
                op("vector", lambda e, tbi=tbi, ecb=ecb:
                   e.tensor_scalar(
                       r2sup[:, tbi * 2048 + ecb * 512:
                             tbi * 2048 + (ecb + 1) * 512],
                       pp[tbi][:, :], scal[:, 9:10], None, ALU.mult),
                   reads=[PP[tbi], SCAL],
                   writes=([R2T[tbi], W2ST[tbi]] if tsup == 0
                           else [R2T[tbi]]) if ecb == 0 else [],
                   appends=[R2T[tbi]] if ecb > 0 else [],
                   tag=f"E{tsup}_{ecb}_{tbi}ev")
        for tbi in range(4):
            tb = tsup * 4 + tbi
            r2 = r2sup[:, tbi * 2048:(tbi + 1) * 2048]
            op("vector", lambda e, r2=r2: e.tensor_tensor(
                r2, r2, b2qb[:, :], ALU.add),
               reads=[R2T[tbi], B2QB], writes=[R2T[tbi]], tag=f"E{tb}badd")
            op("sync", lambda e, tb=tb: e.dma_start(
                out=stage32[:, 0:2048], in_=x2s3[tb]),
               reads=[SCR["x2s"]], writes=[STG[0]], tag=f"E{tb}x2l", dma=True)
            op("vector", lambda e, r2=r2: e.tensor_tensor(
                r2, r2, stage32[:, 0:2048], ALU.add),
               reads=[R2T[tbi], STG[0]], writes=[R2T[tbi]], tag=f"E{tb}radd")
            ln_stats(r2, [R2T[tbi]], f"E{tb}",
                     stage32[:, 2048:4096], [STG[1], T1B[0], T1B[1]])
            op("vector", lambda e, r2=r2: e.tensor_scalar(
                r2, r2, stats[:, 2:3], stats[:, 6:7], ALU.subtract, ALU.mult),
               reads=[R2T[tbi], STATS], writes=[R2T[tbi]], tag=f"E{tb}x3")
            op("sync", lambda e, tb=tb, r2=r2: e.dma_start(out=out3[tb],
                                                           in_=r2),
               reads=[R2T[tbi]], tag=f"E{tb}out", dma=True)

    t.finalize()

    with nc.Block() as block:
        @block.sync
        def _(eng):
            t.emit("sync", eng)
            for q in ("hw", "sw"):
                for i in range(NLANES):
                    lane = f"{q}{i}"
                    if t.totals[lane]:
                        eng.wait_ge(t.sems[lane], t.totals[lane])

        @block.gpsimd
        def _(eng):
            t.emit("gpsimd", eng)
            for i in range(NLANES):
                lane = f"sw{i}"
                if t.totals[lane]:
                    eng.wait_ge(t.sems[lane], t.totals[lane])

        @block.tensor
        def _(eng):
            t.emit("tensor", eng)

        @block.vector
        def _(eng):
            t.emit("vector", eng)

        @block.scalar
        def _(eng):
            t.emit("scalar", eng)

    return nc


_cached = {}


def kernel(**inputs):
    if "nc" not in _cached:
        _cached["nc"] = build()
    nc = _cached["nc"]
    f32 = np.float32
    x = np.ascontiguousarray(np.asarray(inputs["L0_attn_x"], dtype=f32))
    wkva = np.asarray(inputs["L0_attn_w_kva"], dtype=f32)
    kva_used = np.ascontiguousarray(np.concatenate(
        [wkva[:, 0:KR]] +
        [wkva[:, KR + h * 2 * D:KR + h * 2 * D + D] for h in range(H)], axis=1))
    base = {
        "gq": np.ascontiguousarray(np.asarray(inputs["L0_attn_g_q"], f32)[None, :]),
        "gkv": np.ascontiguousarray(np.asarray(inputs["L0_attn_g_kv"], f32)[None, :]),
        "wqa": np.ascontiguousarray(np.asarray(inputs["L0_attn_w_qa"], f32)),
        "wqb": np.ascontiguousarray(np.asarray(inputs["L0_attn_w_qb"], f32)),
        "wkva": kva_used,
        "wkvb": np.ascontiguousarray(np.asarray(inputs["L0_attn_w_kvb"], f32)),
        "wo": np.ascontiguousarray(np.asarray(inputs["L0_attn_w_o"], f32)),
        "w1t": np.ascontiguousarray(np.asarray(inputs["L0_ffn_W1"], f32).T),
        "b1": np.ascontiguousarray(np.asarray(inputs["L0_ffn_b1"], f32)[None, :]),
        "w2t": np.ascontiguousarray(np.asarray(inputs["L0_ffn_W2"], f32).T),
        "b2": np.ascontiguousarray(np.asarray(inputs["L0_ffn_b2"], f32)[None, :]),
        "s1": np.ascontiguousarray(np.asarray(inputs["L0_ffn_s_act1"], f32)[:, None]),
        "s2": np.ascontiguousarray(np.asarray(inputs["L0_ffn_s_act2"], f32)[:, None]),
    }
    in_maps = []
    for c in range(8):
        m = dict(base)
        m["x"] = np.ascontiguousarray(x[c])
        in_maps.append(m)
    res = run_bass_kernel_spmd(nc, in_maps, core_ids=list(range(8)))
    return np.stack([np.asarray(res.results[c]["out"]) for c in range(8)],
                    axis=0)
